# version 06
# baseline (speedup 1.0000x reference)
import numpy as np
from contextlib import contextmanager
import concourse.bass as bass
import concourse.mybir as mybir
from concourse.ap import AP
from concourse.bass_core_rust import IndirectOffsetOnAxis
from concourse.bass_utils import run_bass_kernel_spmd

F32 = mybir.dt.float32
BF16 = mybir.dt.bfloat16
I32 = mybir.dt.int32
AF = mybir.ActivationFunctionType
ALU = mybir.AluOpType
AX = mybir.AxisListType.X

NCORES = 8
BLOC = 2
T = 2048
NT = BLOC * T
D = 1024
DC = D // 128
TCH = 512
NTCH = NT // TCH
EPS = 1e-5


class Flow:

    def __init__(self, nc):
        self.nc = nc
        self.sems = {}
        self.stack = []

    def _s(self, name):
        if name not in self.sems:
            self.sems[name] = [self.nc.alloc_semaphore(name), 0]
        return self.sems[name]

    def inc(self, inst, name, n=1):
        s = self._s(name)
        inst.then_inc(s[0], n)
        s[1] += n
        return s[1]

    def dma(self, inst, name="d"):
        return self.inc(inst, name, 16)

    def val(self, name):
        return self._s(name)[1]

    def _scratch(self, eng, k):
        key = (eng.engine, k)
        if key not in getattr(self, "scratch", {}):
            if not hasattr(self, "scratch"):
                self.scratch = {}
            self.scratch[key] = self.nc.alloc_register(
                eng.engine, name=f"flow_scr_{eng.engine}_{k}")
        return self.scratch[key]

    def wait(self, eng, name):
        s = self._s(name)
        terms = [(i, iters[name]) for (i, iters, base) in self.stack
                 if name in iters and iters[name]]
        if not terms:
            eng.wait_ge(s[0], s[1])
            return
        r = self._scratch(eng, 0)
        (i0, per0) = terms[0]
        eng.reg_alu(r, i0, per0, mybir.AluOpType.mult)
        for (ii, pp) in terms[1:]:
            r2 = self._scratch(eng, 1)
            eng.reg_alu(r2, ii, pp, mybir.AluOpType.mult)
            eng.reg_alu(r, r, r2, mybir.AluOpType.add)
        if s[1]:
            eng.reg_alu(r, r, s[1], mybir.AluOpType.add)
        eng.wait_ge(s[0], r)

    def _free_loop_regs(self, n0):
        import re as _re
        from concourse.bass_primitives_rust import RegisterHandle as _RH
        emap = {"SP": mybir.EngineType.SP, "Pool": mybir.EngineType.Pool,
                "Activation": mybir.EngineType.Activation,
                "PE": mybir.EngineType.PE, "DVE": mybir.EngineType.DVE}
        if not hasattr(self, "_freed"):
            self._freed = set()
        names = self.nc.inst_names()[n0:]
        regs = set()
        for nm in names:
            try:
                s = str(self.nc.get_inst_by_name(nm))
            except Exception:
                continue
            for rn in _re.findall(r"R\[([A-Za-z0-9_.]+)\]", s):
                regs.add(rn)
        for rn in regs:
            if rn in self._freed:
                continue
            if _re.search(r"_(tmp|snap)_\d+$", rn) or "_fori_" in rn:
                pre = rn.split("_")[0]
                if pre in emap and "flow_scr" not in rn:
                    try:
                        self.nc.free_register(_RH(rn, emap[pre]))
                        self._freed.add(rn)
                    except Exception:
                        pass

    @contextmanager
    def loop(self, start, end, iters, barrier=None):
        assert start == 0, "loops assumed to start at 0 for wait arithmetic"
        self._loop_n0 = self.nc.n_instructions()
        with self.nc.Fori(start, end) as i:
            base = {k: self._s(k)[1] for k in set(self.sems) | set(iters)}
            self.stack.append((i, iters, base))
            for eng in (barrier or []):
                for k in iters:
                    if iters[k]:
                        self.wait(eng, k)
            yield i
            self.stack.pop()
            for k, per in iters.items():
                got = self._s(k)[1] - base[k]
                assert got == per, f"loop sem {k}: declared {per}/iter, emitted {got}"
            for k in self.sems:
                if k not in iters:
                    got = self._s(k)[1] - base.get(k, 0)
                    assert got == 0, f"loop sem {k}: incremented {got} but undeclared"
        n = end - start
        for k, per in iters.items():
            self._s(k)[1] = base[k] + n * per
        self._free_loop_regs(self._loop_n0)


DEBUG_INT_OUT = False
P3_DEBUG_W0 = False

def dram(nc, name, shape, dtype, kind):
    if kind == "int" and DEBUG_INT_OUT:
        kind = "out"
    cache = getattr(nc, "_dram_cache", None)
    if cache is None:
        cache = {}
        nc._dram_cache = cache
    if name in cache:
        return cache[name]
    k = {"in": "ExternalInput", "out": "ExternalOutput", "int": "Internal"}[kind]
    ap = nc.dram_tensor(name, list(shape), dtype, kind=k).ap()
    cache[name] = ap
    return ap


def register_consts(nc, values):
    for val in values:
        t = nc.alloc_sbuf_tensor(f"cst_{val}", [128, 1], F32)
        nc.vector.memset(t[:, :], float(val))
        nc.const_aps.register(F32, float(val), t[:, :])




class Alloc:

    def __init__(self, nc):
        from contextlib import ExitStack
        self.nc = nc
        self.st = ExitStack()

    def sb(self, name, shape, dtype):
        return self.st.enter_context(self.nc.sbuf_tensor(name, shape, dtype))

    def ps(self, name, shape, dtype=F32):
        return self.st.enter_context(self.nc.psum_tensor(name, shape, dtype))

    def close(self):
        self.st.close()



def emit_p1(nc, fl, io):
    x0T = dram(nc, "x0T", (D, NT), F32, io("x0T"))
    xn0T = dram(nc, "xn0T", (D, NT), F32, io("xn0T"))
    xn0Tb = dram(nc, "xn0Tb", (D, NT), BF16, io("xn0Tb"))
    t1Tb = dram(nc, "t1Tb", (D, NT), BF16, io("t1Tb"))

    with (
        nc.sbuf_tensor("p1_x", [128, DC * TCH], F32) as xs,
        nc.sbuf_tensor("p1_xb", [128, DC * TCH], BF16) as xb,
        nc.sbuf_tensor("p1_x2", [128, DC * TCH], BF16) as x2b,
        nc.sbuf_tensor("p1_y", [128, DC * TCH], F32) as ys,
        nc.sbuf_tensor("p1_yb", [128, DC * TCH], BF16) as yb,
        nc.sbuf_tensor("p1_ones", [128, 1], BF16) as ones_col,
        nc.sbuf_tensor("p1_row", [1, 6 * TCH], F32) as rows,
        nc.sbuf_tensor("p1_onesr", [1, 128], F32) as ones_row_f,
        nc.psum_tensor("p1_ps_s", [1, TCH], F32) as ps_s,
        nc.psum_tensor("p1_ps_s2", [1, TCH], F32) as ps_s2,
        nc.psum_tensor("p1_ps_mu", [128, TCH], F32) as ps_mu,
        nc.psum_tensor("p1_ps_rs", [128, TCH], F32) as ps_rs,
    ):
        MU, RSTD, SS, S2 = 0, TCH, 2 * TCH, 3 * TCH
        v = nc.vector
        s = nc.scalar
        t = nc.tensor
        sy = nc.sync
        fl.inc(v.memset(ones_col[:, :], 1.0), "c")
        fl.inc(v.memset(ones_row_f[:, :], 1.0), "c")

        def one_pass(src_dram, wr_f32, wr_bf16, pass_id):
            n_st = 16 * ((wr_f32 is not None) + (wr_bf16 is not None))
            src3 = src_dram.rearrange("(c p) n -> p c n", p=128)
            xs3 = xs[:, :].rearrange("p (c n) -> p c n", c=DC)
            ys3 = ys[:, :].rearrange("p (c n) -> p c n", c=DC)
            yb3 = yb[:, :].rearrange("p (c n) -> p c n", c=DC)
            w1_3 = wr_f32.rearrange("(c p) n -> p c n", p=128) if wr_f32 is not None else None
            w2_3 = wr_bf16.rearrange("(c p) n -> p c n", p=128) if wr_bf16 is not None else None
            with fl.loop(0, NTCH, {"d": 16, "c": 9, "st": n_st},
                         barrier=[sy, v, s, t]) as tc:
                off = tc * TCH
                fl.dma(sy.dma_start(out=xs3, in_=src3[:, :, bass.ds(off, TCH)]))
                fl.wait(v, "d")
                fl.inc(v.tensor_copy(xb[:, :], xs[:, :]), "c")
                fl.wait(s, "c")
                fl.inc(s.activation(x2b[:, :], xs[:, :], AF.Square), "c")
                fl.wait(t, "c")
                for c in range(DC):
                    t.matmul(ps_s[:, :], ones_col[:, :], xb[:, c * TCH:(c + 1) * TCH],
                             start=(c == 0), stop=(c == DC - 1))
                last = None
                for c in range(DC):
                    last = t.matmul(ps_s2[:, :], ones_col[:, :], x2b[:, c * TCH:(c + 1) * TCH],
                                    start=(c == 0), stop=(c == DC - 1))
                fl.inc(last, "c")
                fl.wait(v, "c")
                v.tensor_scalar_mul(rows[:, MU:MU + TCH], ps_s[:, :], 1.0 / D)
                v.tensor_scalar_mul(rows[:, S2:S2 + TCH], ps_s2[:, :], 1.0 / D)
                v.tensor_mul(rows[:, SS:SS + TCH], rows[:, MU:MU + TCH], rows[:, MU:MU + TCH])
                fl.inc(v.tensor_sub(rows[:, S2:S2 + TCH], rows[:, S2:S2 + TCH],
                                    rows[:, SS:SS + TCH]), "c")
                fl.wait(s, "c")
                fl.inc(s.activation(rows[:, SS:SS + TCH], rows[:, S2:S2 + TCH],
                                    AF.Sqrt, bias=EPS), "c")
                fl.wait(v, "c")
                fl.inc(v.reciprocal(rows[:, RSTD:RSTD + TCH], rows[:, SS:SS + TCH]), "c")
                fl.wait(t, "c")
                t.matmul(ps_mu[:, :], ones_row_f[:, :], rows[0:1, MU:MU + TCH],
                         start=True, stop=True)
                fl.inc(t.matmul(ps_rs[:, :], ones_row_f[:, :], rows[0:1, RSTD:RSTD + TCH],
                                start=True, stop=True), "c")
                fl.wait(v, "c")
                lastv = None
                for c in range(DC):
                    sl = slice(c * TCH, (c + 1) * TCH)
                    v.tensor_sub(ys[:, sl], xs[:, sl], ps_mu[:, :])
                    lastv = v.tensor_mul(ys[:, sl], ys[:, sl], ps_rs[:, :])
                fl.inc(lastv, "c")
                fl.wait(s, "c")
                fl.inc(s.copy(yb[:, :], ys[:, :]), "c")
                fl.wait(sy, "c")
                if wr_f32 is not None:
                    fl.dma(sy.dma_start(out=w1_3[:, :, bass.ds(off, TCH)], in_=ys3), "st")
                if wr_bf16 is not None:
                    fl.dma(sy.dma_start(out=w2_3[:, :, bass.ds(off, TCH)], in_=yb3), "st")
                fl.wait(sy, "st")

        one_pass(x0T, xn0T, xn0Tb, "a")
        one_pass(xn0T, None, t1Tb, "b")
        fl.wait(sy, "st")
    return {}



def emit_p2(nc, fl, io):
    t1Tb = dram(nc, "t1Tb", (D, NT), BF16, io("t1Tb"))
    attn0T = dram(nc, "attn0T", (D, NT), F32, io("attn0T"))
    qwT = dram(nc, "qwT", (D, D), F32, "in")
    kwT = dram(nc, "kwT", (D, D), F32, "in")
    vwT = dram(nc, "vwT", (D, D), F32, "in")
    owT = dram(nc, "owT", (D, D), F32, "in")
    gb = dram(nc, "a_gb", (10, 128, 8), F32, "in")
    beta_in = dram(nc, "beta_param", (1, 16), F32, "in")
    vb_rep = dram(nc, "a_vb_rep", (128, D), F32, "in")
    qhTd = dram(nc, "p2_qhT", (D, NT), BF16, "int")
    khTd = dram(nc, "p2_khT", (D, NT), BF16, "int")
    Vd = dram(nc, "p2_V", (BLOC, 16, 128, 16 * 65), BF16, "int")
    aTd = dram(nc, "p2_aT", (D, NT), BF16, "int")
    dbg_nrm = dram(nc, "p2_dbg_nrm", (3 * D, NT), BF16, "int")
    dbg_qt = dram(nc, "p2_dbg_qt", (D, NT), BF16, "int")

    H2 = 16 * 65
    v = nc.vector
    s = nc.scalar
    t = nc.tensor
    sy = nc.sync
    g = nc.gpsimd

    al = Alloc(nc)
    gbs = al.sb("p2_gb", [128, 10 * 8], F32)
    beta_r = al.sb("p2_beta", [1, 16], F32)
    beta8 = al.sb("p2_beta8", [128, 16], F32)
    ones1 = al.sb("p2_ones1", [1, 128], F32)
    ones_cb = al.sb("p2_onescb", [128, 1], BF16)
    rowbuf = al.sb("p2_row", [1, 4 * TCH], F32)
    rb = al.sb("p2_rb", [64, TCH], F32)
    ao = al.sb("p2_ao", [128, DC * TCH], BF16)
    oo = al.sb("p2_oo", [128, DC * TCH], F32)
    psA = al.ps("p2_psA", [128, 1024], F32)
    psB = al.ps("p2_psB", [128, 1024], F32)
    psv0 = al.ps("p2_psv0", [128, TCH], F32)
    psv1 = al.ps("p2_psv1", [128, TCH], F32)
    psbc = al.ps("p2_psbc", [128, TCH], F32)
    psr = al.ps("p2_psr", [1, TCH], F32)
    alp = Alloc(nc)
    wq = alp.sb("p2_wq", [128, DC * D], BF16)
    wk = alp.sb("p2_wk", [128, DC * D], BF16)
    wv = alp.sb("p2_wv", [128, DC * D], BF16)
    vbr = alp.sb("p2_vbr", [128, D], F32)
    t1s = alp.sb("p2_t1", [128, DC * TCH], BF16)
    nrm = alp.sb("p2_nrm", [128, 3 * DC * TCH], BF16)
    qt = alp.sb("p2_qt", [128, DC * TCH], BF16)
    kt_s = alp.sb("p2_kt", [128, DC * TCH], BF16)
    q2 = alp.sb("p2_q2", [128, DC * TCH], BF16)
    nq = alp.sb("p2_nq", [128, DC * TCH], BF16)
    vsb = alp.sb("p2_vsb", [128, 4 * H2], BF16)
    if True:
        psv = (psv0, psv1)
        fl.dma(g.dma_start(out=wq[:, :].rearrange("p (c n) -> p c n", c=DC),
                           in_=qwT.rearrange("(c p) n -> p c n", p=128)), "g")
        fl.dma(g.dma_start(out=wk[:, :].rearrange("p (c n) -> p c n", c=DC),
                           in_=kwT.rearrange("(c p) n -> p c n", p=128)), "g")
        fl.dma(g.dma_start(out=wv[:, :].rearrange("p (c n) -> p c n", c=DC),
                           in_=vwT.rearrange("(c p) n -> p c n", p=128)), "g")
        fl.dma(sy.dma_start(out=gbs[:, :].rearrange("p (a n) -> p a n", a=10),
                            in_=gb.rearrange("a p n -> p a n")))
        fl.dma(sy.dma_start(out=vbr[:, :], in_=vb_rep))
        fl.dma(sy.dma_start(out=beta_r[:, :], in_=beta_in))
        fl.wait(v, "d")
        fl.wait(v, "g")
        fl.wait(t, "g")
        fl.inc(v.memset(ones1[:, :], 1.0), "c")
        fl.inc(v.memset(ones_cb[:, :], 1.0), "c")
        fl.wait(s, "d")
        fl.wait(s, "c")
        fl.inc(s.activation(beta_r[:, :], beta_r[:, :], AF.Exp), "c")
        fl.wait(s, "c")
        fl.inc(s.activation(beta_r[:, :], beta_r[:, :], AF.Ln, bias=1.0), "c")
        fl.wait(v, "c")
        fl.inc(v.tensor_scalar(beta_r[:, :], beta_r[:, :], 0.1, 10.0, ALU.max, ALU.min), "c")
        fl.wait(v, "c")
        fl.inc(v.tensor_scalar_mul(beta_r[:, :], beta_r[:, :], 1.0 / 8.0), "c")
        fl.wait(t, "c")
        fl.inc(t.matmul(psbc[:, 0:16], ones1[:, :], beta_r[:, :], start=True, stop=True), "c")
        fl.wait(v, "c")
        fl.inc(v.tensor_copy(beta8[:, :], psbc[:, 0:16]), "c")

        def col(idx, c):
            return gbs[:, idx * 8 + c:idx * 8 + c + 1]

        t1_3 = t1Tb.rearrange("(c p) n -> p c n", p=128)
        t1s3 = t1s[:, :].rearrange("p (c n) -> p c n", c=DC)
        qh_d3 = qhTd.rearrange("(c p) n -> p c n", p=128)
        kh_d3 = khTd.rearrange("(c p) n -> p c n", p=128)

        with fl.loop(0, NTCH, {"d": 16, "c": 64, "st": 80},
                     barrier=[sy, v, s, t]) as tcv:
            off = tcv * TCH
            fl.dma(sy.dma_start(out=t1s3, in_=t1_3[:, :, bass.ds(off, TCH)]))
            fl.wait(v, "d")
            lastv = None
            for i3, (gi, bi) in enumerate(((0, 1), (2, 3), (4, 5))):
                for c in range(DC):
                    dst = nrm[:, (i3 * DC + c) * TCH:(i3 * DC + c + 1) * TCH]
                    lastv = v.tensor_scalar(dst, t1s[:, c * TCH:(c + 1) * TCH],
                                            col(gi, c), col(bi, c), ALU.mult, ALU.add)
            fl.inc(lastv, "c")
            fl.wait(t, "c")
            for (w_sb, src_off, dst, bcol) in ((wq, 0, qt, 6), (wk, DC, kt_s, 7)):
                for m in range(DC):
                    pdst = psA[:, 0:TCH] if m % 2 == 0 else psA[:, TCH:1024]
                    fl.wait(t, "c")
                    lastm = None
                    for c in range(DC):
                        lastm = t.matmul(pdst,
                                         w_sb[:, c * D + m * 128:c * D + (m + 1) * 128],
                                         nrm[:, (src_off + c) * TCH:(src_off + c + 1) * TCH],
                                         start=(c == 0), stop=(c == DC - 1))
                    fl.inc(lastm, "c")
                    fl.wait(v, "c")
                    fl.inc(v.tensor_scalar(dst[:, m * TCH:(m + 1) * TCH], pdst,
                                           col(bcol, m), None, ALU.add), "c")
            fl.wait(sy, "c")
            fl.dma(sy.dma_start(out=dbg_nrm.rearrange("(c p) n -> p c n", p=128)[:, :, bass.ds(off, TCH)],
                                in_=nrm[:, :].rearrange("p (c n) -> p c n", c=3 * DC)), "st")
            fl.dma(sy.dma_start(out=dbg_qt.rearrange("(c p) n -> p c n", p=128)[:, :, bass.ds(off, TCH)],
                                in_=qt[:, :].rearrange("p (c n) -> p c n", c=DC)), "st")
            for (src, dstd) in ((qt, qh_d3), (kt_s, kh_d3)):
                fl.wait(s, "c")
                fl.inc(s.activation(q2[:, :], src[:, :], AF.Square), "c")
                fl.wait(t, "c")
                lastm = None
                for c in range(DC):
                    lastm = t.matmul(psr[:, :], ones_cb[:, :],
                                     q2[:, c * TCH:(c + 1) * TCH],
                                     start=(c == 0), stop=(c == DC - 1))
                fl.inc(lastm, "c")
                fl.wait(s, "c")
                fl.inc(s.activation(rowbuf[:, 0:TCH], psr[:, :], AF.Sqrt), "c")
                fl.wait(v, "c")
                fl.inc(v.tensor_scalar_max(rowbuf[:, 0:TCH], rowbuf[:, 0:TCH], 1e-6), "c")
                fl.wait(v, "c")
                fl.inc(v.reciprocal(rowbuf[:, TCH:2 * TCH], rowbuf[:, 0:TCH]), "c")
                fl.wait(t, "c")
                fl.inc(t.matmul(psbc[:, :], ones1[:, :], rowbuf[0:1, TCH:2 * TCH],
                                start=True, stop=True), "c")
                fl.wait(v, "c")
                fl.wait(v, "st")
                lastv = None
                for c in range(DC):
                    lastv = v.tensor_mul(nq[:, c * TCH:(c + 1) * TCH],
                                         src[:, c * TCH:(c + 1) * TCH], psbc[:, :])
                fl.inc(lastv, "c")
                fl.wait(sy, "c")
                fl.dma(sy.dma_start(out=dstd[:, :, bass.ds(off, TCH)],
                                    in_=nq[:, :].rearrange("p (c n) -> p c n", c=DC)), "st")
            fl.wait(v, "st")
            fl.wait(v, "c")
            fl.inc(v.memset(vsb[:, :], 1.0), "c")
            for tt in range(4):
                for half in range(2):
                    fl.wait(t, "c")
                    lastm = None
                    for c in range(DC):
                        lastm = t.matmul(psB[:, 0:TCH],
                                         nrm[:, (2 * DC + c) * TCH + tt * 128:
                                             (2 * DC + c) * TCH + (tt + 1) * 128],
                                         wv[:, c * D + half * TCH:c * D + (half + 1) * TCH],
                                         start=(c == 0), stop=(c == DC - 1))
                    fl.inc(lastm, "c")
                    fl.wait(v, "c")
                    dst = vsb[:, tt * H2:(tt + 1) * H2].rearrange(
                        "p (h x) -> p h x", x=65)[:, half * 8:(half + 1) * 8, 0:64]
                    src_ps = psB[:, 0:TCH].rearrange("p (h x) -> p h x", x=64)
                    vb_sl = vbr[:, half * TCH:(half + 1) * TCH].rearrange(
                        "p (h x) -> p h x", x=64)
                    fl.inc(v.tensor_add(dst, src_ps, vb_sl), "c")
            fl.wait(sy, "c")
            fl.dma(sy.dma_start(
                out=Vd.rearrange("b k p n -> p (b k) n")[:, bass.ds(tcv * 4, 4), :],
                in_=vsb[:, :].rearrange("p (k n) -> p k n", k=4)), "st")
            fl.wait(sy, "st")

        alp.close()
        ala = Alloc(nc)
        vball = ala.sb("p2_vball", [128, 2 * 16 * H2], BF16)
        qh = ala.sb("p2_qh", [128, NT], BF16)
        kh = ala.sb("p2_kh", [128, NT], BF16)
        pex = ala.sb("p2_pex", [128, 2 * 1024], BF16)
        vh = ala.sb("p2_vh", [128, 16 * 65], BF16)
        fl.dma(sy.dma_start(out=vball[:, :].rearrange("p (k n) -> p k n", k=32),
                            in_=Vd.rearrange("b k p n -> p (b k) n")))
        qh5 = qhTd.rearrange("(c p) n -> p c n", p=128)
        kh5 = khTd.rearrange("(c p) n -> p c n", p=128)
        vb5 = vball[:, :].rearrange("p (b k h x) -> p b k h x", b=BLOC, k=16, h=16)
        aT3 = aTd.rearrange("(c p) n -> p c n", p=128)
        with fl.loop(0, 8, {"d": 32, "c": 452, "st": 64},
                     barrier=[sy, v, s, t]) as hp:
            fl.dma(sy.dma_start(out=qh[:, :].rearrange("p (o n) -> p o n", o=1),
                                in_=qh5[:, bass.ds(hp, 1), :]))
            fl.dma(sy.dma_start(out=kh[:, :].rearrange("p (o n) -> p o n", o=1),
                                in_=kh5[:, bass.ds(hp, 1), :]))
            h0 = hp * 2
            fl.wait(t, "d")
            for b in range(BLOC):
                for j in range(2):
                    hsl = bass.ds(h0 + j, 1)
                    rsl = slice(64 * j, 64 * j + 64)
                    fl.wait(v, "c")
                    fl.inc(v.tensor_copy(vh[:, :].rearrange("p (k x) -> p k x", k=16),
                                         vb5[:, b, :, :, :][:, :, hsl, :].opt()
                                         if False else
                                         vb5[:, b:b + 1, :, hsl, :].opt()), "c")
                    fl.wait(t, "c")
                    for qc2 in range(2):
                        for kt in range(16):
                            par = kt % 2
                            pa = psA if par == 0 else psB
                            lastm = None
                            for qq in range(2):
                                qc = qc2 * 2 + qq
                                lastm = t.matmul(
                                    pa[:, qq * TCH:(qq + 1) * TCH],
                                    kh[rsl, b * T + kt * 128:b * T + (kt + 1) * 128],
                                    qh[rsl, b * T + qc * TCH:b * T + (qc + 1) * TCH],
                                    start=True, stop=True)
                            fl.inc(lastm, "c")
                            fl.wait(s, "c")
                            fl.inc(s.activation(pex[:, par * 1024:(par + 1) * 1024],
                                                pa[:, :], AF.Exp,
                                                scale=beta8[:, hsl]), "c")
                            fl.wait(t, "c")
                            lastm = None
                            for qq in range(2):
                                lastm = t.matmul(
                                    psv[qq][0:65, :], vh[:, kt * 65:(kt + 1) * 65],
                                    pex[:, par * 1024 + qq * TCH:
                                        par * 1024 + (qq + 1) * TCH],
                                    start=(kt == 0), stop=(kt == 15))
                            fl.inc(lastm, "c")
                        for qq in range(2):
                            qc = qc2 * 2 + qq
                            fl.wait(v, "c")
                            fl.inc(v.reciprocal(rowbuf[0:1, 2 * TCH:3 * TCH],
                                                psv[qq][64:65, :]), "c")
                            fl.wait(t, "c")
                            fl.inc(t.matmul(psbc[0:64, :], ones1[:, 0:64],
                                            rowbuf[0:1, 2 * TCH:3 * TCH],
                                            start=True, stop=True), "c")
                            fl.wait(v, "c")
                            fl.inc(v.tensor_copy(rb[:, :], psbc[0:64, :]), "c")
                            fl.wait(v, "c")
                            fl.inc(v.tensor_mul(ao[rsl, qc * TCH:(qc + 1) * TCH],
                                                psv[qq][0:64, :], rb[:, :]), "c")
                    fl.wait(sy, "c")
                    fl.dma(sy.dma_start(
                        out=aT3[rsl, bass.ds(hp, 1), b * T:(b + 1) * T],
                        in_=ao[rsl, 0:T].rearrange("p (o n) -> p o n", o=1)), "st")
                    fl.wait(sy, "st")

        ala.close()
        alo = Alloc(nc)
        wo = alo.sb("p2_wo", [128, DC * D], BF16)
        fl.wait(g, "c")
        fl.wait(g, "st")
        fl.wait(g, "g")
        fl.dma(g.dma_start(out=wo[:, :].rearrange("p (c n) -> p c n", c=DC),
                           in_=owT.rearrange("(c p) n -> p c n", p=128)), "g")
        fl.wait(t, "g")
        aT3b = aTd.rearrange("(c p) n -> p c n", p=128)
        at0 = attn0T.rearrange("(c p) n -> p c n", p=128)
        with fl.loop(0, NTCH, {"d": 16, "c": 16, "st": 16},
                     barrier=[sy, v, s, t]) as tcv:
            off2 = tcv * TCH
            fl.dma(sy.dma_start(out=ao[:, :].rearrange("p (c n) -> p c n", c=DC),
                                in_=aT3b[:, :, bass.ds(off2, TCH)]))
            fl.wait(t, "d")
            for m in range(DC):
                pdst = psA[:, 0:TCH] if m % 2 == 0 else psA[:, TCH:1024]
                fl.wait(t, "c")
                lastm = None
                for c in range(DC):
                    lastm = t.matmul(pdst,
                                     wo[:, c * D + m * 128:c * D + (m + 1) * 128],
                                     ao[:, c * TCH:(c + 1) * TCH],
                                     start=(c == 0), stop=(c == DC - 1))
                fl.inc(lastm, "c")
                fl.wait(v, "c")
                fl.inc(v.tensor_scalar(oo[:, m * TCH:(m + 1) * TCH], pdst,
                                       col(9, m), None, ALU.add), "c")
            fl.wait(sy, "c")
            fl.dma(sy.dma_start(out=at0[:, :, bass.ds(off2, TCH)],
                                in_=oo[:, :].rearrange("p (c n) -> p c n", c=DC)), "st")
            fl.wait(sy, "st")
    alo.close()
    al.close()
    return {}



def emit_p3(nc, fl, io):
    WIN = 512
    HIDP = 2816
    NDD = 2 * HIDP // 128
    H2L = 8 * 65
    xn0T = dram(nc, "xn0T", (D, NT), F32, io("xn0T"))
    ufT = dram(nc, "u_finT", (D, NT), F32, io("u_finT"))
    u_f32 = dram(nc, "p3_u32", (D, NT), F32, "int")
    dbgu = [dram(nc, f"p3_dbgu{k}", (D, NT), F32, "int") for k in range(4)]
    u_bfp = dram(nc, "p3_ubfp", (D, NT + 2), BF16, "int")
    u_bfp2 = dram(nc, "p3_ubfp2", (D, NT + 2), BF16, "int")
    wqkv_d = [dram(nc, f"f{l}_wqkv", (D, 1536), F32, "in") for l in range(2)]
    w1_d = [dram(nc, f"f{l}_w1", (D, 2 * HIDP), F32, "in") for l in range(2)]
    b1_d = [dram(nc, f"f{l}_b1", (128, NDD), F32, "in") for l in range(2)]
    w2_d = [dram(nc, f"f{l}_w2", (HIDP, D), F32, "in") for l in range(2)]
    b2_d = [dram(nc, f"f{l}_b2", (128, DC), F32, "in") for l in range(2)]
    wo_d = [dram(nc, f"f{l}_wo", (WIN, D), F32, "in") for l in range(2)]
    qs_d = [dram(nc, f"f{l}_qs", (128, 1), F32, "in") for l in range(2)]
    ks_d = [dram(nc, f"f{l}_ks", (128, 1), F32, "in") for l in range(2)]
    lg_d = [dram(nc, f"f{l}_lg", (128, DC), F32, "in") for l in range(2)]
    lb_d = [dram(nc, f"f{l}_lb", (128, DC), F32, "in") for l in range(2)]
    rot_d = {nm: dram(nc, "rot_" + nm, (128, NT), F32, "in")
             for nm in ("caq", "cbq", "cak", "cbk")}
    lmask_d = dram(nc, "lmask2", (2, 1024, WIN), F32, "in")
    hm4_d = dram(nc, "hm4", (4, 128, 8), F32, "in")
    hm4T_d = dram(nc, "hm4T", (4, 8, 128), F32, "in")
    w0tab_d = dram(nc, "w0tab", (1, 8), I32, "in")

    v = nc.vector
    s = nc.scalar
    t = nc.tensor
    sy = nc.sync
    g = nc.gpsimd

    al = Alloc(nc)
    ones1 = al.sb("p3_ones1", [1, 128], F32)
    ones_cb3 = al.sb("p3_onescb", [128, 1], BF16)
    lmask = al.sb("p3_lmask", [128, 2 * 8 * WIN], BF16)
    hm4 = al.sb("p3_hm4", [128, 4 * 8], BF16)
    hm4T = al.sb("p3_hm4T", [8, 4 * 128], F32)
    qs_c = al.sb("p3_qs", [128, 2], F32)
    w0tab = al.sb("p3_w0tab", [1, 8], I32)
    rowb = al.sb("p3_rowb", [8, 3 * WIN], F32)
    rowc = al.sb("p3_rowc", [1, TCH], F32)
    rowd = al.sb("p3_rowd", [1, TCH], F32)
    rowe = al.sb("p3_rowe", [1, TCH], F32)
    rb = al.sb("p3_rb", [64, WIN], F32)
    psb = al.ps("p3_psb", [128, 1024], F32)
    psc = al.ps("p3_psc", [128, 1024], F32)
    psv_ = al.ps("p3_psv", [128, WIN], F32)
    psS = al.ps("p3_psS", [8, TCH], F32)
    psr2 = al.ps("p3_psr2", [128, TCH], F32)

    wqkvB = [dram(nc, f"p3_wqkvB{l}", (D, 1536), BF16, "int") for l in range(2)]
    woB = [dram(nc, f"p3_woB{l}", (WIN, D), BF16, "int") for l in range(2)]
    w1B = [dram(nc, f"p3_w1B{l}", (D, 2 * HIDP), BF16, "int") for l in range(2)]
    w2B = [dram(nc, f"p3_w2B{l}", (HIDP, D), BF16, "int") for l in range(2)]
    fl.dma(g.dma_start(out=lmask[:, :].rearrange("p (a k n) -> p a k n", a=2, k=8),
                       in_=lmask_d.rearrange("a (k p) n -> p a k n", p=128)), "g")
    fl.dma(g.dma_start(out=hm4[:, :].rearrange("p (a n) -> p a n", a=4),
                       in_=hm4_d.rearrange("a p n -> p a n")), "g")
    fl.wait(sy, "g")
    alcv = Alloc(nc)
    cv = alcv.sb("p3_cv", [128, 22 * 1024], BF16)
    cvt_jobs = []
    for l in range(2):
        cvt_jobs.append((wqkv_d[l], wqkvB[l], 8, 1536, 0))
        cvt_jobs.append((wo_d[l], woB[l], 4, 1024, 0))
        cvt_jobs.append((w2_d[l], w2B[l], 22, 1024, 0))
        cvt_jobs.append((w1_d[l], w1B[l], 8, 2816, 0))
        cvt_jobs.append((w1_d[l], w1B[l], 8, 2816, 2816))
    for (src_d, dst_d, nch, wdt, coff) in cvt_jobs:
        s3 = src_d.rearrange("(c p) n -> p c n", p=128)
        d3 = dst_d.rearrange("(c p) n -> p c n", p=128)
        fl.dma(g.dma_start(out=cv[:, 0:nch * wdt].rearrange("p (c n) -> p c n", c=nch),
                           in_=s3[:, :, coff:coff + wdt]), "g")
        fl.wait(sy, "g")
        fl.dma(sy.dma_start(out=d3[:, :, coff:coff + wdt],
                            in_=cv[:, 0:nch * wdt].rearrange("p (c n) -> p c n", c=nch)), "st")
        fl.wait(g, "st")
    fl.wait(sy, "st")
    fl.wait(v, "st")
    fl.wait(t, "st")
    fl.wait(s, "st")
    fl.wait(g, "st")
    alcv.close()
    fl.wait(v, "g")
    fl.wait(t, "g")
    fl.inc(v.memset(ones1[:, :], 1.0), "c")
    fl.inc(v.memset(ones_cb3[:, :], 1.0), "c")
    fl.dma(sy.dma_start(out=hm4T[:, :].rearrange("p (a n) -> p a n", a=4),
                        in_=hm4T_d.rearrange("a p n -> p a n")))
    fl.dma(sy.dma_start(out=w0tab[:, :], in_=w0tab_d))
    lm4 = lmask[:, :].rearrange("p (a k n) -> p a k n", a=2, k=8)

    uin3 = xn0T.rearrange("(c p) n -> p c n", p=128)
    u32_3 = u_f32.rearrange("(c p) n -> p c n", p=128)
    ubf_3 = u_bfp.rearrange("(c p) n -> p c n", p=128)
    ubf2_3 = u_bfp2.rearrange("(c p) n -> p c n", p=128)

    al0 = Alloc(nc)
    cpy = al0.sb("p3_cpy", [128, DC * TCH], F32)
    cpyb = al0.sb("p3_cpyb", [128, DC * (TCH + 2)], BF16)
    fl.inc(v.memset(cpyb[:, :], 0.0), "c")
    fl.wait(sy, "c")
    for c in range(DC):
        for jh in range(2):
            fl.dma(sy.dma_start(out=ubf_3[:, c:c + 1, jh * 2049:(jh + 1) * 2049],
                                in_=cpyb[:, 0:2049].rearrange("p (o n) -> p o n", o=1)), "st")
            fl.dma(sy.dma_start(out=ubf2_3[:, c:c + 1, jh * 2049:(jh + 1) * 2049],
                                in_=cpyb[:, 0:2049].rearrange("p (o n) -> p o n", o=1)), "st")
    fl.wait(sy, "st")
    for tc in range(NTCH):
        off = tc * TCH
        offp = tc * TCH + 1 + tc // 4
        fl.wait(sy, "st")
        fl.wait(sy, "c")
        fl.dma(sy.dma_start(out=cpy[:, :].rearrange("p (c n) -> p c n", c=DC),
                            in_=uin3[:, :, off:off + TCH]))
        fl.wait(v, "d")
        fl.wait(v, "st")
        fl.inc(v.tensor_copy(cpyb[:, 0:DC * TCH].rearrange("p (c n) -> p c n", c=DC),
                             cpy[:, :].rearrange("p (c n) -> p c n", c=DC)), "c")
        fl.wait(sy, "c")
        fl.dma(sy.dma_start(out=u32_3[:, :, off:off + TCH],
                            in_=cpy[:, :].rearrange("p (c n) -> p c n", c=DC)), "st")
        fl.dma(sy.dma_start(out=ubf_3[:, :, offp:offp + TCH],
                            in_=cpyb[:, 0:DC * TCH].rearrange("p (c n) -> p c n", c=DC)), "st")
    fl.wait(sy, "st")
    al0.close()


    def snap_u(which):
        alS = Alloc(nc)
        cS = alS.sb(f"p3_cS{which}", [128, DC * TCH], F32)
        d3 = dbgu[which].rearrange("(c p) n -> p c n", p=128)
        for tcq in range(NTCH):
            offq = tcq * TCH
            fl.wait(sy, "st")
            fl.wait(sy, "c")
            fl.dma(sy.dma_start(out=cS[:, :].rearrange("p (c n) -> p c n", c=DC),
                                in_=u32_3[:, :, offq:offq + TCH]))
            fl.wait(sy, "d")
            fl.dma(sy.dma_start(out=d3[:, :, offq:offq + TCH],
                                in_=cS[:, :].rearrange("p (c n) -> p c n", c=DC)), "st")
        fl.wait(sy, "st")
        alS.close()
    for l in range(2):
        ala = Alloc(nc)
        wqkv_sb = ala.sb(f"p3_wqkv{l}", [128, DC * 1536], BF16)
        wo_sb = ala.sb(f"p3_wo{l}", [128, 4 * D], BF16)
        rot = {nm: ala.sb(f"p3_rot{l}_{nm}", [128, NT], F32) for nm in
               ("caq", "cbq", "cak", "cbk")}
        hh = ala.sb(f"p3_hh{l}", [128, DC * WIN], BF16)
        qw = ala.sb(f"p3_qw{l}", [128, 4 * WIN], BF16)
        kcur = ala.sb(f"p3_kc{l}", [128, 4 * WIN], BF16)
        kprev = ala.sb(f"p3_kp{l}", [128, 4 * WIN], BF16)
        vcur = ala.sb(f"p3_vc{l}", [128, 4 * H2L], BF16)
        vprev = ala.sb(f"p3_vp{l}", [128, 4 * H2L], BF16)
        scr = ala.sb(f"p3_scr{l}", [128, 4 * WIN], BF16)
        sw_b = ala.sb(f"p3_sw{l}", [128, 2 * WIN], BF16)
        pex = ala.sb(f"p3_pex{l}", [128, 2 * WIN], BF16)
        aw = ala.sb(f"p3_aw{l}", [128, 4 * WIN], BF16)
        lms = ala.sb(f"p3_lms{l}", [128, WIN], F32)
        if P3_DEBUG_W0:
            dbgraw_q = ala.sb("p3_dbgrq", [128, 4 * WIN], BF16)
            dbgraw_k = ala.sb("p3_dbgrk", [128, 4 * WIN], BF16)
        uc = ala.sb(f"p3_uc{l}", [128, WIN], F32)
        ucb = ala.sb(f"p3_ucb{l}", [128, WIN], BF16)
        fl.dma(sy.dma_start(out=wqkv_sb[:, :].rearrange("p (c n) -> p c n", c=DC),
                            in_=wqkvB[l].rearrange("(c p) n -> p c n", p=128)))
        fl.dma(sy.dma_start(out=wo_sb[:, :].rearrange("p (c n) -> p c n", c=4),
                            in_=woB[l].rearrange("(c p) n -> p c n", p=128)))
        for nm in rot:
            fl.dma(sy.dma_start(out=rot[nm][:, :], in_=rot_d[nm]))
        fl.dma(sy.dma_start(out=qs_c[:, 0:1], in_=qs_d[l]))
        fl.dma(sy.dma_start(out=qs_c[:, 1:2], in_=ks_d[l]))
        fl.wait(v, "d")
        fl.wait(v, "g")
        fl.inc(v.memset(kcur[:, :], 0.0), "c")
        fl.inc(v.memset(vcur[:, :], 0.0), "c")
        fl.wait(t, "g")
        fl.wait(t, "c")
        fl.wait(sy, "g")
        fl.wait(v, "g")
        with fl.loop(0, 1 if P3_DEBUG_W0 else 8,
                     {"d": 160, "c": 453 if P3_DEBUG_W0 else 452, "st": 256},
                     barrier=[sy, v, s, t]) as bw:
            off = bw * WIN
            b_ix = bw // 4
            offp = bw * WIN + 1 + b_ix
            offs = bw * WIN + b_ix
            w0 = (bw % 4 + 3) // 4 * (-1) + 1
            fl.wait(v, "c")
            v.tensor_copy(kprev[:, :], kcur[:, :])
            fl.inc(v.tensor_copy(vprev[:, :], vcur[:, :]), "c")
            fl.dma(sy.dma_start(out=hh[:, 0:4 * WIN].rearrange("p (c n) -> p c n", c=4),
                                in_=ubf_3[:, 0:4, bass.ds(offp, WIN)]))
            fl.dma(sy.dma_start(out=hh[:, 4 * WIN:DC * WIN].rearrange("p (c n) -> p c n", c=4),
                                in_=ubf_3[:, 4:8, bass.ds(offs, WIN)]))
            fl.wait(t, "d")
            for m in range(8):
                pdst = psb[:, 0:WIN] if m % 2 == 0 else psb[:, WIN:1024]
                fl.wait(t, "c")
                lastm = None
                for c in range(DC):
                    lastm = t.matmul(pdst,
                                     wqkv_sb[:, c * 1536 + m * 128:c * 1536 + (m + 1) * 128],
                                     hh[:, c * WIN:(c + 1) * WIN],
                                     start=(c == 0), stop=(c == DC - 1))
                fl.inc(lastm, "c")
                fl.wait(v, "c")
                dsl = (qw[:, m * WIN:(m + 1) * WIN] if m < 4 else
                       kcur[:, (m - 4) * WIN:(m - 3) * WIN])
                fl.inc(v.tensor_copy(dsl, pdst), "c")
            fl.wait(v, "c")
            fl.inc(v.memset(vcur[:, :], 1.0), "c")
            for tt in range(4):
                fl.wait(t, "c")
                lastm = None
                for c in range(DC):
                    lastm = t.matmul(psc[:, 0:WIN],
                                     hh[:, c * WIN + tt * 128:c * WIN + (tt + 1) * 128],
                                     wqkv_sb[:, c * 1536 + 1024:c * 1536 + 1536],
                                     start=(c == 0), stop=(c == DC - 1))
                fl.inc(lastm, "c")
                fl.wait(v, "c")
                dst = vcur[:, tt * H2L:(tt + 1) * H2L]
                fl.inc(v.tensor_copy(
                    dst.rearrange("p (h x) -> p h x", x=65)[:, :, 0:64],
                    psc[:, 0:WIN].rearrange("p (h x) -> p h x", x=64)), "c")
            if P3_DEBUG_W0:
                fl.wait(v, "c")
                v.tensor_copy(dbgraw_q[:, :], qw[:, :])
                fl.inc(v.tensor_copy(dbgraw_k[:, :], kcur[:, :]), "c")
            for (dst, base, scol, ca, cb) in (
                    (qw, 0, 0, "caq", "cbq"),
                    (kcur, 0, 1, "cak", "cbk")):
                fl.wait(s, "c")
                fl.inc(s.activation(scr[:, 0:4 * WIN], dst[:, 0:4 * WIN],
                                    AF.Square), "c")
                fl.wait(t, "c")
                lastm = None
                for c in range(4):
                    lastm = t.matmul(psS[:, 0:WIN], hm4[:, c * 8:(c + 1) * 8],
                                     scr[:, c * WIN:(c + 1) * WIN],
                                     start=(c == 0), stop=(c == 3))
                fl.inc(lastm, "c")
                fl.wait(s, "c")
                fl.inc(s.activation(rowb[:, 0:WIN], psS[:, 0:WIN], AF.Sqrt), "c")
                fl.wait(v, "c")
                fl.inc(v.tensor_scalar_max(rowb[:, WIN:2 * WIN], rowb[:, 0:WIN], 1e-12), "c")
                fl.wait(v, "c")
                fl.inc(v.reciprocal(rowb[:, 2 * WIN:3 * WIN], rowb[:, WIN:2 * WIN]), "c")
                for c in range(4):
                    fl.wait(t, "c")
                    fl.inc(t.matmul(psr2[:, 0:WIN], hm4T[:, c * 128:(c + 1) * 128],
                                    rowb[:, 2 * WIN:3 * WIN], start=True, stop=True), "c")
                    sl = slice(base + c * WIN, base + (c + 1) * WIN)
                    fl.wait(v, "c")
                    fl.inc(v.scalar_tensor_tensor(scr[:, c * WIN:(c + 1) * WIN],
                                                  dst[:, sl], qs_c[:, scol:scol + 1],
                                                  psr2[:, 0:WIN], ALU.mult, ALU.mult), "c")
                for c in range(4):
                    ro = bass.ds(off, WIN)
                    fl.wait(v, "c")
                    v.tensor_copy(sw_b[0:32, 0:WIN], scr[32:64, c * WIN:(c + 1) * WIN])
                    v.tensor_copy(sw_b[32:64, 0:WIN], scr[0:32, c * WIN:(c + 1) * WIN])
                    v.tensor_copy(sw_b[64:96, 0:WIN], scr[96:128, c * WIN:(c + 1) * WIN])
                    fl.inc(v.tensor_copy(sw_b[96:128, 0:WIN],
                                         scr[64:96, c * WIN:(c + 1) * WIN]), "c")
                    fl.wait(v, "c")
                    v.tensor_mul(scr[:, c * WIN:(c + 1) * WIN],
                                 scr[:, c * WIN:(c + 1) * WIN], rot[ca][:, ro])
                    fl.inc(v.tensor_mul(sw_b[:, WIN:2 * WIN], sw_b[:, 0:WIN],
                                        rot[cb][:, ro]), "c")
                    fl.wait(v, "c")
                    sl = slice(base + c * WIN, base + (c + 1) * WIN)
                    fl.inc(v.tensor_add(dst[:, sl], scr[:, c * WIN:(c + 1) * WIN],
                                        sw_b[:, WIN:2 * WIN]), "c")
            for h in range(8):
                ch = h // 2
                rs = slice(64 * (h % 2), 64 * (h % 2) + 64)
                for kt in range(8):
                    ksrc = kcur if kt >= 4 else kprev
                    vsrc = vcur if kt >= 4 else vprev
                    kcol = ch * WIN + (kt % 4) * 128
                    fl.wait(t, "c")
                    fl.inc(t.matmul(psb[:, 0:WIN], ksrc[rs, kcol:kcol + 128],
                                    qw[rs, ch * WIN:(ch + 1) * WIN],
                                    start=True, stop=True), "c")
                    fl.wait(v, "c")
                    fl.inc(v.tensor_tensor(lms[:, :], psb[:, 0:WIN],
                                           lm4[:, bass.ds(w0, 1), kt, :].opt(),
                                           ALU.mult), "c")
                    fl.wait(s, "c")
                    fl.inc(s.activation(pex[:, 0:WIN], lms[:, :], AF.Exp,
                                        scale=8.0), "c")
                    fl.wait(v, "c")
                    fl.inc(v.tensor_mul(pex[:, WIN:2 * WIN], pex[:, 0:WIN],
                                        lm4[:, bass.ds(w0, 1), kt, :].opt()), "c")
                    vcol = (kt % 4) * H2L + h * 65
                    fl.wait(t, "c")
                    fl.inc(t.matmul(psv_[0:65, :], vsrc[:, vcol:vcol + 65],
                                    pex[:, WIN:2 * WIN],
                                    start=(kt == 0), stop=(kt == 7)), "c")
                fl.wait(v, "c")
                fl.inc(v.reciprocal(rowc[0:1, 0:WIN], psv_[64:65, :]), "c")
                fl.wait(t, "c")
                fl.inc(t.matmul(psr2[0:64, 0:WIN], ones1[:, 0:64], rowc[0:1, 0:WIN],
                                start=True, stop=True), "c")
                fl.wait(v, "c")
                fl.inc(v.tensor_copy(rb[:, :], psr2[0:64, 0:WIN]), "c")
                fl.wait(v, "c")
                fl.inc(v.tensor_mul(aw[rs, ch * WIN:(ch + 1) * WIN],
                                    psv_[0:64, :], rb[:, :]), "c")
            for m in range(DC):
                pdst = psc[:, 0:WIN] if m % 2 == 0 else psc[:, WIN:1024]
                fl.wait(t, "c")
                lastm = None
                for c in range(4):
                    lastm = t.matmul(pdst, wo_sb[:, c * D + m * 128:c * D + (m + 1) * 128],
                                     aw[:, c * WIN:(c + 1) * WIN],
                                     start=(c == 0), stop=(c == 3))
                fl.inc(lastm, "c")
                fl.dma(sy.dma_start(out=uc[:, :].rearrange("p n -> p 1 n") if False
                                    else uc[:, :],
                                    in_=u32_3[:, m, bass.ds(off, WIN)].opt() if False
                                    else u32_3[:, m:m + 1, bass.ds(off, WIN)].opt()))
                fl.wait(v, "d")
                fl.wait(v, "c")
                fl.inc(v.tensor_add(uc[:, :], uc[:, :], pdst), "c")
                fl.wait(s, "c")
                fl.inc(s.copy(ucb[:, :], uc[:, :]), "c")
                fl.wait(sy, "c")
                fl.dma(sy.dma_start(out=u32_3[:, m:m + 1, bass.ds(off, WIN)].opt(),
                                    in_=uc[:, :]), "st")
                fl.dma(sy.dma_start(out=ubf2_3[:, m:m + 1, bass.ds(offp, WIN)].opt(),
                                    in_=ucb[:, :]), "st")
                fl.wait(sy, "st")
        if P3_DEBUG_W0:
            for nm, buf, w_ in (("wqkvsb", wqkv_sb, DC * 1536),
                                ("rawq", dbgraw_q, 2048), ("rawk", dbgraw_k, 2048),
                                ("qw", qw, 2048), ("kcur", kcur, 2048),
                                ("vcur", vcur, 4 * H2L), ("aw", aw, 2048),
                                ("hh", hh, DC * WIN), ("pex", pex, 1024)):
                dd_ = dram(nc, "p3dbg_" + nm, (128, w_), BF16, "out")
                fl.wait(sy, "c")
                fl.dma(sy.dma_start(out=dd_, in_=buf[:, :]), "st")
            fl.wait(sy, "st")
            ala.close()
            snap_u(0)
            al.close()
            return {}
        ala.close()
        snap_u(2 * l)

        alm = Alloc(nc)
        w2_sb = alm.sb(f"p3_w2{l}", [128, 22 * D], BF16)
        w1_sb = alm.sb(f"p3_w1b{l}", [128, 2 * DC * 128], BF16)
        b1_sb = alm.sb(f"p3_b1{l}", [128, NDD], F32)
        b2_sb = alm.sb(f"p3_b2{l}", [128, DC], F32)
        lgb = alm.sb(f"p3_lgb{l}", [128, 2 * DC], F32)
        h2b = alm.sb(f"p3_h2b{l}", [128, DC * TCH], BF16)
        sq = alm.sb(f"p3_sq{l}", [128, DC * TCH], BF16)
        t2b = alm.sb(f"p3_t2b{l}", [128, DC * TCH], BF16)
        asb = alm.sb(f"p3_asb{l}", [128, 22 * TCH], BF16)
        h1sb = alm.sb(f"p3_h1{l}", [128, 22 * TCH], BF16)
        gg = alm.sb(f"p3_gg{l}", [128, TCH], BF16)
        uc2 = alm.sb(f"p3_uc2{l}", [128, DC * TCH], F32)
        ucb2 = alm.sb(f"p3_ucb2{l}", [128, DC * TCH], BF16)
        fl.dma(sy.dma_start(out=w2_sb[:, :].rearrange("p (c n) -> p c n", c=22),
                            in_=w2B[l].rearrange("(c p) n -> p c n", p=128)))
        fl.dma(sy.dma_start(out=b1_sb[:, :], in_=b1_d[l]))
        fl.dma(sy.dma_start(out=b2_sb[:, :], in_=b2_d[l]))
        fl.dma(sy.dma_start(out=lgb[:, 0:DC], in_=lg_d[l]))
        fl.dma(sy.dma_start(out=lgb[:, DC:2 * DC], in_=lb_d[l]))
        fl.wait(t, "d")
        fl.wait(v, "d")
        with fl.loop(0, NTCH, {"d": 48 + 16 * NDD, "c": 143, "st": 32},
                     barrier=[sy, v, s, t]) as tc:
            off = tc * TCH
            offp = tc * TCH + 1 + tc // 4
            offs = tc * TCH + tc // 4
            fl.dma(sy.dma_start(out=h2b[:, 0:4 * TCH].rearrange("p (c n) -> p c n", c=4),
                                in_=ubf2_3[:, 0:4, bass.ds(offp, TCH)]))
            fl.dma(sy.dma_start(out=h2b[:, 4 * TCH:DC * TCH].rearrange("p (c n) -> p c n", c=4),
                                in_=ubf2_3[:, 4:8, bass.ds(offs, TCH)]))
            fl.wait(s, "d")
            fl.inc(s.activation(sq[:, :], h2b[:, :], AF.Square), "c")
            fl.wait(t, "c")
            fl.wait(t, "d")
            lastm = None
            for c in range(DC):
                lastm = t.matmul(psS[0:1, :], ones_cb3[:, :], h2b[:, c * TCH:(c + 1) * TCH],
                                 start=(c == 0), stop=(c == DC - 1))
            fl.inc(lastm, "c")
            lastm = None
            for c in range(DC):
                lastm = t.matmul(psv_[0:1, :], ones_cb3[:, :], sq[:, c * TCH:(c + 1) * TCH],
                                 start=(c == 0), stop=(c == DC - 1))
            fl.inc(lastm, "c")
            fl.wait(v, "c")
            v.tensor_scalar_mul(rowc[:, :], psS[0:1, :], 1.0 / D)
            fl.inc(v.tensor_scalar_mul(rowd[:, :], psv_[0:1, 0:TCH], 1.0 / D), "c")
            fl.wait(v, "c")
            v.tensor_mul(rowe[:, :], rowc[:, :], rowc[:, :])
            fl.inc(v.tensor_sub(rowd[:, :], rowd[:, :], rowe[:, :]), "c")
            fl.wait(s, "c")
            fl.inc(s.activation(rowd[:, :], rowd[:, :], AF.Sqrt, bias=EPS), "c")
            fl.wait(v, "c")
            fl.inc(v.reciprocal(rowe[:, :], rowd[:, :]), "c")
            fl.wait(t, "c")
            t.matmul(psr2[:, :], ones1[:, :], rowc[0:1, :], start=True, stop=True)
            fl.inc(t.matmul(psb[:, 0:TCH], ones1[:, :], rowe[0:1, :],
                            start=True, stop=True), "c")
            fl.wait(v, "c")
            lastv = None
            for c in range(DC):
                sl = slice(c * TCH, (c + 1) * TCH)
                v.tensor_sub(uc2[:, sl], h2b[:, sl], psr2[:, :])
                v.tensor_mul(uc2[:, sl], uc2[:, sl], psb[:, 0:TCH])
                lastv = v.tensor_scalar(t2b[:, sl], uc2[:, sl], lgb[:, c:c + 1],
                                        lgb[:, DC + c:DC + c + 1], ALU.mult, ALU.add)
            fl.inc(lastv, "c")
            for dd in range(NDD):
                wpar = dd % 2
                fl.wait(sy, "c")
                fl.dma(sy.dma_start(
                    out=w1_sb[:, wpar * DC * 128:(wpar + 1) * DC * 128]
                        .rearrange("p (c n) -> p c n", c=DC),
                    in_=w1B[l].rearrange("(c p) n -> p c n", p=128)[:, :, dd * 128:(dd + 1) * 128]))
                fl.wait(t, "d")
                fl.wait(t, "c")
                pdst = psc[:, 0:TCH] if dd % 2 == 0 else psc[:, TCH:1024]
                lastm = None
                for c in range(DC):
                    lastm = t.matmul(pdst, w1_sb[:, wpar * DC * 128 + c * 128:
                                                 wpar * DC * 128 + (c + 1) * 128],
                                     t2b[:, c * TCH:(c + 1) * TCH],
                                     start=(c == 0), stop=(c == DC - 1))
                fl.inc(lastm, "c")
                if dd < 22:
                    fl.wait(v, "c")
                    fl.inc(v.tensor_scalar(asb[:, dd * TCH:(dd + 1) * TCH], pdst,
                                           b1_sb[:, dd:dd + 1], None, ALU.add), "c")
                else:
                    de = dd - 22
                    fl.wait(s, "c")
                    fl.inc(s.activation(gg[:, :], pdst, AF.Gelu,
                                        bias=b1_sb[:, dd:dd + 1]), "c")
                    fl.wait(v, "c")
                    fl.inc(v.tensor_mul(h1sb[:, de * TCH:(de + 1) * TCH],
                                        asb[:, de * TCH:(de + 1) * TCH], gg[:, :]), "c")
            fl.wait(sy, "c")
            fl.dma(sy.dma_start(out=uc2[:, :].rearrange("p (c n) -> p c n", c=DC),
                                in_=u32_3[:, :, bass.ds(off, TCH)]))
            fl.wait(v, "d")
            for m in range(DC):
                pdst = psb[:, 0:TCH] if m % 2 == 0 else psb[:, TCH:1024]
                fl.wait(t, "c")
                lastm = None
                for c in range(22):
                    lastm = t.matmul(pdst, w2_sb[:, c * D + m * 128:c * D + (m + 1) * 128],
                                     h1sb[:, c * TCH:(c + 1) * TCH],
                                     start=(c == 0), stop=(c == 21))
                fl.inc(lastm, "c")
                sl = slice(m * TCH, (m + 1) * TCH)
                fl.wait(v, "c")
                fl.inc(v.scalar_tensor_tensor(uc2[:, sl], pdst, b2_sb[:, m:m + 1],
                                              uc2[:, sl], ALU.add, ALU.add), "c")
                fl.wait(s, "c")
                fl.inc(s.copy(ucb2[:, sl], uc2[:, sl]), "c")
            fl.wait(sy, "c")
            fl.dma(sy.dma_start(out=u32_3[:, :, bass.ds(off, TCH)],
                                in_=uc2[:, :].rearrange("p (c n) -> p c n", c=DC)), "st")
            fl.dma(sy.dma_start(out=ubf_3[:, :, bass.ds(offp, TCH)],
                                in_=ucb2[:, :].rearrange("p (c n) -> p c n", c=DC)), "st")
            fl.wait(sy, "st")
        alm.close()
        snap_u(2 * l + 1)

    alf = Alloc(nc)
    cf = alf.sb("p3_cf", [128, DC * TCH], F32)
    uf3 = ufT.rearrange("(c p) n -> p c n", p=128)
    for tc in range(NTCH):
        off = tc * TCH
        fl.dma(sy.dma_start(out=cf[:, :].rearrange("p (c n) -> p c n", c=DC),
                            in_=u32_3[:, :, off:off + TCH]))
        fl.wait(sy, "d")
        fl.dma(sy.dma_start(out=uf3[:, :, off:off + TCH],
                            in_=cf[:, :].rearrange("p (c n) -> p c n", c=DC)), "st")
        fl.wait(sy, "st")
    alf.close()
    al.close()
    return {}



def emit_p4(nc, fl, io):
    x0T = dram(nc, "x0T", (D, NT), F32, io("x0T"))
    xn0T = dram(nc, "xn0T", (D, NT), F32, io("xn0T"))
    attn0T = dram(nc, "attn0T", (D, NT), F32, io("attn0T"))
    ufT = dram(nc, "u_finT", (D, NT), F32, io("u_finT"))
    x1T = dram(nc, "x1T", (D, NT), F32, io("x1T"))
    xn1T = dram(nc, "xn1T", (D, NT), F32, io("xn1T"))
    xn1Tb = dram(nc, "xn1Tb", (D, NT), BF16, io("xn1Tb"))

    v = nc.vector
    s = nc.scalar
    t = nc.tensor
    sy = nc.sync
    al = Alloc(nc)
    a_ = al.sb("p4_a", [128, DC * TCH], F32)
    b_ = al.sb("p4_b", [128, DC * TCH], F32)
    xb = al.sb("p4_xb", [128, DC * TCH], BF16)
    x2b = al.sb("p4_x2", [128, DC * TCH], BF16)
    ys = al.sb("p4_y", [128, DC * TCH], F32)
    yb = al.sb("p4_yb", [128, DC * TCH], BF16)
    onc = al.sb("p4_onc", [128, 1], BF16)
    onr = al.sb("p4_onr", [1, 128], F32)
    rows = al.sb("p4_rows", [1, 4 * TCH], F32)
    ps_s = al.ps("p4_ps_s", [2, TCH], F32)
    ps_mu = al.ps("p4_ps_mu", [128, TCH], F32)
    ps_rs = al.ps("p4_ps_rs", [128, TCH], F32)
    fl.inc(v.memset(onc[:, :], 1.0), "c")
    fl.inc(v.memset(onr[:, :], 1.0), "c")
    MU, RSTD, SS, S2 = 0, TCH, 2 * TCH, 3 * TCH
    v3 = lambda ap: ap.rearrange("(c p) n -> p c n", p=128)
    s3 = lambda sb: sb[:, :].rearrange("p (c n) -> p c n", c=DC)
    srcs = [v3(x0T), v3(xn0T), v3(attn0T), v3(ufT)]
    with fl.loop(0, NTCH, {"d": 64, "c": 12, "st": 48},
                 barrier=[sy, v, s, t]) as tc:
        off = tc * TCH
        fl.dma(sy.dma_start(out=s3(a_), in_=srcs[0][:, :, bass.ds(off, TCH)]))
        fl.dma(sy.dma_start(out=s3(b_), in_=srcs[1][:, :, bass.ds(off, TCH)]))
        fl.wait(v, "d")
        fl.inc(v.tensor_add(a_[:, :], a_[:, :], b_[:, :]), "c")
        fl.wait(sy, "c")
        fl.dma(sy.dma_start(out=s3(b_), in_=srcs[2][:, :, bass.ds(off, TCH)]))
        fl.wait(v, "d")
        fl.inc(v.tensor_add(a_[:, :], a_[:, :], b_[:, :]), "c")
        fl.wait(sy, "c")
        fl.dma(sy.dma_start(out=s3(b_), in_=srcs[3][:, :, bass.ds(off, TCH)]))
        fl.wait(v, "d")
        fl.inc(v.tensor_add(a_[:, :], a_[:, :], b_[:, :]), "c")
        fl.wait(sy, "c")
        fl.dma(sy.dma_start(out=v3(x1T)[:, :, bass.ds(off, TCH)], in_=s3(a_)), "st")
        fl.wait(v, "c")
        fl.inc(v.tensor_copy(xb[:, :], a_[:, :]), "c")
        fl.wait(s, "c")
        fl.inc(s.activation(x2b[:, :], a_[:, :], AF.Square), "c")
        fl.wait(t, "c")
        for c in range(DC):
            t.matmul(ps_s[0:1, :], onc[:, :], xb[:, c * TCH:(c + 1) * TCH],
                     start=(c == 0), stop=(c == DC - 1))
        last = None
        for c in range(DC):
            last = t.matmul(ps_mu[0:1, :], onc[:, :], x2b[:, c * TCH:(c + 1) * TCH],
                            start=(c == 0), stop=(c == DC - 1))
        fl.inc(last, "c")
        fl.wait(v, "c")
        v.tensor_scalar_mul(rows[:, MU:MU + TCH], ps_s[0:1, :], 1.0 / D)
        v.tensor_scalar_mul(rows[:, S2:S2 + TCH], ps_mu[0:1, :], 1.0 / D)
        v.tensor_mul(rows[:, SS:SS + TCH], rows[:, MU:MU + TCH], rows[:, MU:MU + TCH])
        fl.inc(v.tensor_sub(rows[:, S2:S2 + TCH], rows[:, S2:S2 + TCH],
                            rows[:, SS:SS + TCH]), "c")
        fl.wait(s, "c")
        fl.inc(s.activation(rows[:, SS:SS + TCH], rows[:, S2:S2 + TCH],
                            AF.Sqrt, bias=EPS), "c")
        fl.wait(v, "c")
        fl.inc(v.reciprocal(rows[:, RSTD:RSTD + TCH], rows[:, SS:SS + TCH]), "c")
        fl.wait(t, "c")
        t.matmul(ps_mu[:, :], onr[:, :], rows[0:1, MU:MU + TCH], start=True, stop=True)
        fl.inc(t.matmul(ps_rs[:, :], onr[:, :], rows[0:1, RSTD:RSTD + TCH],
                        start=True, stop=True), "c")
        fl.wait(v, "c")
        lastv = None
        for c in range(DC):
            sl = slice(c * TCH, (c + 1) * TCH)
            v.tensor_sub(ys[:, sl], a_[:, sl], ps_mu[:, :])
            lastv = v.tensor_mul(ys[:, sl], ys[:, sl], ps_rs[:, :])
        fl.inc(lastv, "c")
        fl.wait(s, "c")
        fl.inc(s.copy(yb[:, :], ys[:, :]), "c")
        fl.wait(sy, "c")
        fl.dma(sy.dma_start(out=v3(xn1T)[:, :, bass.ds(off, TCH)], in_=s3(ys)), "st")
        fl.dma(sy.dma_start(out=v3(xn1Tb)[:, :, bass.ds(off, TCH)], in_=s3(yb)), "st")
        fl.wait(sy, "st")
    al.close()
    return {}



def emit_p7(nc, fl, io):
    x1T = dram(nc, "x1T", (D, NT), F32, io("x1T"))
    attn1T = dram(nc, "attn1T", (D, NT), F32, io("attn1T"))
    xn1T = dram(nc, "xn1T", (D, NT), F32, io("xn1T"))
    moeT = dram(nc, "moeT", (D, NT), F32, io("moeT"))
    outT = dram(nc, "outT", (D, NT), F32, "out")
    v = nc.vector
    sy = nc.sync
    al = Alloc(nc)
    a_ = al.sb("p7_a", [128, DC * TCH], F32)
    b_ = al.sb("p7_b", [128, DC * TCH], F32)
    v3 = lambda ap: ap.rearrange("(c p) n -> p c n", p=128)
    s3 = lambda sb: sb[:, :].rearrange("p (c n) -> p c n", c=DC)
    srcs = [v3(x1T), v3(attn1T), v3(xn1T), v3(moeT)]
    with fl.loop(0, NTCH, {"d": 64, "c": 3, "st": 16}, barrier=[sy, v]) as tc:
        off = tc * TCH
        fl.dma(sy.dma_start(out=s3(a_), in_=srcs[0][:, :, bass.ds(off, TCH)]))
        for k in range(1, 4):
            fl.dma(sy.dma_start(out=s3(b_), in_=srcs[k][:, :, bass.ds(off, TCH)]))
            fl.wait(v, "d")
            fl.inc(v.tensor_add(a_[:, :], a_[:, :], b_[:, :]), "c")
            if k < 3:
                fl.wait(sy, "c")
        fl.wait(sy, "c")
        fl.dma(sy.dma_start(out=v3(outT)[:, :, bass.ds(off, TCH)], in_=s3(a_)), "st")
        fl.wait(sy, "st")
    al.close()
    return {}



def emit_p5(nc, fl, io):
    xn1Tb = dram(nc, "xn1Tb", (D, NT), BF16, io("xn1Tb"))
    attn1T = dram(nc, "attn1T", (D, NT), F32, io("attn1T"))
    WqT_d = dram(nc, "WqT", (D, D), F32, "in")
    WkT_d = dram(nc, "WkT", (D, D), F32, "in")
    WvT_d = dram(nc, "WvT", (D, D), F32, "in")
    WoT_d = dram(nc, "WoT", (D, D), F32, "in")
    bq_d = dram(nc, "l1_bq", (128, DC), F32, "in")
    bk_d = dram(nc, "l1_bk", (128, DC), F32, "in")
    bv_d = dram(nc, "l1_bv_rep", (128, D), F32, "in")
    bo_d = dram(nc, "l1_bo", (128, DC), F32, "in")
    embKT_d = dram(nc, "embKT", (64, 129), F32, "in")
    embV_d = dram(nc, "embV", (129, 64), F32, "in")
    idf_d = dram(nc, "iden128_p5", (128, 128), F32, "in")
    dlo64_d = dram(nc, "dlo64", (128, 64), F32, "in")
    d65_d = dram(nc, "d65", (128, 65), F32, "in")
    sel65_d = dram(nc, "sel65", (128, 1), F32, "in")
    bmf_d = dram(nc, "bmask_first", (128, 127), F32, "in")
    bml_d = dram(nc, "bmask_last", (128, 127), F32, "in")
    qT = dram(nc, "p5_qT", (D, NT), BF16, "int")
    kT = dram(nc, "p5_kT", (D, NT), BF16, "int")
    Vtok = dram(nc, "p5_vtok", (NT, D), BF16, "int")
    aT = dram(nc, "p5_aT", (D, NT), BF16, "int")
    Zb_t = nc.dram_tensor("p5_zb", [128 * 641 + 512], BF16, kind="Internal")
    Wb_t = nc.dram_tensor("p5_wb", [128 * 513 + 512], BF16, kind="Internal")

    v = nc.vector
    s = nc.scalar
    t = nc.tensor
    sy = nc.sync
    g = nc.gpsimd

    al = Alloc(nc)
    idb = al.sb("p5_idb", [128, 128], BF16)
    idf = al.sb("p5_idf", [128, 128], F32)
    embk = al.sb("p5_embk", [64, 129], BF16)
    embv_sb = al.sb("p5_embv", [128, 64], BF16)
    embv_hi = al.sb("p5_embvh", [1, 64], BF16)
    dlo64 = al.sb("p5_dlo64", [128, 64], F32)
    d65 = al.sb("p5_d65", [128, 65], F32)
    sel65 = al.sb("p5_sel65", [128, 1], F32)
    bmf = al.sb("p5_bmf", [128, 127], BF16)
    bml = al.sb("p5_bml", [128, 127], BF16)
    ones1f = al.sb("p5_ones1f", [1, 128], F32)
    onesb = al.sb("p5_onesb", [128, 191], BF16)
    onesrow = al.sb("p5_onesrow", [1, 512], BF16)
    zrow1 = al.sb("p5_zrow1", [1, 128], BF16)
    wo_sb = al.sb("p5_wo", [128, DC * D], BF16)
    fl.dma(g.dma_start(out=idb[:, :], in_=idf_d), "g")
    fl.dma(g.dma_start(out=embk[:, :], in_=embKT_d), "g")
    fl.dma(g.dma_start(out=embv_sb[:, :], in_=embV_d[0:128, :]), "g")
    fl.dma(g.dma_start(out=embv_hi[:, :], in_=embV_d[128:129, :]), "g")
    fl.dma(g.dma_start(out=bmf[:, :], in_=bmf_d), "g")
    fl.dma(g.dma_start(out=bml[:, :], in_=bml_d), "g")
    fl.dma(g.dma_start(out=wo_sb[:, :].rearrange("p (c n) -> p c n", c=DC),
                       in_=WoT_d.rearrange("(c p) n -> p c n", p=128)), "g")
    fl.wait(sy, "g")
    fl.dma(sy.dma_start(out=idf[:, :], in_=idf_d))
    fl.dma(sy.dma_start(out=dlo64[:, :], in_=dlo64_d))
    fl.dma(sy.dma_start(out=d65[:, :], in_=d65_d))
    fl.dma(sy.dma_start(out=sel65[:, :], in_=sel65_d))
    fl.inc(v.memset(ones1f[:, :], 1.0), "c")
    fl.inc(v.memset(onesb[:, :], 1.0), "c")
    fl.inc(v.memset(onesrow[:, :], 1.0), "c")
    fl.inc(v.memset(zrow1[:, :], 0.0), "c")

    xn3 = xn1Tb.rearrange("(c p) n -> p c n", p=128)
    qT3 = qT.rearrange("(m p) n -> p m n", p=128)
    kT3 = kT.rearrange("(m p) n -> p m n", p=128)
    vt3 = Vtok.rearrange("(a p) n -> p a n", p=128)

    alp = Alloc(nc)
    wq_sb = alp.sb("p5_wq", [128, DC * D], BF16)
    wk_sb = alp.sb("p5_wk", [128, DC * D], BF16)
    wv_sb = alp.sb("p5_wv", [128, DC * D], BF16)
    bqc = alp.sb("p5_bqc", [128, 2 * DC], F32)
    bvr = alp.sb("p5_bvr", [128, D], F32)
    xin = alp.sb("p5_xin", [128, DC * TCH], BF16)
    prj = alp.sb("p5_prj", [128, 2 * DC * TCH], BF16)
    vpr = alp.sb("p5_vpr", [128, 4 * D], BF16)
    psA = alp.ps("p5_psA", [128, 512], F32)
    psB_ = alp.ps("p5_psB", [128, 512], F32)
    fl.wait(g, "d")
    fl.dma(g.dma_start(out=wq_sb[:, :].rearrange("p (c n) -> p c n", c=DC),
                       in_=WqT_d.rearrange("(c p) n -> p c n", p=128)), "g")
    fl.dma(g.dma_start(out=wk_sb[:, :].rearrange("p (c n) -> p c n", c=DC),
                       in_=WkT_d.rearrange("(c p) n -> p c n", p=128)), "g")
    fl.dma(g.dma_start(out=wv_sb[:, :].rearrange("p (c n) -> p c n", c=DC),
                       in_=WvT_d.rearrange("(c p) n -> p c n", p=128)), "g")
    fl.wait(sy, "g")
    fl.dma(sy.dma_start(out=bqc[:, 0:DC], in_=bq_d))
    fl.dma(sy.dma_start(out=bqc[:, DC:2 * DC], in_=bk_d))
    fl.dma(sy.dma_start(out=bvr[:, :], in_=bv_d))
    fl.wait(t, "g")
    fl.wait(t, "d")
    fl.wait(t, "c")
    fl.wait(sy, "g")
    with fl.loop(0, NTCH, {"d": 16, "c": 48, "st": 320}, barrier=[sy, v, s, t]) as tc:
        off = tc * TCH
        fl.dma(sy.dma_start(out=xin[:, :].rearrange("p (c n) -> p c n", c=DC),
                            in_=xn3[:, :, bass.ds(off, TCH)]))
        fl.wait(t, "d")
        for (wsb, bcol, dst3) in ((wq_sb, 0, qT3), (wk_sb, DC, kT3)):
            for m in range(DC):
                pdst = psA if m % 2 == 0 else psB_
                fl.wait(t, "c")
                lastm = None
                for c in range(DC):
                    lastm = t.matmul(pdst[:, :],
                                     wsb[:, c * D + m * 128:c * D + (m + 1) * 128],
                                     xin[:, c * TCH:(c + 1) * TCH],
                                     start=(c == 0), stop=(c == DC - 1))
                fl.inc(lastm, "c")
                fl.wait(v, "c")
                po = 0 if dst3 is qT3 else DC * TCH
                fl.inc(v.tensor_scalar(prj[:, po + m * TCH:po + (m + 1) * TCH],
                                       pdst[:, :], bqc[:, bcol + m:bcol + m + 1],
                                       None, ALU.add), "c")
                fl.wait(sy, "c")
                fl.dma(sy.dma_start(out=dst3[:, m:m + 1, bass.ds(off, TCH)].opt(),
                                    in_=prj[:, po + m * TCH:po + (m + 1) * TCH]), "st")
        for tt in range(4):
            for half in range(2):
                pdst = psA if half == 0 else psB_
                fl.wait(t, "c")
                lastm = None
                for c in range(DC):
                    lastm = t.matmul(pdst[:, :],
                                     xin[:, c * TCH + tt * 128:c * TCH + tt * 128 + 128],
                                     wv_sb[:, c * D + half * 512:c * D + (half + 1) * 512],
                                     start=(c == 0), stop=(c == DC - 1))
                fl.inc(lastm, "c")
                fl.wait(v, "c")
                fl.inc(v.tensor_tensor(vpr[:, tt * D + half * 512:tt * D + (half + 1) * 512],
                                       pdst[:, :], bvr[:, half * 512:(half + 1) * 512],
                                       ALU.add), "c")
            fl.wait(sy, "c")
            fl.dma(sy.dma_start(out=vt3[:, bass.ds(tc * 4 + tt, 1), :].opt(),
                                in_=vpr[:, tt * D:(tt + 1) * D]), "st")
        fl.wait(sy, "st")
    alp.close()

    ala = Alloc(nc)
    qh = ala.sb("p5_qh", [64, T], BF16)
    kh = ala.sb("p5_kh", [64, T], BF16)
    vh = ala.sb("p5_vh", [128, 16 * 64], BF16)
    rpad = ala.sb("p5_rpad", [128, 512], BF16)
    chb = ala.sb("p5_chb", [128, 128], BF16)
    chrow = ala.sb("p5_chrow", [1, 128], BF16)
    zeros128 = ala.sb("p5_z128", [128, 128], BF16)
    zer2k = ala.sb("p5_zer2k", [128, T], BF16)
    pex = ala.sb("p5_pex", [128, T], BF16)
    Csb = ala.sb("p5_Csb", [128, T], F32)
    rblk = ala.sb("p5_rblk", [128, 3 * 128], BF16)
    relf = ala.sb("p5_relf", [128, T], F32)
    onesF = ala.sb("p5_onesF", [128, T], F32)
    lg2 = ala.sb("p5_lg2", [128, T], F32)
    rB = ala.sb("p5_rB", [128, 127], BF16)
    ext = ala.sb("p5_ext", [128, 128], F32)
    cols5 = ala.sb("p5_cols", [128, 8], F32)
    wsum = ala.sb("p5_wsum", [128, 129], BF16)
    T1sb = ala.sb("p5_T1", [128, 128], BF16)
    t2sb = ala.sb("p5_t2", [1, 128], BF16)
    ptbA = ala.sb("p5_ptbA", [128, 128], BF16)
    ptbB = ala.sb("p5_ptbB", [128, 128], BF16)
    rrow = ala.sb("p5_rrow", [1, 128], F32)
    rbb = ala.sb("p5_rbb", [64, 128], F32)
    rsb = ala.sb("p5_rsb", [128, 130], F32)
    aall = ala.sb("p5_aall", [64, T], BF16)
    psL = ala.ps("p5_psL", [128, T], F32)
    psR = ala.ps("p5_psR", [128, 512], F32)
    psO = ala.ps("p5_psO", [64, 128], F32)
    psT_b = ala.ps("p5_psT", [128, 128], BF16)
    fl.inc(v.memset(rpad[:, 0:192], 0.0), "c")
    fl.inc(v.memset(zeros128[:, :], 0.0), "c")
    fl.inc(v.memset(zer2k[:, :], 0.0), "c")
    fl.inc(v.memset(onesF[:, :], 1.0), "c")
    fl.wait(sy, "c")
    fl.dma(sy.dma_start(out=AP(Wb_t, 0, [[513, 128], [1, 513]]),
                        in_=zer2k[:, 0:513]), "st")
    fl.wait(sy, "st")
    qT4 = qT.rearrange("(h d) n -> d h n", d=64)
    kT4 = kT.rearrange("(h d) n -> d h n", d=64)
    aT4 = aT.rearrange("(h d) n -> d h n", d=64)
    with fl.loop(0, 32, {"d": 1040, "c": 1251, "st": 528}, barrier=[sy, v, s, t]) as bh:
        b = bh // 16
        h = bh % 16
        fl.dma(sy.dma_start(out=qh[:, :], in_=qT4[:, bass.ds(h, 1), bass.ds(b * T, T)].opt()))
        fl.dma(sy.dma_start(out=kh[:, :], in_=kT4[:, bass.ds(h, 1), bass.ds(b * T, T)].opt()))
        fl.dma(sy.dma_start(out=vh[:, :].rearrange("p (a n) -> p a n", a=16),
                            in_=vt3[:, bass.ds(b * 16, 16), bass.ds(h * 64, 64)]))
        fl.wait(t, "d")
        for qt in range(16):
            q0 = qt * 128
            qtile = qh[:, q0:q0 + 128]
            fl.wait(t, "c")
            fl.inc(t.matmul(psR[:, 0:129], qtile, embk[:, :], start=True, stop=True), "c")
            fl.wait(v, "c")
            fl.inc(v.tensor_copy(rsb[:, 0:129], psR[:, 0:129]), "c")
            fl.wait(v, "c")
            fl.inc(v.tensor_scalar(rpad[:, 192:321], rsb[:, 0:129], rsb[:, 0:1],
                                   None, ALU.subtract), "c")
            fl.inc(v.tensor_tensor(cols5[:, 6:7], rsb[:, 128:129], rsb[:, 0:1],
                                   ALU.subtract), "c")
            fl.wait(v, "c")
            fl.inc(v.tensor_scalar(rpad[:, 321:512], onesb[:, :], cols5[:, 6:7],
                                   None, ALU.mult), "c")
            fl.wait(sy, "c")
            fl.wait(sy, "d")
            fl.dma(sy.dma_start(out=AP(Zb_t, 0, [[641, 128], [1, 512]]),
                                in_=rpad[:, 0:512]), "st")
            fl.wait(sy, "st")
            bb_lo = max(qt - 1, 0)
            bb_hi = min(qt + 1, 15)
            nbb = bb_hi - bb_lo + 1
            for j, bb in enumerate(range(bb_lo, bb_hi + 1)):
                d0 = (bb - qt) * 128
                fl.dma(sy.dma_start(out=rblk[:, j * 128:(j + 1) * 128],
                                    in_=AP(Zb_t, d0 + 256, [[640, 128], [1, 128]])))
            fl.wait(t, "d")
            fl.wait(t, "c")
            lastm = None
            for a in range(4):
                lastm = t.matmul(psL[:, a * 512:(a + 1) * 512], qtile,
                                 kh[:, a * 512:(a + 1) * 512],
                                 start=True, stop=True)
            fl.inc(lastm, "c")
            klo2 = max(q0 - 128, 0)
            khi2 = min(q0 + 256, T)
            fl.wait(v, "c")
            if klo2 > 0:
                v.memset(relf[:, 0:klo2], 0.0)
            v.tensor_copy(relf[:, klo2:khi2], rblk[:, 0:khi2 - klo2])
            if khi2 < T:
                v.tensor_scalar(relf[:, khi2:T], onesF[:, khi2:T], cols5[:, 6:7],
                                None, ALU.mult)
            fl.inc(v.tensor_copy(cols5[:, 7:8], cols5[:, 6:7]), "c")
            fl.wait(v, "c")
            fl.inc(v.scalar_tensor_tensor(lg2[:, :], psL[:, :], 1.0, relf[:, :],
                                          ALU.mult, ALU.add), "c")
            fl.wait(v, "c")
            fl.inc(v.max(cols5[:, 0:8], lg2[:, :]), "c")
            fl.wait(v, "c")
            fl.inc(v.tensor_scalar_mul(cols5[:, 0:1], cols5[:, 0:1], -1.0), "c")
            fl.wait(s, "c")
            fl.inc(s.activation(pex[:, :], lg2[:, :], AF.Exp, bias=cols5[:, 0:1],
                                accum_out=cols5[:, 1:2]), "c")
            klo = max(q0 - 128, 0)
            khi = min(q0 + 256, T)
            fl.wait(sy, "c")
            fl.wait(sy, "d")
            fl.dma(sy.dma_start(out=AP(Wb_t, 64, [[512, 128], [1, khi - klo]]),
                                in_=pex[:, klo:khi]), "st")
            fl.wait(v, "c")
            fl.inc(v.tensor_tensor_scan(Csb[:, :], pex[:, :], zer2k[:, :], 0.0,
                                        ALU.add, ALU.add), "c")
            fl.wait(v, "c")
            if qt == 0:
                fl.inc(v.tensor_tensor(ext[:, 0:64], Csb[:, 0:64], dlo64[:, :],
                                       ALU.mult), "c")
                fl.wait(v, "c")
                fl.inc(v.tensor_reduce(cols5[:, 2:3], ext[:, 0:64], AX, ALU.add), "c")
            else:
                fl.inc(v.tensor_tensor(ext[:, :], Csb[:, q0 - 64:q0 + 64], idf[:, :],
                                       ALU.mult), "c")
                fl.wait(v, "c")
                fl.inc(v.tensor_reduce(cols5[:, 2:3], ext[:, :], AX, ALU.add), "c")
            fl.wait(v, "c")
            if qt == 15:
                fl.inc(v.tensor_tensor(ext[:, 0:65], Csb[:, 1983:2048], d65[:, :],
                                       ALU.mult), "c")
                fl.wait(v, "c")
                fl.inc(v.tensor_reduce(cols5[:, 3:4], ext[:, 0:65], AX, ALU.add), "c")
            else:
                fl.inc(v.tensor_tensor(ext[:, :], Csb[:, q0 + 63:q0 + 191], idf[:, :],
                                       ALU.mult), "c")
                fl.wait(v, "c")
                fl.inc(v.tensor_reduce(cols5[:, 3:4], ext[:, :], AX, ALU.add), "c")
            fl.wait(v, "c")
            fl.inc(v.tensor_tensor(cols5[:, 4:5], Csb[:, T - 1:T], cols5[:, 3:4],
                                   ALU.subtract), "c")
            if qt == 15:
                fl.wait(v, "c")
                fl.inc(v.tensor_mul(cols5[:, 4:5], cols5[:, 4:5], sel65[:, :]), "c")
            fl.wait(v, "st")
            fl.wait(sy, "st")
            fl.dma(sy.dma_start(out=rB[:, :],
                                in_=AP(Wb_t, 1 if qt == 0 else 129,
                                       [[513, 128], [1, 127]])))
            fl.wait(v, "d")
            if qt == 0:
                fl.inc(v.tensor_mul(rB[:, :], rB[:, :], bmf[:, :]), "c")
            if qt == 15:
                fl.inc(v.tensor_mul(rB[:, :], rB[:, :], bml[:, :]), "c")
            fl.wait(v, "c")
            v.tensor_copy(wsum[:, 1:128], rB[:, :])
            v.tensor_copy(wsum[:, 0:1], cols5[:, 2:3])
            fl.inc(v.tensor_copy(wsum[:, 128:129], cols5[:, 4:5]), "c")
            fl.wait(t, "c")
            fl.inc(t.transpose(psT_b[:, :], wsum[:, 0:128], idb[:, :]), "c")
            fl.wait(v, "c")
            fl.inc(v.tensor_copy(T1sb[:, :], psT_b[:, :]), "c")
            fl.wait(t, "c")
            fl.inc(t.transpose(psT_b[0:1, 0:128], wsum[:, 128:129], idb[:, :]), "c")
            fl.wait(v, "c")
            fl.inc(v.tensor_copy(t2sb[:, :], psT_b[0:1, 0:128]), "c")
            for kt in range(16):
                ptb = ptbA if kt % 2 == 0 else ptbB
                fl.wait(t, "c")
                fl.inc(t.transpose(psT_b[:, :], pex[:, kt * 128:(kt + 1) * 128],
                                   idb[:, :]), "c")
                fl.wait(v, "c")
                fl.inc(v.tensor_copy(ptb[:, :], psT_b[:, :]), "c")
                fl.wait(t, "c")
                fl.inc(t.matmul(psO[:, :], vh[:, kt * 64:(kt + 1) * 64], ptb[:, :],
                                start=(kt == 0), stop=False), "c")
            fl.inc(t.matmul(psO[:, :], embv_sb[:, :], T1sb[:, :],
                            start=False, stop=False), "c")
            fl.inc(t.matmul(psO[:, :], embv_hi[:, :], t2sb[:, :],
                            start=False, stop=True), "c")
            fl.wait(v, "c")
            fl.inc(v.reciprocal(cols5[:, 5:6], cols5[:, 1:2]), "c")
            fl.wait(t, "c")
            fl.inc(t.transpose(psR[0:1, 0:128], cols5[:, 5:6], idf[:, :]), "c")
            fl.wait(v, "c")
            fl.inc(v.tensor_copy(rrow[:, :], psR[0:1, 0:128]), "c")
            fl.wait(t, "c")
            fl.inc(t.matmul(psR[0:64, 128:256], ones1f[:, 0:64], rrow[0:1, :],
                            start=True, stop=True), "c")
            fl.wait(v, "c")
            fl.inc(v.tensor_copy(rbb[:, :], psR[0:64, 128:256]), "c")
            fl.wait(v, "c")
            fl.inc(v.tensor_tensor(aall[:, q0:q0 + 128], psO[:, :],
                                   rbb[:, :], ALU.mult), "c")
        fl.wait(sy, "c")
        fl.dma(sy.dma_start(out=aT4[0:64, bass.ds(h, 1), bass.ds(b * T, T)].opt(),
                            in_=aall[:, :]), "st")
        fl.wait(sy, "st")
    ala.close()

    alo = Alloc(nc)
    boc = alo.sb("p5_boc", [128, DC], F32)
    ain = alo.sb("p5_ain", [128, DC * TCH], BF16)
    oout = alo.sb("p5_oout", [128, DC * TCH], F32)
    psA2 = alo.ps("p5_psA2", [128, 512], F32)
    psB2 = alo.ps("p5_psB2", [128, 512], F32)
    fl.dma(sy.dma_start(out=boc[:, :], in_=bo_d))
    aT3 = aT.rearrange("(c p) n -> p c n", p=128)
    at3o = attn1T.rearrange("(c p) n -> p c n", p=128)
    fl.wait(t, "d")
    with fl.loop(0, NTCH, {"d": 16, "c": 16, "st": 16}, barrier=[sy, v, s, t]) as tc:
        off = tc * TCH
        fl.dma(sy.dma_start(out=ain[:, :].rearrange("p (c n) -> p c n", c=DC),
                            in_=aT3[:, :, bass.ds(off, TCH)]))
        fl.wait(t, "d")
        for m in range(DC):
            pdst = psA2 if m % 2 == 0 else psB2
            fl.wait(t, "c")
            lastm = None
            for c in range(DC):
                lastm = t.matmul(pdst[:, :],
                                 wo_sb[:, c * D + m * 128:c * D + (m + 1) * 128],
                                 ain[:, c * TCH:(c + 1) * TCH],
                                 start=(c == 0), stop=(c == DC - 1))
            fl.inc(lastm, "c")
            fl.wait(v, "c")
            fl.inc(v.tensor_scalar(oout[:, m * TCH:(m + 1) * TCH], pdst[:, :],
                                   boc[:, m:m + 1], None, ALU.add), "c")
        fl.wait(sy, "c")
        fl.dma(sy.dma_start(out=at3o[:, :, bass.ds(off, TCH)],
                            in_=oout[:, :].rearrange("p (c n) -> p c n", c=DC)), "st")
        fl.wait(sy, "st")
    alo.close()
    al.close()
    return {}



KCAP = 512
NEXP = 32
NSLOT = NEXP * KCAP
CGLOB = 1280


def emit_p6(nc, fl, io):
    xn1T = dram(nc, "xn1T", (D, NT), F32, io("xn1T"))
    xn1Tb = dram(nc, "xn1Tb", (D, NT), BF16, io("xn1Tb"))
    moeT = dram(nc, "moeT", (D, NT), F32, io("moeT"))
    wr_d = dram(nc, "w_router", (D, NEXP), F32, "in")
    wg_d = dram(nc, "w_gate", (NEXP, D, D), F32, "in")
    wu_d = dram(nc, "w_up", (NEXP, D, D), F32, "in")
    wd_d = dram(nc, "w_down", (NEXP, D, D), F32, "in")
    iota_d = dram(nc, "iota32r", (128, NEXP), F32, "in")
    tri_d = dram(nc, "tri_strict128", (128, 128), F32, "in")
    idf_d = dram(nc, "iden128", (128, 128), F32, "in")
    Xslots = dram(nc, "p6_xslots", (NSLOT, D), BF16, "int")
    Yslots = dram(nc, "p6_yslots", (NSLOT, D), BF16, "int")
    amat_i = nc.dram_tensor("p6_amat_i", [8, NEXP], F32, kind="Internal").ap()
    amat_o = nc.dram_tensor("p6_amat_o", [8, NEXP], F32, kind="Internal").ap()
    incl_d = dram(nc, "p6_incl", (8, NEXP), F32, "int")

    v = nc.vector
    s = nc.scalar
    t = nc.tensor
    sy = nc.sync
    g = nc.gpsimd
    rank = nc.partition_id()

    al = Alloc(nc)
    wr_sb = al.sb("p6_wr", [128, DC * NEXP], F32)
    iota32 = al.sb("p6_iota", [128, NEXP], F32)
    iotam = al.sb("p6_iotam", [128, NEXP], F32)
    tria = al.sb("p6_tri", [128, 128], F32)
    idf = al.sb("p6_idf", [128, 128], F32)
    idb = al.sb("p6_idb", [128, 128], BF16)
    ones1f = al.sb("p6_ones1", [1, 128], F32)
    onescf = al.sb("p6_onesc", [128, 1], F32)
    big = al.sb("p6_big", [128, 1], F32)
    ecol_a = al.sb("p6_ecol", [128, 32], F32)
    pos_a = al.sb("p6_pos", [128, 32], F32)
    gate_a = al.sb("p6_gate", [128, 32], F32)
    islot = al.sb("p6_islot", [128, 32], I32)
    islot1 = al.sb("p6_islot1", [128, 1], I32)
    rt = al.sb("p6_rt", [1, NEXP], F32)
    capb = al.sb("p6_capb", [128, NEXP], F32)
    xn1_3 = xn1T.rearrange("(c p) n -> p c n", p=128)
    xn1b_3 = xn1Tb.rearrange("(c p) n -> p c n", p=128)
    moe_3 = moeT.rearrange("(c p) n -> p c n", p=128)
    xs_3 = Xslots.rearrange("(a p) n -> p a n", p=128)
    ys_3 = Yslots.rearrange("(a p) n -> p a n", p=128)

    fl.inc(v.memset(ones1f[:, :], 1.0), "c")
    fl.inc(v.memset(onescf[:, :], 1.0), "c")
    fl.inc(v.memset(big[:, :], 1.0e9), "c")
    fl.inc(v.memset(rt[:, :], 0.0), "c")
    fl.dma(g.dma_start(out=idb[:, :], in_=idf_d), "g")
    fl.wait(sy, "g")
    fl.dma(sy.dma_start(out=wr_sb[:, :].rearrange("p (c n) -> p c n", c=DC),
                        in_=wr_d.rearrange("(c p) n -> p c n", p=128)))
    fl.dma(sy.dma_start(out=iota32[:, :], in_=iota_d))
    fl.dma(sy.dma_start(out=tria[:, :], in_=tri_d))
    fl.dma(sy.dma_start(out=idf[:, :], in_=idf_d))
    fl.wait(v, "d")
    fl.inc(v.tensor_scalar(iotam[:, :], iota32[:, :], -1.0e4, None, ALU.add), "c")

    alr = Alloc(nc)
    xt_sb = alr.sb("p6_xt", [128, DC * 128], F32)
    lg_sb = alr.sb("p6_lg", [128, NEXP], F32)
    esc = alr.sb("p6_esc", [128, NEXP], F32)
    oh = alr.sb("p6_oh", [128, NEXP], F32)
    oh2 = alr.sb("p6_oh2", [128, NEXP], F32)
    tmp = alr.sb("p6_tmp", [128, NEXP], F32)
    m8 = alr.sb("p6_m8", [128, 8], F32)
    ng = alr.sb("p6_ng", [128, 1], F32)
    den = alr.sb("p6_den", [128, 1], F32)
    ps1 = alr.ps("p6_ps1", [128, 512], F32)
    ps2 = alr.ps("p6_ps2", [128, 512], F32)
    ps3 = alr.ps("p6_ps3", [128, 512], F32)
    ps4 = alr.ps("p6_ps4", [128, 512], F32)
    fl.wait(t, "d")
    fl.wait(t, "c")
    fl.wait(sy, "g")
    with fl.loop(0, 32, {"d": 16, "c": 17}, barrier=[sy, v, s, t]) as ti:
        toff = ti * 128
        fl.dma(sy.dma_start(out=xt_sb[:, :].rearrange("p (c n) -> p c n", c=DC),
                            in_=xn1_3[:, :, bass.ds(toff, 128)]))
        fl.wait(t, "d")
        fl.wait(t, "c")
        lastm = None
        for c in range(DC):
            lastm = t.matmul(ps1[:, 0:NEXP], xt_sb[:, c * 128:(c + 1) * 128],
                             wr_sb[:, c * NEXP:(c + 1) * NEXP],
                             start=(c == 0), stop=(c == DC - 1))
        fl.inc(lastm, "c")
        fl.wait(v, "c")
        fl.inc(v.tensor_copy(lg_sb[:, :], ps1[:, 0:NEXP]), "c")
        fl.wait(v, "c")
        fl.inc(v.max(m8[:, :], lg_sb[:, :]), "c")
        fl.wait(v, "c")
        fl.inc(v.tensor_scalar_mul(ng[:, :], m8[:, 0:1], -1.0), "c")
        fl.inc(v.tensor_scalar(oh[:, :], lg_sb[:, :], m8[:, 0:1], None,
                               ALU.is_ge), "c")
        fl.wait(s, "c")
        fl.inc(s.activation(esc[:, :], lg_sb[:, :], AF.Exp, bias=ng[:, 0:1],
                            accum_out=den[:, 0:1]), "c")
        fl.wait(v, "c")
        fl.inc(v.reciprocal(gate_a[:, bass.ds(ti, 1)], den[:, 0:1]), "c")
        fl.inc(v.tensor_mul(tmp[:, :], oh[:, :], iotam[:, :]), "c")
        fl.wait(v, "c")
        fl.inc(v.tensor_scalar(tmp[:, :], tmp[:, :], 1.0e4, None, ALU.add), "c")
        fl.wait(v, "c")
        fl.inc(v.tensor_reduce(ecol_a[:, bass.ds(ti, 1)], tmp[:, :], AX,
                               ALU.min), "c")
        fl.wait(v, "c")
        fl.inc(v.tensor_scalar(oh2[:, :], iota32[:, :],
                               ecol_a[:, bass.ds(ti, 1)], None, ALU.is_equal), "c")
        fl.wait(t, "c")
        t.matmul(ps2[:, 0:NEXP], tria[:, :], oh2[:, :], start=True, stop=True)
        t.matmul(ps4[0:1, 0:NEXP], onescf[:, :], oh2[:, :], start=True, stop=True)
        fl.inc(t.matmul(ps3[:, 0:NEXP], ones1f[:, :], rt[0:1, :],
                        start=True, stop=True), "c")
        fl.wait(v, "c")
        fl.inc(v.tensor_copy(tmp[:, :], ps2[:, 0:NEXP]), "c")
        fl.wait(v, "c")
        fl.inc(v.tensor_tensor(tmp[:, :], tmp[:, :], ps3[:, 0:NEXP], ALU.add), "c")
        fl.wait(v, "c")
        fl.inc(v.tensor_mul(tmp[:, :], tmp[:, :], oh2[:, :]), "c")
        fl.wait(v, "c")
        fl.inc(v.tensor_reduce(pos_a[:, bass.ds(ti, 1)], tmp[:, :], AX,
                               ALU.add), "c")
        fl.inc(v.tensor_add(rt[0:1, :], rt[0:1, :], ps4[0:1, 0:NEXP]), "c")

    zz8 = alr.sb("p6_zz8", [NEXP, 8], F32)
    sc8 = alr.sb("p6_sc8", [NEXP, 8], F32)
    incl = alr.sb("p6_incl_sb", [NEXP, 8], F32)
    a8 = alr.sb("p6_a8", [8, NEXP], F32)
    prow = alr.sb("p6_prow", [1, NEXP], F32)
    caprow = alr.sb("p6_caprow", [1, NEXP], F32)
    fl.wait(v, "c")
    fl.inc(v.memset(zz8[:, :], 0.0), "c")
    fl.inc(v.memset(a8[:, :], 0.0), "c")
    fl.wait(sy, "c")
    fl.dma(sy.dma_start(out=amat_i, in_=a8[:, :]), "st")
    fl.wait(sy, "st")
    fl.dma(sy.dma_start(out=amat_i[bass.ds(rank, 1), :], in_=rt[0:1, :]), "st")
    fl.wait(g, "st")
    fl.inc(g.collective_compute("AllReduce", ALU.add, [[0, 1, 2, 3, 4, 5, 6, 7]],
                                [amat_i], [amat_o], cc_dim="Partition"), "cc")
    fl.wait(sy, "cc")
    fl.dma(sy.dma_start(out=a8[:, :], in_=amat_o))
    fl.wait(t, "d")
    fl.wait(t, "c")
    fl.inc(t.transpose(ps1[0:NEXP, 0:8], a8[:, :], idf[0:8, 0:8]), "c")
    fl.wait(v, "c")
    fl.inc(v.tensor_copy(sc8[:, :], ps1[0:NEXP, 0:8]), "c")
    fl.wait(v, "c")
    fl.inc(v.tensor_tensor_scan(incl[:, :], sc8[:, :], zz8[:, :], 0.0,
                                ALU.add, ALU.add), "c")
    fl.wait(t, "c")
    fl.inc(t.transpose(ps2[0:8, 0:NEXP], incl[:, :], idf[0:NEXP, 0:NEXP]), "c")
    fl.wait(v, "c")
    fl.inc(v.tensor_copy(a8[:, :], ps2[0:8, 0:NEXP]), "c")
    fl.wait(sy, "c")
    fl.dma(sy.dma_start(out=incl_d, in_=a8[:, :]), "st")
    fl.wait(sy, "st")
    fl.dma(sy.dma_start(out=prow[:, :], in_=incl_d[bass.ds(rank, 1), :]))
    fl.wait(v, "d")
    fl.inc(v.tensor_sub(prow[:, :], prow[:, :], rt[0:1, :]), "c")
    fl.wait(v, "c")
    fl.inc(v.tensor_scalar(caprow[:, :], prow[:, :], -1.0, float(CGLOB),
                           ALU.mult, ALU.add), "c")
    fl.wait(v, "c")
    fl.inc(v.tensor_scalar(caprow[:, :], caprow[:, :], 0.0, float(KCAP),
                           ALU.max, ALU.min), "c")
    fl.wait(t, "c")
    fl.inc(t.matmul(ps2[:, 0:NEXP], ones1f[:, :], caprow[0:1, :],
                    start=True, stop=True), "c")
    fl.wait(v, "c")
    fl.inc(v.tensor_copy(capb[:, :], ps2[:, 0:NEXP]), "c")
    if DEBUG_INT_OUT:
        for dbn, dbuf, dbw in (("lg", lg_sb, 32), ("m8", m8, 8), ("oh", oh, 32),
                               ("tmp", tmp, 32), ("ec", ecol_a, 32),
                               ("pos", pos_a, 32), ("gate", gate_a, 32)):
            dd2 = dram(nc, "p6_dbg_" + dbn, (128, dbw), F32, "out")
            fl.wait(sy, "c")
            fl.dma(sy.dma_start(out=dd2, in_=dbuf[:, 0:dbw]), "st")
        fl.wait(sy, "st")
        d_a8 = dram(nc, "p6_dbg_a8", (8, NEXP), F32, "out")
        d_cap = dram(nc, "p6_dbg_cap", (1, NEXP), F32, "out")
        d_rt = dram(nc, "p6_dbg_rt", (1, NEXP), F32, "out")
        fl.wait(sy, "c")
        fl.dma(sy.dma_start(out=d_a8, in_=a8[:, :]), "st")
        fl.dma(sy.dma_start(out=d_cap, in_=caprow[:, :]), "st")
        fl.dma(sy.dma_start(out=d_rt, in_=rt[0:1, :]), "st")
        fl.wait(sy, "st")

    xcol = alr.sb("p6_xcol", [128, DC * 128], BF16)
    xrow = alr.sb("p6_xrow", [128, D], BF16)
    slotf = alr.sb("p6_slotf", [128, 4], F32)
    acc_i = alr.sb("p6_acci", [128, 1], I32)
    pst = alr.ps("p6_pst", [128, 128], BF16)
    with fl.loop(0, 32, {"d": 16, "c": 24, "g": 16}, barrier=[sy, v, s, t, g]) as ti:
        toff = ti * 128
        fl.dma(sy.dma_start(out=xcol[:, :].rearrange("p (c n) -> p c n", c=DC),
                            in_=xn1b_3[:, :, bass.ds(toff, 128)]))
        fl.wait(t, "d")
        for c in range(DC):
            fl.wait(t, "c")
            fl.inc(t.transpose(pst[:, :], xcol[:, c * 128:(c + 1) * 128],
                               idb[:, :]), "c")
            fl.wait(v, "c")
            fl.inc(v.tensor_copy(xrow[:, c * 128:(c + 1) * 128], pst[:, :]), "c")
        fl.wait(v, "c")
        fl.inc(v.tensor_scalar(oh2[:, :], iota32[:, :],
                               ecol_a[:, bass.ds(ti, 1)], None, ALU.is_equal), "c")
        fl.wait(v, "c")
        fl.inc(v.tensor_mul(tmp[:, :], oh2[:, :], capb[:, :]), "c")
        fl.wait(v, "c")
        fl.inc(v.tensor_reduce(slotf[:, 0:1], tmp[:, :], AX, ALU.add), "c")
        fl.wait(v, "c")
        fl.inc(v.tensor_tensor(acc_i[:, :], pos_a[:, bass.ds(ti, 1)],
                               slotf[:, 0:1], ALU.is_lt), "c")
        fl.inc(v.scalar_tensor_tensor(slotf[:, 2:3], ecol_a[:, bass.ds(ti, 1)],
                                      float(KCAP), pos_a[:, bass.ds(ti, 1)],
                                      ALU.mult, ALU.add), "c")
        fl.wait(v, "c")
        fl.inc(v.select(slotf[:, 3:4], acc_i[:, :], slotf[:, 2:3],
                        big[:, 0:1]), "c")
        fl.wait(v, "c")
        fl.inc(v.tensor_copy(islot1[:, :], slotf[:, 3:4]), "c")
        fl.wait(v, "c")
        fl.inc(v.tensor_copy(islot[:, bass.ds(ti, 1)], islot1[:, :]), "c")
        fl.wait(g, "c")
        fl.dma(g.indirect_dma_start(
            out=Xslots[:, :], out_offset=IndirectOffsetOnAxis(islot1[:, 0:1], 0),
            in_=xrow[:, :], in_offset=None,
            bounds_check=NSLOT - 1, oob_is_err=False), "g")
    alr.close()

    alw = Alloc(nc)
    wga = alw.sb("p6_wga", [128, DC * D], BF16)
    wua = alw.sb("p6_wua", [128, DC * D], BF16)
    wda = alw.sb("p6_wda", [128, DC * D], BF16)
    xe = alw.sb("p6_xe", [128, 4 * D], BF16)
    xeT = alw.sb("p6_xeT", [128, DC * 128], BF16)
    h1sb = alw.sb("p6_h1", [128, D], BF16)
    h1T = alw.sb("p6_h1T", [128, DC * 128], BF16)
    gsc = alw.sb("p6_gsc", [128, 512], BF16)
    usb = alw.sb("p6_usb", [128, 512], BF16)
    ysb = alw.sb("p6_ysb", [128, 4 * D], BF16)
    psG0 = alw.ps("p6_psG0", [128, 512], F32)
    psG1 = alw.ps("p6_psG1", [128, 512], F32)
    psU0 = alw.ps("p6_psU0", [128, 512], F32)
    psU1 = alw.ps("p6_psU1", [128, 512], F32)
    psD0 = alw.ps("p6_psD0", [128, 512], F32)
    psD1 = alw.ps("p6_psD1", [128, 512], F32)
    pstE = alw.ps("p6_pstE", [128, 128], BF16)
    wg3 = wg_d.rearrange("e (c p) n -> p e c n", p=128)
    wu3 = wu_d.rearrange("e (c p) n -> p e c n", p=128)
    wd3 = wd_d.rearrange("e (c p) n -> p e c n", p=128)
    with fl.loop(0, NEXP, {"d": 16, "c": 184, "g": 48, "st": 16},
                 barrier=[sy, v, s, t, g]) as ee:
        fl.dma(g.dma_start(out=wga[:, :].rearrange("p (c n) -> p c n", c=DC),
                           in_=wg3[:, bass.ds(ee, 1), :, :].opt()), "g")
        fl.dma(g.dma_start(out=wua[:, :].rearrange("p (c n) -> p c n", c=DC),
                           in_=wu3[:, bass.ds(ee, 1), :, :].opt()), "g")
        fl.dma(g.dma_start(out=wda[:, :].rearrange("p (c n) -> p c n", c=DC),
                           in_=wd3[:, bass.ds(ee, 1), :, :].opt()), "g")
        fl.wait(sy, "g")
        fl.dma(sy.dma_start(out=xe[:, :].rearrange("p (k n) -> p k n", k=4),
                            in_=xs_3[:, bass.ds(ee * 4, 4), :]))
        for tt in range(4):
            for c in range(DC):
                fl.wait(t, "c")
                if tt == 0 and c == 0:
                    fl.wait(t, "d")
                fl.inc(t.transpose(pstE[:, :],
                                   xe[:, tt * D + c * 128:tt * D + (c + 1) * 128],
                                   idb[:, :]), "c")
                fl.wait(v, "c")
                fl.inc(v.tensor_copy(xeT[:, c * 128:(c + 1) * 128], pstE[:, :]), "c")
            fl.wait(t, "c")
            if tt == 0:
                fl.wait(t, "g")
            for fc, pdst in ((0, psG0), (1, psG1)):
                lastm = None
                for hc in range(DC):
                    lastm = t.matmul(pdst[:, :], xeT[:, hc * 128:(hc + 1) * 128],
                                     wga[:, hc * D + fc * 512:hc * D + (fc + 1) * 512],
                                     start=(hc == 0), stop=(hc == DC - 1))
                fl.inc(lastm, "c")
            for fc, pdst in ((0, psU0), (1, psU1)):
                lastm = None
                for hc in range(DC):
                    lastm = t.matmul(pdst[:, :], xeT[:, hc * 128:(hc + 1) * 128],
                                     wua[:, hc * D + fc * 512:hc * D + (fc + 1) * 512],
                                     start=(hc == 0), stop=(hc == DC - 1))
                fl.inc(lastm, "c")
            for fc in range(2):
                fl.wait(s, "c")
                fl.inc(s.activation(gsc[:, :], (psG0 if fc == 0 else psG1)[:, :],
                                    AF.Silu), "c")
                fl.inc(s.copy(usb[:, :], (psU0 if fc == 0 else psU1)[:, :]), "c")
                fl.wait(v, "c")
                fl.inc(v.tensor_mul(h1sb[:, fc * 512:(fc + 1) * 512],
                                    gsc[:, :], usb[:, :]), "c")
            for c in range(DC):
                fl.wait(t, "c")
                fl.inc(t.transpose(pstE[:, :], h1sb[:, c * 128:(c + 1) * 128],
                                   idb[:, :]), "c")
                fl.wait(v, "c")
                fl.inc(v.tensor_copy(h1T[:, c * 128:(c + 1) * 128], pstE[:, :]), "c")
            fl.wait(t, "c")
            for oc, pdst in ((0, psD0), (1, psD1)):
                lastm = None
                for fc in range(DC):
                    lastm = t.matmul(pdst[:, :], h1T[:, fc * 128:(fc + 1) * 128],
                                     wda[:, fc * D + oc * 512:fc * D + (oc + 1) * 512],
                                     start=(fc == 0), stop=(fc == DC - 1))
                fl.inc(lastm, "c")
            for oc in range(2):
                fl.wait(v, "c")
                fl.inc(v.tensor_copy(ysb[:, tt * D + oc * 512:tt * D + (oc + 1) * 512],
                                     (psD0 if oc == 0 else psD1)[:, :]), "c")
        fl.wait(sy, "c")
        fl.dma(sy.dma_start(out=ys_3[:, bass.ds(ee * 4, 4), :],
                            in_=ysb[:, :].rearrange("p (k n) -> p k n", k=4)), "st")
    alw.close()

    alg = Alloc(nc)
    ytk = alg.sb("p6_ytk", [128, D], BF16)
    ytk32 = alg.sb("p6_ytk32", [128, D], F32)
    ytkb = alg.sb("p6_ytkb", [128, D], BF16)
    ymT = alg.sb("p6_ymT", [128, DC * 128], F32)
    pstG = alg.ps("p6_pstG", [128, 128], BF16)
    fl.wait(g, "st")
    with fl.loop(0, 32, {"c": 21, "g": 16, "st": 16}, barrier=[sy, v, s, t, g]) as ti:
        toff = ti * 128
        fl.wait(v, "g")
        fl.inc(v.memset(ytk[:, :], 0.0), "c")
        fl.wait(v, "c")
        fl.inc(v.tensor_copy(islot1[:, :], islot[:, bass.ds(ti, 1)]), "c")
        fl.wait(g, "c")
        fl.dma(g.indirect_dma_start(
            out=ytk[:, :], out_offset=None,
            in_=Yslots[:, :], in_offset=IndirectOffsetOnAxis(islot1[:, 0:1], 0),
            bounds_check=NSLOT - 1, oob_is_err=False), "g")
        fl.wait(v, "g")
        fl.inc(v.tensor_copy(ytk32[:, :], ytk[:, :]), "c")
        fl.wait(v, "c")
        fl.inc(v.tensor_scalar(ytk32[:, :], ytk32[:, :],
                               gate_a[:, bass.ds(ti, 1)], None, ALU.mult), "c")
        fl.wait(v, "c")
        fl.inc(v.tensor_copy(ytkb[:, :], ytk32[:, :]), "c")
        for c in range(DC):
            fl.wait(t, "c")
            fl.inc(t.transpose(pstG[:, :], ytkb[:, c * 128:(c + 1) * 128],
                               idb[:, :]), "c")
            fl.wait(v, "c")
            fl.inc(v.tensor_copy(ymT[:, c * 128:(c + 1) * 128], pstG[:, :]), "c")
        fl.wait(sy, "c")
        fl.dma(sy.dma_start(out=moe_3[:, :, bass.ds(toff, 128)],
                            in_=ymT[:, :].rearrange("p (c n) -> p c n", c=DC)), "st")
        fl.wait(sy, "st")
    alg.close()
    al.close()
    return {}



def _colpack(vec):
    v = np.asarray(vec, np.float32)
    k = v.size // 128
    return np.ascontiguousarray(v.reshape(k, 128).T)


def host_prep(inputs):
    h = {}
    x0 = np.asarray(inputs["L0_attn_query"], np.float32)
    percore = []
    for c in range(NCORES):
        xc = x0[2 * c:2 * c + 2].reshape(NT, D)
        percore.append({"x0T": np.ascontiguousarray(xc.T)})
    h["percore"] = percore

    sh = {}
    sh["qwT"] = np.ascontiguousarray(np.asarray(inputs["L0_attn_q_w"], np.float32).T)
    sh["kwT"] = np.ascontiguousarray(np.asarray(inputs["L0_attn_k_w"], np.float32).T)
    sh["vwT"] = np.ascontiguousarray(np.asarray(inputs["L0_attn_v_w"], np.float32).T)
    sh["owT"] = np.ascontiguousarray(np.asarray(inputs["L0_attn_o_w"], np.float32).T)
    order = ["ln_q_g", "ln_q_b", "ln_k_g", "ln_k_b", "ln_v_g", "ln_v_b",
             "q_b", "k_b", "v_b", "o_b"]
    sh["a_gb"] = np.stack([_colpack(inputs["L0_attn_" + nm]) for nm in order])
    sh["a_vb_rep"] = np.ascontiguousarray(
        np.tile(np.asarray(inputs["L0_attn_v_b"], np.float32).reshape(1, D), (128, 1)))
    sh["beta_param"] = np.asarray(inputs["L0_attn_beta_param"], np.float32).reshape(1, 16)

    HID = 2730
    HIDP = 2816
    wqkv = np.asarray(inputs["L0_ffn_wqkv"], np.float32)
    w1 = np.asarray(inputs["L0_ffn_w1"], np.float32)
    b1 = np.asarray(inputs["L0_ffn_b1"], np.float32)
    w2 = np.asarray(inputs["L0_ffn_w2"], np.float32)
    b2 = np.asarray(inputs["L0_ffn_b2"], np.float32)
    wo = np.asarray(inputs["L0_ffn_wo"], np.float32)
    w1p = np.zeros((2, D, 2 * HIDP), np.float32)
    w1p[:, :, :HID] = w1[:, :, :HID]
    w1p[:, :, HIDP:HIDP + HID] = w1[:, :, HID:]
    b1p = np.zeros((2, 2 * HIDP), np.float32)
    b1p[:, :HID] = b1[:, :HID]
    b1p[:, HIDP:HIDP + HID] = b1[:, HID:]
    w2p = np.zeros((2, HIDP, D), np.float32)
    w2p[:, :HID, :] = w2
    for l in range(2):
        sh[f"f{l}_wqkv"] = np.ascontiguousarray(wqkv[l])
        sh[f"f{l}_w1"] = np.ascontiguousarray(w1p[l])
        sh[f"f{l}_b1"] = _colpack(b1p[l])
        sh[f"f{l}_w2"] = np.ascontiguousarray(w2p[l])
        sh[f"f{l}_b2"] = _colpack(b2[l])
        sh[f"f{l}_wo"] = np.ascontiguousarray(wo[l])
        qs = np.asarray(inputs["L0_ffn_q_scale"], np.float32)[l]
        ks = np.asarray(inputs["L0_ffn_k_scale"], np.float32)[l]
        sh[f"f{l}_qs"] = np.ascontiguousarray(np.tile(qs, 2).reshape(128, 1))
        sh[f"f{l}_ks"] = np.ascontiguousarray(np.tile(ks, 2).reshape(128, 1))
        sh[f"f{l}_lg"] = _colpack(np.asarray(inputs["L0_ffn_ln_g"], np.float32)[l])
        sh[f"f{l}_lb"] = _colpack(np.asarray(inputs["L0_ffn_ln_b"], np.float32)[l])

    sh["WqT"] = np.ascontiguousarray(np.asarray(inputs["L1_attn_Wq"], np.float32).T)
    sh["WkT"] = np.ascontiguousarray(np.asarray(inputs["L1_attn_Wk"], np.float32).T)
    sh["WvT"] = np.ascontiguousarray(np.asarray(inputs["L1_attn_Wv"], np.float32).T)
    sh["WoT"] = np.ascontiguousarray(np.asarray(inputs["L1_attn_Wo"], np.float32).T)
    for nm in ["bq", "bk", "bv", "bo"]:
        sh["l1_" + nm] = _colpack(inputs["L1_attn_" + nm])
    sh["l1_bv_rep"] = np.ascontiguousarray(
        np.tile(np.asarray(inputs["L1_attn_bv"], np.float32).reshape(1, D), (128, 1)))
    sh["embKT"] = np.ascontiguousarray(np.asarray(inputs["L1_attn_embK"], np.float32).T)
    sh["embV"] = np.ascontiguousarray(np.asarray(inputs["L1_attn_embV"], np.float32))

    sh["w_router"] = np.ascontiguousarray(np.asarray(inputs["L1_ffn_w_router"], np.float32))
    sh["w_gate"] = np.asarray(inputs["L1_ffn_w_gate"], np.float32)
    sh["w_up"] = np.asarray(inputs["L1_ffn_w_up"], np.float32)
    sh["w_down"] = np.asarray(inputs["L1_ffn_w_down"], np.float32)

    co = {}
    co["iden_bf"] = np.eye(128, dtype=np.float32)
    DIM_HEAD, WINDOW = 64, 512
    half = np.arange(0, DIM_HEAD, 2, dtype=np.float64)
    inv_freq = 1.0 / 10000.0 ** (half / DIM_HEAD)
    tpos = np.arange(T, dtype=np.float64)
    freqs = tpos[:, None] * inv_freq[None, :]
    freqs = np.concatenate([freqs, freqs], axis=-1)
    base = (half + 0.4 * DIM_HEAD) / (1.4 * DIM_HEAD)
    power = (tpos - T // 2) / float(WINDOW // 2)
    scale = base[None, :] ** power[:, None]
    scale = np.concatenate([scale, scale], axis=-1)
    cos, sin = np.cos(freqs), np.sin(freqs)
    sgn = np.where(np.arange(DIM_HEAD) < DIM_HEAD // 2, -1.0, 1.0)
    CA_q = (cos * scale).T
    CB_q = (sin * scale * sgn[:, None].T).T if False else (sin * scale * sgn[None, :]).T
    CA_k = (cos / scale).T
    CB_k = (sin / scale * sgn[None, :]).T
    def tile_tab(tab):
        tt = np.concatenate([tab, tab], axis=0)
        return np.ascontiguousarray(np.concatenate([tt, tt], axis=1).astype(np.float32))
    co["rot_caq"] = tile_tab(CA_q)
    co["rot_cbq"] = tile_tab(CB_q)
    co["rot_cak"] = tile_tab(CA_k)
    co["rot_cbk"] = tile_tab(CB_k)
    kk = np.arange(128)
    qq = np.arange(512)
    masks = []
    for kt in range(8):
        j = kt * 128 + kk[:, None]
        i = 512 + qq[None, :]
        m = ((i - j) >= 0) & ((i - j) <= 512)
        masks.append(m)
    lm = np.concatenate(masks, axis=0).astype(np.float32)
    lm0 = lm.copy()
    lm0[:512] = 0.0
    co["lmask2"] = np.ascontiguousarray(np.stack([lm, lm0]))
    hm = np.zeros((4, 128, 8), np.float32)
    for c in range(4):
        hm[c, 0:64, 2 * c] = 1.0
        hm[c, 64:128, 2 * c + 1] = 1.0
    co["hm4"] = hm
    co["hm4T"] = np.ascontiguousarray(hm.transpose(0, 2, 1))
    co["w0tab"] = np.array([[1, 0, 0, 0, 1, 0, 0, 0]], np.int32)
    qq = np.arange(128)[:, None]
    w_ = np.arange(192)[None, :]
    co["pmask_lo"] = ((w_ >= 128 * (qq >= 64)) & (w_ <= 64 + qq)).astype(np.float32)
    w2_ = np.arange(256)[None, :]
    co["pmask_hi"] = ((w2_ >= qq + 64) & (w2_ < 128 * (1 + (qq >= 64)))).astype(np.float32)
    a16 = np.arange(16)
    co["tri_lo16"] = (a16[:, None] < a16[None, :]).astype(np.float32)
    co["tri_hi16"] = (a16[:, None] > a16[None, :]).astype(np.float32)
    co["selq64"] = (np.arange(128) < 64).astype(np.float32).reshape(128, 1)
    vv = np.arange(127)[None, :]
    co["bmask_first"] = ((qq - 63 + vv) >= 0).astype(np.float32)
    co["bmask_last"] = ((1920 + qq - 63 + vv) < 2048).astype(np.float32)
    co["iota32_col"] = np.arange(32, dtype=np.float32).reshape(32, 1)
    co["iota32r"] = np.ascontiguousarray(
        np.tile(np.arange(32, dtype=np.float32), (128, 1)))
    co["tri_strict128"] = (np.arange(128)[:, None] <
                           np.arange(128)[None, :]).astype(np.float32)
    co["iden128"] = np.eye(128, dtype=np.float32)
    co["iden128_p5"] = np.eye(128, dtype=np.float32)
    qq128 = np.arange(128)[:, None]
    co["dlo64"] = (np.arange(64)[None, :] == qq128 - 64).astype(np.float32)
    co["d65"] = (np.arange(65)[None, :] == qq128).astype(np.float32)
    co["sel65"] = (qq128 < 65).astype(np.float32)
    co["tri8"] = (np.arange(8)[:, None] <= np.arange(8)[None, :]).astype(np.float32)
    h["shared"] = sh
    h["const"] = co
    return h



def build_p1():
    nc = bass.Bass("TRN2", target_bir_lowering=False, detect_race_conditions=False)
    fl = Flow(nc)
    register_consts(nc, [EPS])
    io = lambda name: "in" if name == "x0T" else "out"
    emit_p1(nc, fl, io)
    return nc


def build_p3():
    nc = bass.Bass("TRN2", target_bir_lowering=False, detect_race_conditions=False)
    fl = Flow(nc)
    register_consts(nc, [EPS, 1.0, 8.0])
    io = lambda name: "in" if name == "xn0T" else "out"
    emit_p3(nc, fl, io)
    return nc


def build_p4():
    nc = bass.Bass("TRN2", target_bir_lowering=False, detect_race_conditions=False)
    fl = Flow(nc)
    register_consts(nc, [EPS])
    ins = {"x0T", "xn0T", "attn0T", "u_finT"}
    io = lambda name: "in" if name in ins else "out"
    emit_p4(nc, fl, io)
    return nc


def build_p7():
    nc = bass.Bass("TRN2", target_bir_lowering=False, detect_race_conditions=False)
    fl = Flow(nc)
    register_consts(nc, [EPS])
    io = lambda name: "in"
    emit_p7(nc, fl, io)
    return nc


def build_p6():
    nc = bass.Bass("TRN2", target_bir_lowering=False, detect_race_conditions=False,
                   num_devices=8)
    fl = Flow(nc)
    register_consts(nc, [EPS, 1.0, 8.0])
    ins = {"xn1T", "xn1Tb"}
    io = lambda name: "in" if name in ins else "out"
    emit_p6(nc, fl, io)
    return nc


def build_p5():
    nc = bass.Bass("TRN2", target_bir_lowering=False, detect_race_conditions=False)
    fl = Flow(nc)
    register_consts(nc, [EPS, 1.0, 8.0])
    io = lambda name: "in" if name == "xn1Tb" else "out"
    emit_p5(nc, fl, io)
    return nc


def build_p2():
    nc = bass.Bass("TRN2", target_bir_lowering=False, detect_race_conditions=False)
    fl = Flow(nc)
    register_consts(nc, [EPS, 1.0])
    io = lambda name: "in" if name == "t1Tb" else "out"
    emit_p2(nc, fl, io)
    return nc



def build_full():
    nc = bass.Bass("TRN2", target_bir_lowering=False, detect_race_conditions=False,
                   num_devices=8)
    fl = Flow(nc)
    register_consts(nc, [EPS, 1.0, 8.0])
    ins = {"x0T"}
    mk = lambda out_names: (lambda name: "in" if name in ins else
                            ("out" if name in out_names else "int"))
    def fence():
        for e in (nc.sync, nc.vector, nc.scalar, nc.tensor, nc.gpsimd):
            for s_ in ("st", "d", "g", "c"):
                if s_ in fl.sems:
                    fl.wait(e, s_)
    emit_p1(nc, fl, mk(set()))
    fence()
    emit_p2(nc, fl, mk(set()))
    fence()
    emit_p3(nc, fl, mk(set()))
    fence()
    emit_p4(nc, fl, mk(set()))
    fence()
    emit_p5(nc, fl, mk(set()))
    fence()
    emit_p6(nc, fl, mk(set()))
    fence()
    emit_p7(nc, fl, mk(set()))
    return nc


P2_NAMES = ["qwT", "kwT", "vwT", "owT", "a_gb", "beta_param", "a_vb_rep"]
P3_NAMES = [f"f{l}_{n}" for l in range(2) for n in
            ["wqkv", "w1", "b1", "w2", "b2", "wo", "qs", "ks", "lg", "lb"]]
P3_CO = ["rot_caq", "rot_cbq", "rot_cak", "rot_cbk", "lmask2", "hm4", "hm4T", "w0tab"]
P5_NAMES = ["WqT", "WkT", "WvT", "WoT", "l1_bq", "l1_bk", "l1_bv_rep", "l1_bo",
            "embKT", "embV"]
P5_CO = ["iden128_p5", "dlo64", "d65", "sel65", "bmask_first", "bmask_last"]
P6_NAMES = ["w_router", "w_gate", "w_up", "w_down"]
P6_CO = ["iota32r", "tri_strict128", "iden128"]


KERNEL_FUSED = True


def kernel(**inputs):
    if KERNEL_FUSED:
        return kernel_fused(**inputs)
    return kernel_multi(**inputs)


def kernel_fused(**inputs):
    h = host_prep(inputs)
    sh = h["shared"]
    co = h["const"]
    R = list(range(NCORES))
    nc = build_full()
    base = {**{n: sh[n] for n in P2_NAMES}, **{n: sh[n] for n in P3_NAMES},
            **{n: co[n] for n in P3_CO}, **{n: sh[n] for n in P5_NAMES},
            **{n: co[n] for n in P5_CO}, **{n: sh[n] for n in P6_NAMES},
            **{n: co[n] for n in P6_CO}}
    maps = [{**base, "x0T": h["percore"][c]["x0T"]} for c in R]
    rr = run_bass_kernel_spmd(nc, maps, core_ids=R).results
    out = np.empty((16, T, D), np.float32)
    for c in R:
        out[2 * c:2 * c + 2] = np.asarray(rr[c]["outT"], np.float32).T.reshape(BLOC, T, D)
    return out


def kernel_multi(**inputs):
    h = host_prep(inputs)
    sh = h["shared"]
    co = h["const"]
    R = list(range(NCORES))

    def run(nc, maps, keys):
        rr = run_bass_kernel_spmd(nc, maps, core_ids=R).results
        return {k: [rr[c][k] for c in R] for k in keys}

    r1 = run(build_p1(), [{"x0T": h["percore"][c]["x0T"]} for c in R],
             ["xn0T", "t1Tb"])
    r2 = run(build_p2(),
             [{**{n: sh[n] for n in P2_NAMES}, "t1Tb": r1["t1Tb"][c]} for c in R],
             ["attn0T"])
    m3 = {**{n: sh[n] for n in P3_NAMES}, **{n: co[n] for n in P3_CO}}
    r3 = run(build_p3(), [{**m3, "xn0T": r1["xn0T"][c]} for c in R], ["u_finT"])
    r4 = run(build_p4(),
             [{"x0T": h["percore"][c]["x0T"], "xn0T": r1["xn0T"][c],
               "attn0T": r2["attn0T"][c], "u_finT": r3["u_finT"][c]} for c in R],
             ["x1T", "xn1T", "xn1Tb"])
    m5 = {**{n: sh[n] for n in P5_NAMES}, **{n: co[n] for n in P5_CO}}
    r5 = run(build_p5(), [{**m5, "xn1Tb": r4["xn1Tb"][c]} for c in R], ["attn1T"])
    m6 = {**{n: sh[n] for n in P6_NAMES}, **{n: co[n] for n in P6_CO}}
    r6 = run(build_p6(),
             [{**m6, "xn1T": r4["xn1T"][c], "xn1Tb": r4["xn1Tb"][c]} for c in R],
             ["moeT"])
    r7 = run(build_p7(),
             [{"x1T": r4["x1T"][c], "attn1T": r5["attn1T"][c],
               "xn1T": r4["xn1T"][c], "moeT": r6["moeT"][c]} for c in R],
             ["outT"])
    out = np.empty((16, T, D), np.float32)
    for c in R:
        out[2 * c:2 * c + 2] = np.asarray(r7["outT"][c], np.float32).T.reshape(BLOC, T, D)
    return out


# version 07 (final)
# speedup vs baseline: 1.0734x; 1.0734x over previous
import numpy as np
from contextlib import contextmanager
import concourse.bass as bass
import concourse.mybir as mybir
from concourse.ap import AP
from concourse.bass_core_rust import IndirectOffsetOnAxis
from concourse.bass_utils import run_bass_kernel_spmd

F32 = mybir.dt.float32
BF16 = mybir.dt.bfloat16
I32 = mybir.dt.int32
AF = mybir.ActivationFunctionType
ALU = mybir.AluOpType
AX = mybir.AxisListType.X

NCORES = 8
BLOC = 2
T = 2048
NT = BLOC * T
D = 1024
DC = D // 128
TCH = 512
NTCH = NT // TCH
EPS = 1e-5


class Flow:

    def __init__(self, nc):
        self.nc = nc
        self.sems = {}
        self.stack = []

    def _s(self, name):
        if name not in self.sems:
            self.sems[name] = [self.nc.alloc_semaphore(name), 0]
        return self.sems[name]

    def inc(self, inst, name, n=1):
        s = self._s(name)
        inst.then_inc(s[0], n)
        s[1] += n
        return s[1]

    def dma(self, inst, name="d"):
        return self.inc(inst, name, 16)

    def val(self, name):
        return self._s(name)[1]

    def _scratch(self, eng, k):
        key = (eng.engine, k)
        if key not in getattr(self, "scratch", {}):
            if not hasattr(self, "scratch"):
                self.scratch = {}
            self.scratch[key] = self.nc.alloc_register(
                eng.engine, name=f"flow_scr_{eng.engine}_{k}")
        return self.scratch[key]

    def wait(self, eng, name, lag=0):
        s = self._s(name)
        terms = [(i, iters[name]) for (i, iters, base) in self.stack
                 if name in iters and iters[name]]
        if not terms:
            eng.wait_ge(s[0], max(s[1] - lag, 0))
            return
        r = self._scratch(eng, 0)
        (i0, per0) = terms[0]
        eng.reg_alu(r, i0, per0, mybir.AluOpType.mult)
        for (ii, pp) in terms[1:]:
            r2 = self._scratch(eng, 1)
            eng.reg_alu(r2, ii, pp, mybir.AluOpType.mult)
            eng.reg_alu(r, r, r2, mybir.AluOpType.add)
        if s[1] - lag:
            eng.reg_alu(r, r, s[1] - lag, mybir.AluOpType.add)
        eng.wait_ge(s[0], r)

    def _free_loop_regs(self, n0):
        import re as _re
        from concourse.bass_primitives_rust import RegisterHandle as _RH
        emap = {"SP": mybir.EngineType.SP, "Pool": mybir.EngineType.Pool,
                "Activation": mybir.EngineType.Activation,
                "PE": mybir.EngineType.PE, "DVE": mybir.EngineType.DVE}
        if not hasattr(self, "_freed"):
            self._freed = set()
        names = self.nc.inst_names()[n0:]
        regs = set()
        for nm in names:
            try:
                s = str(self.nc.get_inst_by_name(nm))
            except Exception:
                continue
            for rn in _re.findall(r"R\[([A-Za-z0-9_.]+)\]", s):
                regs.add(rn)
        for rn in regs:
            if rn in self._freed:
                continue
            if _re.search(r"_(tmp|snap)_\d+$", rn) or "_fori_" in rn:
                pre = rn.split("_")[0]
                if pre in emap and "flow_scr" not in rn:
                    try:
                        self.nc.free_register(_RH(rn, emap[pre]))
                        self._freed.add(rn)
                    except Exception:
                        pass

    @contextmanager
    def loop(self, start, end, iters, barrier=None):
        assert start == 0, "loops assumed to start at 0 for wait arithmetic"
        self._loop_n0 = self.nc.n_instructions()
        with self.nc.Fori(start, end) as i:
            base = {k: self._s(k)[1] for k in set(self.sems) | set(iters)}
            self.stack.append((i, iters, base))
            for eng in (barrier or []):
                for k in iters:
                    if iters[k]:
                        self.wait(eng, k)
            yield i
            self.stack.pop()
            for k, per in iters.items():
                got = self._s(k)[1] - base[k]
                assert got == per, f"loop sem {k}: declared {per}/iter, emitted {got}"
            for k in self.sems:
                if k not in iters:
                    got = self._s(k)[1] - base.get(k, 0)
                    assert got == 0, f"loop sem {k}: incremented {got} but undeclared"
        n = end - start
        for k, per in iters.items():
            self._s(k)[1] = base[k] + n * per
        self._free_loop_regs(self._loop_n0)


DEBUG_INT_OUT = False
P3_DEBUG_W0 = False

def dram(nc, name, shape, dtype, kind):
    if kind == "int" and DEBUG_INT_OUT:
        kind = "out"
    cache = getattr(nc, "_dram_cache", None)
    if cache is None:
        cache = {}
        nc._dram_cache = cache
    if name in cache:
        return cache[name]
    k = {"in": "ExternalInput", "out": "ExternalOutput", "int": "Internal"}[kind]
    ap = nc.dram_tensor(name, list(shape), dtype, kind=k).ap()
    cache[name] = ap
    return ap


def register_consts(nc, values):
    for val in values:
        t = nc.alloc_sbuf_tensor(f"cst_{val}", [128, 1], F32)
        nc.vector.memset(t[:, :], float(val))
        nc.const_aps.register(F32, float(val), t[:, :])




class Alloc:

    def __init__(self, nc):
        from contextlib import ExitStack
        self.nc = nc
        self.st = ExitStack()

    def sb(self, name, shape, dtype):
        return self.st.enter_context(self.nc.sbuf_tensor(name, shape, dtype))

    def ps(self, name, shape, dtype=F32):
        return self.st.enter_context(self.nc.psum_tensor(name, shape, dtype))

    def close(self):
        self.st.close()



def emit_p1(nc, fl, io):
    x0T = dram(nc, "x0T", (D, NT), F32, io("x0T"))
    xn0T = dram(nc, "xn0T", (D, NT), F32, io("xn0T"))
    xn0Tb = dram(nc, "xn0Tb", (D, NT), BF16, io("xn0Tb"))
    t1Tb = dram(nc, "t1Tb", (D, NT), BF16, io("t1Tb"))

    with (
        nc.sbuf_tensor("p1_x", [128, DC * TCH], F32) as xs,
        nc.sbuf_tensor("p1_xb", [128, DC * TCH], BF16) as xb,
        nc.sbuf_tensor("p1_x2", [128, DC * TCH], BF16) as x2b,
        nc.sbuf_tensor("p1_y", [128, DC * TCH], F32) as ys,
        nc.sbuf_tensor("p1_yb", [128, DC * TCH], BF16) as yb,
        nc.sbuf_tensor("p1_ones", [128, 1], BF16) as ones_col,
        nc.sbuf_tensor("p1_row", [1, 6 * TCH], F32) as rows,
        nc.sbuf_tensor("p1_onesr", [1, 128], F32) as ones_row_f,
        nc.psum_tensor("p1_ps_s", [1, TCH], F32) as ps_s,
        nc.psum_tensor("p1_ps_s2", [1, TCH], F32) as ps_s2,
        nc.psum_tensor("p1_ps_mu", [128, TCH], F32) as ps_mu,
        nc.psum_tensor("p1_ps_rs", [128, TCH], F32) as ps_rs,
    ):
        MU, RSTD, SS, S2 = 0, TCH, 2 * TCH, 3 * TCH
        v = nc.vector
        s = nc.scalar
        t = nc.tensor
        sy = nc.sync
        fl.inc(v.memset(ones_col[:, :], 1.0), "c")
        fl.inc(v.memset(ones_row_f[:, :], 1.0), "c")

        def one_pass(src_dram, wr_f32, wr_bf16, pass_id):
            n_st = 16 * ((wr_f32 is not None) + (wr_bf16 is not None))
            src3 = src_dram.rearrange("(c p) n -> p c n", p=128)
            xs3 = xs[:, :].rearrange("p (c n) -> p c n", c=DC)
            ys3 = ys[:, :].rearrange("p (c n) -> p c n", c=DC)
            yb3 = yb[:, :].rearrange("p (c n) -> p c n", c=DC)
            w1_3 = wr_f32.rearrange("(c p) n -> p c n", p=128) if wr_f32 is not None else None
            w2_3 = wr_bf16.rearrange("(c p) n -> p c n", p=128) if wr_bf16 is not None else None
            with fl.loop(0, NTCH, {"d": 16, "c": 9, "st": n_st},
                         barrier=[sy, v, s, t]) as tc:
                off = tc * TCH
                fl.dma(sy.dma_start(out=xs3, in_=src3[:, :, bass.ds(off, TCH)]))
                fl.wait(v, "d")
                fl.inc(v.tensor_copy(xb[:, :], xs[:, :]), "c")
                fl.wait(s, "c")
                fl.inc(s.activation(x2b[:, :], xs[:, :], AF.Square), "c")
                fl.wait(t, "c")
                for c in range(DC):
                    t.matmul(ps_s[:, :], ones_col[:, :], xb[:, c * TCH:(c + 1) * TCH],
                             start=(c == 0), stop=(c == DC - 1))
                last = None
                for c in range(DC):
                    last = t.matmul(ps_s2[:, :], ones_col[:, :], x2b[:, c * TCH:(c + 1) * TCH],
                                    start=(c == 0), stop=(c == DC - 1))
                fl.inc(last, "c")
                fl.wait(v, "c")
                v.tensor_scalar_mul(rows[:, MU:MU + TCH], ps_s[:, :], 1.0 / D)
                v.tensor_scalar_mul(rows[:, S2:S2 + TCH], ps_s2[:, :], 1.0 / D)
                v.tensor_mul(rows[:, SS:SS + TCH], rows[:, MU:MU + TCH], rows[:, MU:MU + TCH])
                fl.inc(v.tensor_sub(rows[:, S2:S2 + TCH], rows[:, S2:S2 + TCH],
                                    rows[:, SS:SS + TCH]), "c")
                fl.wait(s, "c")
                fl.inc(s.activation(rows[:, SS:SS + TCH], rows[:, S2:S2 + TCH],
                                    AF.Sqrt, bias=EPS), "c")
                fl.wait(v, "c")
                fl.inc(v.reciprocal(rows[:, RSTD:RSTD + TCH], rows[:, SS:SS + TCH]), "c")
                fl.wait(t, "c")
                t.matmul(ps_mu[:, :], ones_row_f[:, :], rows[0:1, MU:MU + TCH],
                         start=True, stop=True)
                fl.inc(t.matmul(ps_rs[:, :], ones_row_f[:, :], rows[0:1, RSTD:RSTD + TCH],
                                start=True, stop=True), "c")
                fl.wait(v, "c")
                lastv = None
                for c in range(DC):
                    sl = slice(c * TCH, (c + 1) * TCH)
                    v.tensor_sub(ys[:, sl], xs[:, sl], ps_mu[:, :])
                    lastv = v.tensor_mul(ys[:, sl], ys[:, sl], ps_rs[:, :])
                fl.inc(lastv, "c")
                fl.wait(s, "c")
                fl.inc(s.copy(yb[:, :], ys[:, :]), "c")
                fl.wait(sy, "c")
                if wr_f32 is not None:
                    fl.dma(sy.dma_start(out=w1_3[:, :, bass.ds(off, TCH)], in_=ys3), "st")
                if wr_bf16 is not None:
                    fl.dma(sy.dma_start(out=w2_3[:, :, bass.ds(off, TCH)], in_=yb3), "st")
                fl.wait(sy, "st")

        one_pass(x0T, xn0T, xn0Tb, "a")
        one_pass(xn0T, None, t1Tb, "b")
        fl.wait(sy, "st")
    return {}



def emit_p2(nc, fl, io):
    t1Tb = dram(nc, "t1Tb", (D, NT), BF16, io("t1Tb"))
    attn0T = dram(nc, "attn0T", (D, NT), F32, io("attn0T"))
    qwT = dram(nc, "qwT", (D, D), F32, "in")
    kwT = dram(nc, "kwT", (D, D), F32, "in")
    vwT = dram(nc, "vwT", (D, D), F32, "in")
    owT = dram(nc, "owT", (D, D), F32, "in")
    gb = dram(nc, "a_gb", (10, 128, 8), F32, "in")
    beta_in = dram(nc, "beta_param", (1, 16), F32, "in")
    vb_rep = dram(nc, "a_vb_rep", (128, D), F32, "in")
    qhTd = dram(nc, "p2_qhT", (D, NT), BF16, "int")
    khTd = dram(nc, "p2_khT", (D, NT), BF16, "int")
    Vd = dram(nc, "p2_V", (BLOC, 16, 128, 16 * 65), BF16, "int")
    aTd = dram(nc, "p2_aT", (D, NT), BF16, "int")
    dbg_nrm = dram(nc, "p2_dbg_nrm", (3 * D, NT), BF16, "int")
    dbg_qt = dram(nc, "p2_dbg_qt", (D, NT), BF16, "int")

    H2 = 16 * 65
    v = nc.vector
    s = nc.scalar
    t = nc.tensor
    sy = nc.sync
    g = nc.gpsimd

    al = Alloc(nc)
    gbs = al.sb("p2_gb", [128, 10 * 8], F32)
    beta_r = al.sb("p2_beta", [1, 16], F32)
    beta8 = al.sb("p2_beta8", [128, 16], F32)
    ones1 = al.sb("p2_ones1", [1, 128], F32)
    ones_cb = al.sb("p2_onescb", [128, 1], BF16)
    rowbuf = al.sb("p2_row", [1, 4 * TCH], F32)
    rb = al.sb("p2_rb", [64, TCH], F32)
    ao = al.sb("p2_ao", [128, DC * TCH], BF16)
    oo = al.sb("p2_oo", [128, DC * TCH], F32)
    psA = al.ps("p2_psA", [128, 1024], F32)
    psB = al.ps("p2_psB", [128, 1024], F32)
    psv0 = al.ps("p2_psv0", [128, TCH], F32)
    psv1 = al.ps("p2_psv1", [128, TCH], F32)
    psbc = al.ps("p2_psbc", [128, TCH], F32)
    psr = al.ps("p2_psr", [1, TCH], F32)
    alp = Alloc(nc)
    wq = alp.sb("p2_wq", [128, DC * D], BF16)
    wk = alp.sb("p2_wk", [128, DC * D], BF16)
    wv = alp.sb("p2_wv", [128, DC * D], BF16)
    vbr = alp.sb("p2_vbr", [128, D], F32)
    t1s = alp.sb("p2_t1", [128, DC * TCH], BF16)
    nrm = alp.sb("p2_nrm", [128, 3 * DC * TCH], BF16)
    qt = alp.sb("p2_qt", [128, DC * TCH], BF16)
    kt_s = alp.sb("p2_kt", [128, DC * TCH], BF16)
    q2 = alp.sb("p2_q2", [128, DC * TCH], BF16)
    nq = alp.sb("p2_nq", [128, DC * TCH], BF16)
    vsb = alp.sb("p2_vsb", [128, 4 * H2], BF16)
    if True:
        psv = (psv0, psv1)
        fl.dma(g.dma_start(out=wq[:, :].rearrange("p (c n) -> p c n", c=DC),
                           in_=qwT.rearrange("(c p) n -> p c n", p=128)), "g")
        fl.dma(g.dma_start(out=wk[:, :].rearrange("p (c n) -> p c n", c=DC),
                           in_=kwT.rearrange("(c p) n -> p c n", p=128)), "g")
        fl.dma(g.dma_start(out=wv[:, :].rearrange("p (c n) -> p c n", c=DC),
                           in_=vwT.rearrange("(c p) n -> p c n", p=128)), "g")
        fl.dma(sy.dma_start(out=gbs[:, :].rearrange("p (a n) -> p a n", a=10),
                            in_=gb.rearrange("a p n -> p a n")))
        fl.dma(sy.dma_start(out=vbr[:, :], in_=vb_rep))
        fl.dma(sy.dma_start(out=beta_r[:, :], in_=beta_in))
        fl.wait(v, "d")
        fl.wait(v, "g")
        fl.wait(t, "g")
        fl.inc(v.memset(ones1[:, :], 1.0), "c")
        fl.inc(v.memset(ones_cb[:, :], 1.0), "c")
        fl.wait(s, "d")
        fl.wait(s, "c")
        fl.inc(s.activation(beta_r[:, :], beta_r[:, :], AF.Exp), "c")
        fl.wait(s, "c")
        fl.inc(s.activation(beta_r[:, :], beta_r[:, :], AF.Ln, bias=1.0), "c")
        fl.wait(v, "c")
        fl.inc(v.tensor_scalar(beta_r[:, :], beta_r[:, :], 0.1, 10.0, ALU.max, ALU.min), "c")
        fl.wait(v, "c")
        fl.inc(v.tensor_scalar_mul(beta_r[:, :], beta_r[:, :], 1.0 / 8.0), "c")
        fl.wait(t, "c")
        fl.inc(t.matmul(psbc[:, 0:16], ones1[:, :], beta_r[:, :], start=True, stop=True), "c")
        fl.wait(v, "c")
        fl.inc(v.tensor_copy(beta8[:, :], psbc[:, 0:16]), "c")

        def col(idx, c):
            return gbs[:, idx * 8 + c:idx * 8 + c + 1]

        t1_3 = t1Tb.rearrange("(c p) n -> p c n", p=128)
        t1s3 = t1s[:, :].rearrange("p (c n) -> p c n", c=DC)
        qh_d3 = qhTd.rearrange("(c p) n -> p c n", p=128)
        kh_d3 = khTd.rearrange("(c p) n -> p c n", p=128)

        with fl.loop(0, NTCH, {"d": 16, "c": 64, "st": 80},
                     barrier=[sy, v, s, t]) as tcv:
            off = tcv * TCH
            fl.dma(sy.dma_start(out=t1s3, in_=t1_3[:, :, bass.ds(off, TCH)]))
            fl.wait(v, "d")
            lastv = None
            for i3, (gi, bi) in enumerate(((0, 1), (2, 3), (4, 5))):
                for c in range(DC):
                    dst = nrm[:, (i3 * DC + c) * TCH:(i3 * DC + c + 1) * TCH]
                    lastv = v.tensor_scalar(dst, t1s[:, c * TCH:(c + 1) * TCH],
                                            col(gi, c), col(bi, c), ALU.mult, ALU.add)
            fl.inc(lastv, "c")
            fl.wait(t, "c")
            for (w_sb, src_off, dst, bcol) in ((wq, 0, qt, 6), (wk, DC, kt_s, 7)):
                for m in range(DC):
                    pdst = psA[:, 0:TCH] if m % 2 == 0 else psA[:, TCH:1024]
                    fl.wait(t, "c")
                    lastm = None
                    for c in range(DC):
                        lastm = t.matmul(pdst,
                                         w_sb[:, c * D + m * 128:c * D + (m + 1) * 128],
                                         nrm[:, (src_off + c) * TCH:(src_off + c + 1) * TCH],
                                         start=(c == 0), stop=(c == DC - 1))
                    fl.inc(lastm, "c")
                    fl.wait(v, "c")
                    fl.inc(v.tensor_scalar(dst[:, m * TCH:(m + 1) * TCH], pdst,
                                           col(bcol, m), None, ALU.add), "c")
            fl.wait(sy, "c")
            fl.dma(sy.dma_start(out=dbg_nrm.rearrange("(c p) n -> p c n", p=128)[:, :, bass.ds(off, TCH)],
                                in_=nrm[:, :].rearrange("p (c n) -> p c n", c=3 * DC)), "st")
            fl.dma(sy.dma_start(out=dbg_qt.rearrange("(c p) n -> p c n", p=128)[:, :, bass.ds(off, TCH)],
                                in_=qt[:, :].rearrange("p (c n) -> p c n", c=DC)), "st")
            for (src, dstd) in ((qt, qh_d3), (kt_s, kh_d3)):
                fl.wait(s, "c")
                fl.inc(s.activation(q2[:, :], src[:, :], AF.Square), "c")
                fl.wait(t, "c")
                lastm = None
                for c in range(DC):
                    lastm = t.matmul(psr[:, :], ones_cb[:, :],
                                     q2[:, c * TCH:(c + 1) * TCH],
                                     start=(c == 0), stop=(c == DC - 1))
                fl.inc(lastm, "c")
                fl.wait(s, "c")
                fl.inc(s.activation(rowbuf[:, 0:TCH], psr[:, :], AF.Sqrt), "c")
                fl.wait(v, "c")
                fl.inc(v.tensor_scalar_max(rowbuf[:, 0:TCH], rowbuf[:, 0:TCH], 1e-6), "c")
                fl.wait(v, "c")
                fl.inc(v.reciprocal(rowbuf[:, TCH:2 * TCH], rowbuf[:, 0:TCH]), "c")
                fl.wait(t, "c")
                fl.inc(t.matmul(psbc[:, :], ones1[:, :], rowbuf[0:1, TCH:2 * TCH],
                                start=True, stop=True), "c")
                fl.wait(v, "c")
                fl.wait(v, "st")
                lastv = None
                for c in range(DC):
                    lastv = v.tensor_mul(nq[:, c * TCH:(c + 1) * TCH],
                                         src[:, c * TCH:(c + 1) * TCH], psbc[:, :])
                fl.inc(lastv, "c")
                fl.wait(sy, "c")
                fl.dma(sy.dma_start(out=dstd[:, :, bass.ds(off, TCH)],
                                    in_=nq[:, :].rearrange("p (c n) -> p c n", c=DC)), "st")
            fl.wait(v, "st")
            fl.wait(v, "c")
            fl.inc(v.memset(vsb[:, :], 1.0), "c")
            for tt in range(4):
                for half in range(2):
                    fl.wait(t, "c")
                    lastm = None
                    for c in range(DC):
                        lastm = t.matmul(psB[:, 0:TCH],
                                         nrm[:, (2 * DC + c) * TCH + tt * 128:
                                             (2 * DC + c) * TCH + (tt + 1) * 128],
                                         wv[:, c * D + half * TCH:c * D + (half + 1) * TCH],
                                         start=(c == 0), stop=(c == DC - 1))
                    fl.inc(lastm, "c")
                    fl.wait(v, "c")
                    dst = vsb[:, tt * H2:(tt + 1) * H2].rearrange(
                        "p (h x) -> p h x", x=65)[:, half * 8:(half + 1) * 8, 0:64]
                    src_ps = psB[:, 0:TCH].rearrange("p (h x) -> p h x", x=64)
                    vb_sl = vbr[:, half * TCH:(half + 1) * TCH].rearrange(
                        "p (h x) -> p h x", x=64)
                    fl.inc(v.tensor_add(dst, src_ps, vb_sl), "c")
            fl.wait(sy, "c")
            fl.dma(sy.dma_start(
                out=Vd.rearrange("b k p n -> p (b k) n")[:, bass.ds(tcv * 4, 4), :],
                in_=vsb[:, :].rearrange("p (k n) -> p k n", k=4)), "st")
            fl.wait(sy, "st")

        alp.close()
        ala = Alloc(nc)
        vball = ala.sb("p2_vball", [128, 2 * 16 * H2], BF16)
        qh = ala.sb("p2_qh", [128, NT], BF16)
        kh = ala.sb("p2_kh", [128, NT], BF16)
        pex = ala.sb("p2_pex", [128, 2 * 1024], BF16)
        vh = ala.sb("p2_vh", [128, 16 * 65], BF16)
        fl.dma(sy.dma_start(out=vball[:, :].rearrange("p (k n) -> p k n", k=32),
                            in_=Vd.rearrange("b k p n -> p (b k) n")))
        qh5 = qhTd.rearrange("(c p) n -> p c n", p=128)
        kh5 = khTd.rearrange("(c p) n -> p c n", p=128)
        vb5 = vball[:, :].rearrange("p (b k h x) -> p b k h x", b=BLOC, k=16, h=16)
        aT3 = aTd.rearrange("(c p) n -> p c n", p=128)
        with fl.loop(0, 8, {"d": 32, "c": 452, "st": 64},
                     barrier=[sy, v, s, t]) as hp:
            fl.dma(sy.dma_start(out=qh[:, :].rearrange("p (o n) -> p o n", o=1),
                                in_=qh5[:, bass.ds(hp, 1), :]))
            fl.dma(sy.dma_start(out=kh[:, :].rearrange("p (o n) -> p o n", o=1),
                                in_=kh5[:, bass.ds(hp, 1), :]))
            h0 = hp * 2
            fl.wait(t, "d")
            for b in range(BLOC):
                for j in range(2):
                    hsl = bass.ds(h0 + j, 1)
                    rsl = slice(64 * j, 64 * j + 64)
                    fl.wait(v, "c")
                    fl.inc(v.tensor_copy(vh[:, :].rearrange("p (k x) -> p k x", k=16),
                                         vb5[:, b, :, :, :][:, :, hsl, :].opt()
                                         if False else
                                         vb5[:, b:b + 1, :, hsl, :].opt()), "c")
                    fl.wait(t, "c")
                    for qc2 in range(2):
                        for kt in range(16):
                            par = kt % 2
                            pa = psA if par == 0 else psB
                            lastm = None
                            for qq in range(2):
                                qc = qc2 * 2 + qq
                                lastm = t.matmul(
                                    pa[:, qq * TCH:(qq + 1) * TCH],
                                    kh[rsl, b * T + kt * 128:b * T + (kt + 1) * 128],
                                    qh[rsl, b * T + qc * TCH:b * T + (qc + 1) * TCH],
                                    start=True, stop=True)
                            fl.inc(lastm, "c")
                            fl.wait(s, "c")
                            fl.inc(s.activation(pex[:, par * 1024:(par + 1) * 1024],
                                                pa[:, :], AF.Exp,
                                                scale=beta8[:, hsl]), "c")
                            fl.wait(t, "c")
                            lastm = None
                            for qq in range(2):
                                lastm = t.matmul(
                                    psv[qq][0:65, :], vh[:, kt * 65:(kt + 1) * 65],
                                    pex[:, par * 1024 + qq * TCH:
                                        par * 1024 + (qq + 1) * TCH],
                                    start=(kt == 0), stop=(kt == 15))
                            fl.inc(lastm, "c")
                        for qq in range(2):
                            qc = qc2 * 2 + qq
                            fl.wait(v, "c")
                            fl.inc(v.reciprocal(rowbuf[0:1, 2 * TCH:3 * TCH],
                                                psv[qq][64:65, :]), "c")
                            fl.wait(t, "c")
                            fl.inc(t.matmul(psbc[0:64, :], ones1[:, 0:64],
                                            rowbuf[0:1, 2 * TCH:3 * TCH],
                                            start=True, stop=True), "c")
                            fl.wait(v, "c")
                            fl.inc(v.tensor_copy(rb[:, :], psbc[0:64, :]), "c")
                            fl.wait(v, "c")
                            fl.inc(v.tensor_mul(ao[rsl, qc * TCH:(qc + 1) * TCH],
                                                psv[qq][0:64, :], rb[:, :]), "c")
                    fl.wait(sy, "c")
                    fl.dma(sy.dma_start(
                        out=aT3[rsl, bass.ds(hp, 1), b * T:(b + 1) * T],
                        in_=ao[rsl, 0:T].rearrange("p (o n) -> p o n", o=1)), "st")
                    fl.wait(sy, "st")

        ala.close()
        alo = Alloc(nc)
        wo = alo.sb("p2_wo", [128, DC * D], BF16)
        fl.wait(g, "c")
        fl.wait(g, "st")
        fl.wait(g, "g")
        fl.dma(g.dma_start(out=wo[:, :].rearrange("p (c n) -> p c n", c=DC),
                           in_=owT.rearrange("(c p) n -> p c n", p=128)), "g")
        fl.wait(t, "g")
        aT3b = aTd.rearrange("(c p) n -> p c n", p=128)
        at0 = attn0T.rearrange("(c p) n -> p c n", p=128)
        with fl.loop(0, NTCH, {"d": 16, "c": 16, "st": 16},
                     barrier=[sy, v, s, t]) as tcv:
            off2 = tcv * TCH
            fl.dma(sy.dma_start(out=ao[:, :].rearrange("p (c n) -> p c n", c=DC),
                                in_=aT3b[:, :, bass.ds(off2, TCH)]))
            fl.wait(t, "d")
            for m in range(DC):
                pdst = psA[:, 0:TCH] if m % 2 == 0 else psA[:, TCH:1024]
                fl.wait(t, "c")
                lastm = None
                for c in range(DC):
                    lastm = t.matmul(pdst,
                                     wo[:, c * D + m * 128:c * D + (m + 1) * 128],
                                     ao[:, c * TCH:(c + 1) * TCH],
                                     start=(c == 0), stop=(c == DC - 1))
                fl.inc(lastm, "c")
                fl.wait(v, "c")
                fl.inc(v.tensor_scalar(oo[:, m * TCH:(m + 1) * TCH], pdst,
                                       col(9, m), None, ALU.add), "c")
            fl.wait(sy, "c")
            fl.dma(sy.dma_start(out=at0[:, :, bass.ds(off2, TCH)],
                                in_=oo[:, :].rearrange("p (c n) -> p c n", c=DC)), "st")
            fl.wait(sy, "st")
    alo.close()
    al.close()
    return {}



def emit_p3(nc, fl, io):
    WIN = 512
    HIDP = 2816
    NDD = 2 * HIDP // 128
    H2L = 8 * 65
    xn0T = dram(nc, "xn0T", (D, NT), F32, io("xn0T"))
    ufT = dram(nc, "u_finT", (D, NT), F32, io("u_finT"))
    u_f32 = dram(nc, "p3_u32", (D, NT), F32, "int")
    dbgu = [dram(nc, f"p3_dbgu{k}", (D, NT), F32, "int") for k in range(4)]
    u_bfp = dram(nc, "p3_ubfp", (D, NT + 2), BF16, "int")
    u_bfp2 = dram(nc, "p3_ubfp2", (D, NT + 2), BF16, "int")
    wqkv_d = [dram(nc, f"f{l}_wqkv", (D, 1536), F32, "in") for l in range(2)]
    w1_d = [dram(nc, f"f{l}_w1", (D, 2 * HIDP), F32, "in") for l in range(2)]
    b1_d = [dram(nc, f"f{l}_b1", (128, NDD), F32, "in") for l in range(2)]
    w2_d = [dram(nc, f"f{l}_w2", (HIDP, D), F32, "in") for l in range(2)]
    b2_d = [dram(nc, f"f{l}_b2", (128, DC), F32, "in") for l in range(2)]
    wo_d = [dram(nc, f"f{l}_wo", (WIN, D), F32, "in") for l in range(2)]
    qs_d = [dram(nc, f"f{l}_qs", (128, 1), F32, "in") for l in range(2)]
    ks_d = [dram(nc, f"f{l}_ks", (128, 1), F32, "in") for l in range(2)]
    lg_d = [dram(nc, f"f{l}_lg", (128, DC), F32, "in") for l in range(2)]
    lb_d = [dram(nc, f"f{l}_lb", (128, DC), F32, "in") for l in range(2)]
    rot_d = {nm: dram(nc, "rot_" + nm, (128, NT), F32, "in")
             for nm in ("caq", "cbq", "cak", "cbk")}
    lmask_d = dram(nc, "lmask2", (2, 1024, WIN), F32, "in")
    hm4_d = dram(nc, "hm4", (4, 128, 8), F32, "in")
    hm4T_d = dram(nc, "hm4T", (4, 8, 128), F32, "in")
    w0tab_d = dram(nc, "w0tab", (1, 8), I32, "in")

    v = nc.vector
    s = nc.scalar
    t = nc.tensor
    sy = nc.sync
    g = nc.gpsimd

    al = Alloc(nc)
    ones1 = al.sb("p3_ones1", [1, 128], F32)
    ones_cb3 = al.sb("p3_onescb", [128, 1], BF16)
    lmask = al.sb("p3_lmask", [128, 2 * 8 * WIN], BF16)
    hm4 = al.sb("p3_hm4", [128, 4 * 8], BF16)
    hm4T = al.sb("p3_hm4T", [8, 4 * 128], F32)
    qs_c = al.sb("p3_qs", [128, 2], F32)
    w0tab = al.sb("p3_w0tab", [1, 8], I32)
    rowb = al.sb("p3_rowb", [8, 3 * WIN], F32)
    rowc = al.sb("p3_rowc", [1, TCH], F32)
    rowd = al.sb("p3_rowd", [1, TCH], F32)
    rowe = al.sb("p3_rowe", [1, TCH], F32)
    rb = al.sb("p3_rb", [64, WIN], F32)
    psb = al.ps("p3_psb", [128, 1024], F32)
    psc = al.ps("p3_psc", [128, 1024], F32)
    psv_ = al.ps("p3_psv", [128, WIN], F32)
    psS = al.ps("p3_psS", [8, TCH], F32)
    psr2 = al.ps("p3_psr2", [128, TCH], F32)

    wqkvB = [dram(nc, f"p3_wqkvB{l}", (D, 1536), BF16, "int") for l in range(2)]
    woB = [dram(nc, f"p3_woB{l}", (WIN, D), BF16, "int") for l in range(2)]
    w1B = [dram(nc, f"p3_w1B{l}", (D, 2 * HIDP), BF16, "int") for l in range(2)]
    w2B = [dram(nc, f"p3_w2B{l}", (HIDP, D), BF16, "int") for l in range(2)]
    fl.dma(g.dma_start(out=lmask[:, :].rearrange("p (a k n) -> p a k n", a=2, k=8),
                       in_=lmask_d.rearrange("a (k p) n -> p a k n", p=128)), "g")
    fl.dma(g.dma_start(out=hm4[:, :].rearrange("p (a n) -> p a n", a=4),
                       in_=hm4_d.rearrange("a p n -> p a n")), "g")
    fl.wait(sy, "g")
    alcv = Alloc(nc)
    cv = alcv.sb("p3_cv", [128, 22 * 1024], BF16)
    cvt_jobs = []
    for l in range(2):
        cvt_jobs.append((wqkv_d[l], wqkvB[l], 8, 1536, 0))
        cvt_jobs.append((wo_d[l], woB[l], 4, 1024, 0))
        cvt_jobs.append((w2_d[l], w2B[l], 22, 1024, 0))
        cvt_jobs.append((w1_d[l], w1B[l], 8, 2816, 0))
        cvt_jobs.append((w1_d[l], w1B[l], 8, 2816, 2816))
    for (src_d, dst_d, nch, wdt, coff) in cvt_jobs:
        s3 = src_d.rearrange("(c p) n -> p c n", p=128)
        d3 = dst_d.rearrange("(c p) n -> p c n", p=128)
        fl.dma(g.dma_start(out=cv[:, 0:nch * wdt].rearrange("p (c n) -> p c n", c=nch),
                           in_=s3[:, :, coff:coff + wdt]), "g")
        fl.wait(sy, "g")
        fl.dma(sy.dma_start(out=d3[:, :, coff:coff + wdt],
                            in_=cv[:, 0:nch * wdt].rearrange("p (c n) -> p c n", c=nch)), "st")
        fl.wait(g, "st")
    fl.wait(sy, "st")
    fl.wait(v, "st")
    fl.wait(t, "st")
    fl.wait(s, "st")
    fl.wait(g, "st")
    alcv.close()
    fl.wait(v, "g")
    fl.wait(t, "g")
    fl.inc(v.memset(ones1[:, :], 1.0), "c")
    fl.inc(v.memset(ones_cb3[:, :], 1.0), "c")
    fl.dma(sy.dma_start(out=hm4T[:, :].rearrange("p (a n) -> p a n", a=4),
                        in_=hm4T_d.rearrange("a p n -> p a n")))
    fl.dma(sy.dma_start(out=w0tab[:, :], in_=w0tab_d))
    lm4 = lmask[:, :].rearrange("p (a k n) -> p a k n", a=2, k=8)

    uin3 = xn0T.rearrange("(c p) n -> p c n", p=128)
    u32_3 = u_f32.rearrange("(c p) n -> p c n", p=128)
    ubf_3 = u_bfp.rearrange("(c p) n -> p c n", p=128)
    ubf2_3 = u_bfp2.rearrange("(c p) n -> p c n", p=128)

    al0 = Alloc(nc)
    cpy = al0.sb("p3_cpy", [128, DC * TCH], F32)
    cpyb = al0.sb("p3_cpyb", [128, DC * (TCH + 2)], BF16)
    fl.inc(v.memset(cpyb[:, :], 0.0), "c")
    fl.wait(sy, "c")
    for c in range(DC):
        for jh in range(2):
            fl.dma(sy.dma_start(out=ubf_3[:, c:c + 1, jh * 2049:(jh + 1) * 2049],
                                in_=cpyb[:, 0:2049].rearrange("p (o n) -> p o n", o=1)), "st")
            fl.dma(sy.dma_start(out=ubf2_3[:, c:c + 1, jh * 2049:(jh + 1) * 2049],
                                in_=cpyb[:, 0:2049].rearrange("p (o n) -> p o n", o=1)), "st")
    fl.wait(sy, "st")
    for tc in range(NTCH):
        off = tc * TCH
        offp = tc * TCH + 1 + tc // 4
        fl.wait(sy, "st")
        fl.wait(sy, "c")
        fl.dma(sy.dma_start(out=cpy[:, :].rearrange("p (c n) -> p c n", c=DC),
                            in_=uin3[:, :, off:off + TCH]))
        fl.wait(v, "d")
        fl.wait(v, "st")
        fl.inc(v.tensor_copy(cpyb[:, 0:DC * TCH].rearrange("p (c n) -> p c n", c=DC),
                             cpy[:, :].rearrange("p (c n) -> p c n", c=DC)), "c")
        fl.wait(sy, "c")
        fl.dma(sy.dma_start(out=u32_3[:, :, off:off + TCH],
                            in_=cpy[:, :].rearrange("p (c n) -> p c n", c=DC)), "st")
        fl.dma(sy.dma_start(out=ubf_3[:, :, offp:offp + TCH],
                            in_=cpyb[:, 0:DC * TCH].rearrange("p (c n) -> p c n", c=DC)), "st")
    fl.wait(sy, "st")
    al0.close()


    def snap_u(which):
        alS = Alloc(nc)
        cS = alS.sb(f"p3_cS{which}", [128, DC * TCH], F32)
        d3 = dbgu[which].rearrange("(c p) n -> p c n", p=128)
        for tcq in range(NTCH):
            offq = tcq * TCH
            fl.wait(sy, "st")
            fl.wait(sy, "c")
            fl.dma(sy.dma_start(out=cS[:, :].rearrange("p (c n) -> p c n", c=DC),
                                in_=u32_3[:, :, offq:offq + TCH]))
            fl.wait(sy, "d")
            fl.dma(sy.dma_start(out=d3[:, :, offq:offq + TCH],
                                in_=cS[:, :].rearrange("p (c n) -> p c n", c=DC)), "st")
        fl.wait(sy, "st")
        alS.close()
    for l in range(2):
        ala = Alloc(nc)
        wqkv_sb = ala.sb(f"p3_wqkv{l}", [128, DC * 1536], BF16)
        wo_sb = ala.sb(f"p3_wo{l}", [128, 4 * D], BF16)
        rot = {nm: ala.sb(f"p3_rot{l}_{nm}", [128, NT], F32) for nm in
               ("caq", "cbq", "cak", "cbk")}
        hh = ala.sb(f"p3_hh{l}", [128, DC * WIN], BF16)
        qw = ala.sb(f"p3_qw{l}", [128, 4 * WIN], BF16)
        kcur = ala.sb(f"p3_kc{l}", [128, 4 * WIN], BF16)
        kprev = ala.sb(f"p3_kp{l}", [128, 4 * WIN], BF16)
        vcur = ala.sb(f"p3_vc{l}", [128, 4 * H2L], BF16)
        vprev = ala.sb(f"p3_vp{l}", [128, 4 * H2L], BF16)
        scr = ala.sb(f"p3_scr{l}", [128, 4 * WIN], BF16)
        sw_b = ala.sb(f"p3_sw{l}", [128, 2 * WIN], BF16)
        pex = ala.sb(f"p3_pex{l}", [128, 2 * WIN], BF16)
        aw = ala.sb(f"p3_aw{l}", [128, 4 * WIN], BF16)
        lms = ala.sb(f"p3_lms{l}", [128, WIN], F32)
        if P3_DEBUG_W0:
            dbgraw_q = ala.sb("p3_dbgrq", [128, 4 * WIN], BF16)
            dbgraw_k = ala.sb("p3_dbgrk", [128, 4 * WIN], BF16)
        uc = ala.sb(f"p3_uc{l}", [128, WIN], F32)
        ucb = ala.sb(f"p3_ucb{l}", [128, WIN], BF16)
        fl.dma(sy.dma_start(out=wqkv_sb[:, :].rearrange("p (c n) -> p c n", c=DC),
                            in_=wqkvB[l].rearrange("(c p) n -> p c n", p=128)))
        fl.dma(sy.dma_start(out=wo_sb[:, :].rearrange("p (c n) -> p c n", c=4),
                            in_=woB[l].rearrange("(c p) n -> p c n", p=128)))
        for nm in rot:
            fl.dma(sy.dma_start(out=rot[nm][:, :], in_=rot_d[nm]))
        fl.dma(sy.dma_start(out=qs_c[:, 0:1], in_=qs_d[l]))
        fl.dma(sy.dma_start(out=qs_c[:, 1:2], in_=ks_d[l]))
        fl.wait(v, "d")
        fl.wait(v, "g")
        fl.inc(v.memset(kcur[:, :], 0.0), "c")
        fl.inc(v.memset(vcur[:, :], 0.0), "c")
        fl.wait(t, "g")
        fl.wait(t, "c")
        fl.wait(sy, "g")
        fl.wait(v, "g")
        with fl.loop(0, 1 if P3_DEBUG_W0 else 8,
                     {"d": 160, "c": 453 if P3_DEBUG_W0 else 452, "st": 256},
                     barrier=[sy, v, s, t]) as bw:
            off = bw * WIN
            b_ix = bw // 4
            offp = bw * WIN + 1 + b_ix
            offs = bw * WIN + b_ix
            w0 = (bw % 4 + 3) // 4 * (-1) + 1
            fl.wait(v, "c")
            v.tensor_copy(kprev[:, :], kcur[:, :])
            fl.inc(v.tensor_copy(vprev[:, :], vcur[:, :]), "c")
            fl.dma(sy.dma_start(out=hh[:, 0:4 * WIN].rearrange("p (c n) -> p c n", c=4),
                                in_=ubf_3[:, 0:4, bass.ds(offp, WIN)]))
            fl.dma(sy.dma_start(out=hh[:, 4 * WIN:DC * WIN].rearrange("p (c n) -> p c n", c=4),
                                in_=ubf_3[:, 4:8, bass.ds(offs, WIN)]))
            fl.wait(t, "d")
            for m in range(8):
                pdst = psb[:, 0:WIN] if m % 2 == 0 else psb[:, WIN:1024]
                fl.wait(t, "c")
                lastm = None
                for c in range(DC):
                    lastm = t.matmul(pdst,
                                     wqkv_sb[:, c * 1536 + m * 128:c * 1536 + (m + 1) * 128],
                                     hh[:, c * WIN:(c + 1) * WIN],
                                     start=(c == 0), stop=(c == DC - 1))
                fl.inc(lastm, "c")
                fl.wait(v, "c")
                dsl = (qw[:, m * WIN:(m + 1) * WIN] if m < 4 else
                       kcur[:, (m - 4) * WIN:(m - 3) * WIN])
                fl.inc(v.tensor_copy(dsl, pdst), "c")
            fl.wait(v, "c")
            fl.inc(v.memset(vcur[:, :], 1.0), "c")
            for tt in range(4):
                fl.wait(t, "c")
                lastm = None
                for c in range(DC):
                    lastm = t.matmul(psc[:, 0:WIN],
                                     hh[:, c * WIN + tt * 128:c * WIN + (tt + 1) * 128],
                                     wqkv_sb[:, c * 1536 + 1024:c * 1536 + 1536],
                                     start=(c == 0), stop=(c == DC - 1))
                fl.inc(lastm, "c")
                fl.wait(v, "c")
                dst = vcur[:, tt * H2L:(tt + 1) * H2L]
                fl.inc(v.tensor_copy(
                    dst.rearrange("p (h x) -> p h x", x=65)[:, :, 0:64],
                    psc[:, 0:WIN].rearrange("p (h x) -> p h x", x=64)), "c")
            if P3_DEBUG_W0:
                fl.wait(v, "c")
                v.tensor_copy(dbgraw_q[:, :], qw[:, :])
                fl.inc(v.tensor_copy(dbgraw_k[:, :], kcur[:, :]), "c")
            for (dst, base, scol, ca, cb) in (
                    (qw, 0, 0, "caq", "cbq"),
                    (kcur, 0, 1, "cak", "cbk")):
                fl.wait(s, "c")
                fl.inc(s.activation(scr[:, 0:4 * WIN], dst[:, 0:4 * WIN],
                                    AF.Square), "c")
                fl.wait(t, "c")
                lastm = None
                for c in range(4):
                    lastm = t.matmul(psS[:, 0:WIN], hm4[:, c * 8:(c + 1) * 8],
                                     scr[:, c * WIN:(c + 1) * WIN],
                                     start=(c == 0), stop=(c == 3))
                fl.inc(lastm, "c")
                fl.wait(s, "c")
                fl.inc(s.activation(rowb[:, 0:WIN], psS[:, 0:WIN], AF.Sqrt), "c")
                fl.wait(v, "c")
                fl.inc(v.tensor_scalar_max(rowb[:, WIN:2 * WIN], rowb[:, 0:WIN], 1e-12), "c")
                fl.wait(v, "c")
                fl.inc(v.reciprocal(rowb[:, 2 * WIN:3 * WIN], rowb[:, WIN:2 * WIN]), "c")
                for c in range(4):
                    fl.wait(t, "c")
                    fl.inc(t.matmul(psr2[:, 0:WIN], hm4T[:, c * 128:(c + 1) * 128],
                                    rowb[:, 2 * WIN:3 * WIN], start=True, stop=True), "c")
                    sl = slice(base + c * WIN, base + (c + 1) * WIN)
                    fl.wait(v, "c")
                    fl.inc(v.scalar_tensor_tensor(scr[:, c * WIN:(c + 1) * WIN],
                                                  dst[:, sl], qs_c[:, scol:scol + 1],
                                                  psr2[:, 0:WIN], ALU.mult, ALU.mult), "c")
                for c in range(4):
                    ro = bass.ds(off, WIN)
                    fl.wait(v, "c")
                    v.tensor_copy(sw_b[0:32, 0:WIN], scr[32:64, c * WIN:(c + 1) * WIN])
                    v.tensor_copy(sw_b[32:64, 0:WIN], scr[0:32, c * WIN:(c + 1) * WIN])
                    v.tensor_copy(sw_b[64:96, 0:WIN], scr[96:128, c * WIN:(c + 1) * WIN])
                    fl.inc(v.tensor_copy(sw_b[96:128, 0:WIN],
                                         scr[64:96, c * WIN:(c + 1) * WIN]), "c")
                    fl.wait(v, "c")
                    v.tensor_mul(scr[:, c * WIN:(c + 1) * WIN],
                                 scr[:, c * WIN:(c + 1) * WIN], rot[ca][:, ro])
                    fl.inc(v.tensor_mul(sw_b[:, WIN:2 * WIN], sw_b[:, 0:WIN],
                                        rot[cb][:, ro]), "c")
                    fl.wait(v, "c")
                    sl = slice(base + c * WIN, base + (c + 1) * WIN)
                    fl.inc(v.tensor_add(dst[:, sl], scr[:, c * WIN:(c + 1) * WIN],
                                        sw_b[:, WIN:2 * WIN]), "c")
            for h in range(8):
                ch = h // 2
                rs = slice(64 * (h % 2), 64 * (h % 2) + 64)
                for kt in range(8):
                    ksrc = kcur if kt >= 4 else kprev
                    vsrc = vcur if kt >= 4 else vprev
                    kcol = ch * WIN + (kt % 4) * 128
                    fl.wait(t, "c")
                    fl.inc(t.matmul(psb[:, 0:WIN], ksrc[rs, kcol:kcol + 128],
                                    qw[rs, ch * WIN:(ch + 1) * WIN],
                                    start=True, stop=True), "c")
                    fl.wait(v, "c")
                    fl.inc(v.tensor_tensor(lms[:, :], psb[:, 0:WIN],
                                           lm4[:, bass.ds(w0, 1), kt, :].opt(),
                                           ALU.mult), "c")
                    fl.wait(s, "c")
                    fl.inc(s.activation(pex[:, 0:WIN], lms[:, :], AF.Exp,
                                        scale=8.0), "c")
                    fl.wait(v, "c")
                    fl.inc(v.tensor_mul(pex[:, WIN:2 * WIN], pex[:, 0:WIN],
                                        lm4[:, bass.ds(w0, 1), kt, :].opt()), "c")
                    vcol = (kt % 4) * H2L + h * 65
                    fl.wait(t, "c")
                    fl.inc(t.matmul(psv_[0:65, :], vsrc[:, vcol:vcol + 65],
                                    pex[:, WIN:2 * WIN],
                                    start=(kt == 0), stop=(kt == 7)), "c")
                fl.wait(v, "c")
                fl.inc(v.reciprocal(rowc[0:1, 0:WIN], psv_[64:65, :]), "c")
                fl.wait(t, "c")
                fl.inc(t.matmul(psr2[0:64, 0:WIN], ones1[:, 0:64], rowc[0:1, 0:WIN],
                                start=True, stop=True), "c")
                fl.wait(v, "c")
                fl.inc(v.tensor_copy(rb[:, :], psr2[0:64, 0:WIN]), "c")
                fl.wait(v, "c")
                fl.inc(v.tensor_mul(aw[rs, ch * WIN:(ch + 1) * WIN],
                                    psv_[0:64, :], rb[:, :]), "c")
            for m in range(DC):
                pdst = psc[:, 0:WIN] if m % 2 == 0 else psc[:, WIN:1024]
                fl.wait(t, "c")
                lastm = None
                for c in range(4):
                    lastm = t.matmul(pdst, wo_sb[:, c * D + m * 128:c * D + (m + 1) * 128],
                                     aw[:, c * WIN:(c + 1) * WIN],
                                     start=(c == 0), stop=(c == 3))
                fl.inc(lastm, "c")
                fl.dma(sy.dma_start(out=uc[:, :].rearrange("p n -> p 1 n") if False
                                    else uc[:, :],
                                    in_=u32_3[:, m, bass.ds(off, WIN)].opt() if False
                                    else u32_3[:, m:m + 1, bass.ds(off, WIN)].opt()))
                fl.wait(v, "d")
                fl.wait(v, "c")
                fl.inc(v.tensor_add(uc[:, :], uc[:, :], pdst), "c")
                fl.wait(s, "c")
                fl.inc(s.copy(ucb[:, :], uc[:, :]), "c")
                fl.wait(sy, "c")
                fl.dma(sy.dma_start(out=u32_3[:, m:m + 1, bass.ds(off, WIN)].opt(),
                                    in_=uc[:, :]), "st")
                fl.dma(sy.dma_start(out=ubf2_3[:, m:m + 1, bass.ds(offp, WIN)].opt(),
                                    in_=ucb[:, :]), "st")
                fl.wait(sy, "st")
        if P3_DEBUG_W0:
            for nm, buf, w_ in (("wqkvsb", wqkv_sb, DC * 1536),
                                ("rawq", dbgraw_q, 2048), ("rawk", dbgraw_k, 2048),
                                ("qw", qw, 2048), ("kcur", kcur, 2048),
                                ("vcur", vcur, 4 * H2L), ("aw", aw, 2048),
                                ("hh", hh, DC * WIN), ("pex", pex, 1024)):
                dd_ = dram(nc, "p3dbg_" + nm, (128, w_), BF16, "out")
                fl.wait(sy, "c")
                fl.dma(sy.dma_start(out=dd_, in_=buf[:, :]), "st")
            fl.wait(sy, "st")
            ala.close()
            snap_u(0)
            al.close()
            return {}
        ala.close()
        snap_u(2 * l)

        alm = Alloc(nc)
        w2_sb = alm.sb(f"p3_w2{l}", [128, 22 * D], BF16)
        w1_sb = alm.sb(f"p3_w1b{l}", [128, 2 * DC * 128], BF16)
        b1_sb = alm.sb(f"p3_b1{l}", [128, NDD], F32)
        b2_sb = alm.sb(f"p3_b2{l}", [128, DC], F32)
        lgb = alm.sb(f"p3_lgb{l}", [128, 2 * DC], F32)
        h2b = alm.sb(f"p3_h2b{l}", [128, DC * TCH], BF16)
        sq = alm.sb(f"p3_sq{l}", [128, DC * TCH], BF16)
        t2b = alm.sb(f"p3_t2b{l}", [128, DC * TCH], BF16)
        asb = alm.sb(f"p3_asb{l}", [128, 22 * TCH], BF16)
        h1sb = alm.sb(f"p3_h1{l}", [128, 22 * TCH], BF16)
        gg = alm.sb(f"p3_gg{l}", [128, TCH], BF16)
        uc2 = alm.sb(f"p3_uc2{l}", [128, DC * TCH], F32)
        ucb2 = alm.sb(f"p3_ucb2{l}", [128, DC * TCH], BF16)
        fl.dma(sy.dma_start(out=w2_sb[:, :].rearrange("p (c n) -> p c n", c=22),
                            in_=w2B[l].rearrange("(c p) n -> p c n", p=128)))
        fl.dma(sy.dma_start(out=b1_sb[:, :], in_=b1_d[l]))
        fl.dma(sy.dma_start(out=b2_sb[:, :], in_=b2_d[l]))
        fl.dma(sy.dma_start(out=lgb[:, 0:DC], in_=lg_d[l]))
        fl.dma(sy.dma_start(out=lgb[:, DC:2 * DC], in_=lb_d[l]))
        fl.wait(t, "d")
        fl.wait(v, "d")
        with fl.loop(0, NTCH, {"d": 48 + 16 * NDD, "c": 143, "st": 32},
                     barrier=[sy, v, s, t]) as tc:
            off = tc * TCH
            offp = tc * TCH + 1 + tc // 4
            offs = tc * TCH + tc // 4
            fl.dma(sy.dma_start(out=h2b[:, 0:4 * TCH].rearrange("p (c n) -> p c n", c=4),
                                in_=ubf2_3[:, 0:4, bass.ds(offp, TCH)]))
            fl.dma(sy.dma_start(out=h2b[:, 4 * TCH:DC * TCH].rearrange("p (c n) -> p c n", c=4),
                                in_=ubf2_3[:, 4:8, bass.ds(offs, TCH)]))
            fl.wait(s, "d")
            fl.inc(s.activation(sq[:, :], h2b[:, :], AF.Square), "c")
            fl.wait(t, "c")
            fl.wait(t, "d")
            lastm = None
            for c in range(DC):
                lastm = t.matmul(psS[0:1, :], ones_cb3[:, :], h2b[:, c * TCH:(c + 1) * TCH],
                                 start=(c == 0), stop=(c == DC - 1))
            fl.inc(lastm, "c")
            lastm = None
            for c in range(DC):
                lastm = t.matmul(psv_[0:1, :], ones_cb3[:, :], sq[:, c * TCH:(c + 1) * TCH],
                                 start=(c == 0), stop=(c == DC - 1))
            fl.inc(lastm, "c")
            fl.wait(v, "c")
            v.tensor_scalar_mul(rowc[:, :], psS[0:1, :], 1.0 / D)
            fl.inc(v.tensor_scalar_mul(rowd[:, :], psv_[0:1, 0:TCH], 1.0 / D), "c")
            fl.wait(v, "c")
            v.tensor_mul(rowe[:, :], rowc[:, :], rowc[:, :])
            fl.inc(v.tensor_sub(rowd[:, :], rowd[:, :], rowe[:, :]), "c")
            fl.wait(s, "c")
            fl.inc(s.activation(rowd[:, :], rowd[:, :], AF.Sqrt, bias=EPS), "c")
            fl.wait(v, "c")
            fl.inc(v.reciprocal(rowe[:, :], rowd[:, :]), "c")
            fl.wait(t, "c")
            t.matmul(psr2[:, :], ones1[:, :], rowc[0:1, :], start=True, stop=True)
            fl.inc(t.matmul(psb[:, 0:TCH], ones1[:, :], rowe[0:1, :],
                            start=True, stop=True), "c")
            fl.wait(v, "c")
            lastv = None
            for c in range(DC):
                sl = slice(c * TCH, (c + 1) * TCH)
                v.tensor_sub(uc2[:, sl], h2b[:, sl], psr2[:, :])
                v.tensor_mul(uc2[:, sl], uc2[:, sl], psb[:, 0:TCH])
                lastv = v.tensor_scalar(t2b[:, sl], uc2[:, sl], lgb[:, c:c + 1],
                                        lgb[:, DC + c:DC + c + 1], ALU.mult, ALU.add)
            fl.inc(lastv, "c")
            for dd in range(NDD):
                wpar = dd % 2
                fl.wait(sy, "c", lag=3)
                fl.dma(sy.dma_start(
                    out=w1_sb[:, wpar * DC * 128:(wpar + 1) * DC * 128]
                        .rearrange("p (c n) -> p c n", c=DC),
                    in_=w1B[l].rearrange("(c p) n -> p c n", p=128)[:, :, dd * 128:(dd + 1) * 128]))
                fl.wait(t, "d")
                fl.wait(t, "c")
                pdst = psc[:, 0:TCH] if dd % 2 == 0 else psc[:, TCH:1024]
                lastm = None
                for c in range(DC):
                    lastm = t.matmul(pdst, w1_sb[:, wpar * DC * 128 + c * 128:
                                                 wpar * DC * 128 + (c + 1) * 128],
                                     t2b[:, c * TCH:(c + 1) * TCH],
                                     start=(c == 0), stop=(c == DC - 1))
                fl.inc(lastm, "c")
                if dd < 22:
                    fl.wait(v, "c")
                    fl.inc(v.tensor_scalar(asb[:, dd * TCH:(dd + 1) * TCH], pdst,
                                           b1_sb[:, dd:dd + 1], None, ALU.add), "c")
                else:
                    de = dd - 22
                    fl.wait(s, "c")
                    fl.inc(s.activation(gg[:, :], pdst, AF.Gelu,
                                        bias=b1_sb[:, dd:dd + 1]), "c")
                    fl.wait(v, "c")
                    fl.inc(v.tensor_mul(h1sb[:, de * TCH:(de + 1) * TCH],
                                        asb[:, de * TCH:(de + 1) * TCH], gg[:, :]), "c")
            fl.wait(sy, "c")
            fl.dma(sy.dma_start(out=uc2[:, :].rearrange("p (c n) -> p c n", c=DC),
                                in_=u32_3[:, :, bass.ds(off, TCH)]))
            fl.wait(v, "d")
            for m in range(DC):
                pdst = psb[:, 0:TCH] if m % 2 == 0 else psb[:, TCH:1024]
                fl.wait(t, "c")
                lastm = None
                for c in range(22):
                    lastm = t.matmul(pdst, w2_sb[:, c * D + m * 128:c * D + (m + 1) * 128],
                                     h1sb[:, c * TCH:(c + 1) * TCH],
                                     start=(c == 0), stop=(c == 21))
                fl.inc(lastm, "c")
                sl = slice(m * TCH, (m + 1) * TCH)
                fl.wait(v, "c")
                fl.inc(v.scalar_tensor_tensor(uc2[:, sl], pdst, b2_sb[:, m:m + 1],
                                              uc2[:, sl], ALU.add, ALU.add), "c")
                fl.wait(s, "c")
                fl.inc(s.copy(ucb2[:, sl], uc2[:, sl]), "c")
            fl.wait(sy, "c")
            fl.dma(sy.dma_start(out=u32_3[:, :, bass.ds(off, TCH)],
                                in_=uc2[:, :].rearrange("p (c n) -> p c n", c=DC)), "st")
            fl.dma(sy.dma_start(out=ubf_3[:, :, bass.ds(offp, TCH)],
                                in_=ucb2[:, :].rearrange("p (c n) -> p c n", c=DC)), "st")
            fl.wait(sy, "st")
        alm.close()
        snap_u(2 * l + 1)

    alf = Alloc(nc)
    cf = alf.sb("p3_cf", [128, DC * TCH], F32)
    uf3 = ufT.rearrange("(c p) n -> p c n", p=128)
    for tc in range(NTCH):
        off = tc * TCH
        fl.dma(sy.dma_start(out=cf[:, :].rearrange("p (c n) -> p c n", c=DC),
                            in_=u32_3[:, :, off:off + TCH]))
        fl.wait(sy, "d")
        fl.dma(sy.dma_start(out=uf3[:, :, off:off + TCH],
                            in_=cf[:, :].rearrange("p (c n) -> p c n", c=DC)), "st")
        fl.wait(sy, "st")
    alf.close()
    al.close()
    return {}



def emit_p4(nc, fl, io):
    x0T = dram(nc, "x0T", (D, NT), F32, io("x0T"))
    xn0T = dram(nc, "xn0T", (D, NT), F32, io("xn0T"))
    attn0T = dram(nc, "attn0T", (D, NT), F32, io("attn0T"))
    ufT = dram(nc, "u_finT", (D, NT), F32, io("u_finT"))
    x1T = dram(nc, "x1T", (D, NT), F32, io("x1T"))
    xn1T = dram(nc, "xn1T", (D, NT), F32, io("xn1T"))
    xn1Tb = dram(nc, "xn1Tb", (D, NT), BF16, io("xn1Tb"))

    v = nc.vector
    s = nc.scalar
    t = nc.tensor
    sy = nc.sync
    al = Alloc(nc)
    a_ = al.sb("p4_a", [128, DC * TCH], F32)
    b_ = al.sb("p4_b", [128, DC * TCH], F32)
    xb = al.sb("p4_xb", [128, DC * TCH], BF16)
    x2b = al.sb("p4_x2", [128, DC * TCH], BF16)
    ys = al.sb("p4_y", [128, DC * TCH], F32)
    yb = al.sb("p4_yb", [128, DC * TCH], BF16)
    onc = al.sb("p4_onc", [128, 1], BF16)
    onr = al.sb("p4_onr", [1, 128], F32)
    rows = al.sb("p4_rows", [1, 4 * TCH], F32)
    ps_s = al.ps("p4_ps_s", [2, TCH], F32)
    ps_mu = al.ps("p4_ps_mu", [128, TCH], F32)
    ps_rs = al.ps("p4_ps_rs", [128, TCH], F32)
    fl.inc(v.memset(onc[:, :], 1.0), "c")
    fl.inc(v.memset(onr[:, :], 1.0), "c")
    MU, RSTD, SS, S2 = 0, TCH, 2 * TCH, 3 * TCH
    v3 = lambda ap: ap.rearrange("(c p) n -> p c n", p=128)
    s3 = lambda sb: sb[:, :].rearrange("p (c n) -> p c n", c=DC)
    srcs = [v3(x0T), v3(xn0T), v3(attn0T), v3(ufT)]
    with fl.loop(0, NTCH, {"d": 64, "c": 12, "st": 48},
                 barrier=[sy, v, s, t]) as tc:
        off = tc * TCH
        fl.dma(sy.dma_start(out=s3(a_), in_=srcs[0][:, :, bass.ds(off, TCH)]))
        fl.dma(sy.dma_start(out=s3(b_), in_=srcs[1][:, :, bass.ds(off, TCH)]))
        fl.wait(v, "d")
        fl.inc(v.tensor_add(a_[:, :], a_[:, :], b_[:, :]), "c")
        fl.wait(sy, "c")
        fl.dma(sy.dma_start(out=s3(b_), in_=srcs[2][:, :, bass.ds(off, TCH)]))
        fl.wait(v, "d")
        fl.inc(v.tensor_add(a_[:, :], a_[:, :], b_[:, :]), "c")
        fl.wait(sy, "c")
        fl.dma(sy.dma_start(out=s3(b_), in_=srcs[3][:, :, bass.ds(off, TCH)]))
        fl.wait(v, "d")
        fl.inc(v.tensor_add(a_[:, :], a_[:, :], b_[:, :]), "c")
        fl.wait(sy, "c")
        fl.dma(sy.dma_start(out=v3(x1T)[:, :, bass.ds(off, TCH)], in_=s3(a_)), "st")
        fl.wait(v, "c")
        fl.inc(v.tensor_copy(xb[:, :], a_[:, :]), "c")
        fl.wait(s, "c")
        fl.inc(s.activation(x2b[:, :], a_[:, :], AF.Square), "c")
        fl.wait(t, "c")
        for c in range(DC):
            t.matmul(ps_s[0:1, :], onc[:, :], xb[:, c * TCH:(c + 1) * TCH],
                     start=(c == 0), stop=(c == DC - 1))
        last = None
        for c in range(DC):
            last = t.matmul(ps_mu[0:1, :], onc[:, :], x2b[:, c * TCH:(c + 1) * TCH],
                            start=(c == 0), stop=(c == DC - 1))
        fl.inc(last, "c")
        fl.wait(v, "c")
        v.tensor_scalar_mul(rows[:, MU:MU + TCH], ps_s[0:1, :], 1.0 / D)
        v.tensor_scalar_mul(rows[:, S2:S2 + TCH], ps_mu[0:1, :], 1.0 / D)
        v.tensor_mul(rows[:, SS:SS + TCH], rows[:, MU:MU + TCH], rows[:, MU:MU + TCH])
        fl.inc(v.tensor_sub(rows[:, S2:S2 + TCH], rows[:, S2:S2 + TCH],
                            rows[:, SS:SS + TCH]), "c")
        fl.wait(s, "c")
        fl.inc(s.activation(rows[:, SS:SS + TCH], rows[:, S2:S2 + TCH],
                            AF.Sqrt, bias=EPS), "c")
        fl.wait(v, "c")
        fl.inc(v.reciprocal(rows[:, RSTD:RSTD + TCH], rows[:, SS:SS + TCH]), "c")
        fl.wait(t, "c")
        t.matmul(ps_mu[:, :], onr[:, :], rows[0:1, MU:MU + TCH], start=True, stop=True)
        fl.inc(t.matmul(ps_rs[:, :], onr[:, :], rows[0:1, RSTD:RSTD + TCH],
                        start=True, stop=True), "c")
        fl.wait(v, "c")
        lastv = None
        for c in range(DC):
            sl = slice(c * TCH, (c + 1) * TCH)
            v.tensor_sub(ys[:, sl], a_[:, sl], ps_mu[:, :])
            lastv = v.tensor_mul(ys[:, sl], ys[:, sl], ps_rs[:, :])
        fl.inc(lastv, "c")
        fl.wait(s, "c")
        fl.inc(s.copy(yb[:, :], ys[:, :]), "c")
        fl.wait(sy, "c")
        fl.dma(sy.dma_start(out=v3(xn1T)[:, :, bass.ds(off, TCH)], in_=s3(ys)), "st")
        fl.dma(sy.dma_start(out=v3(xn1Tb)[:, :, bass.ds(off, TCH)], in_=s3(yb)), "st")
        fl.wait(sy, "st")
    al.close()
    return {}



def emit_p7(nc, fl, io):
    x1T = dram(nc, "x1T", (D, NT), F32, io("x1T"))
    attn1T = dram(nc, "attn1T", (D, NT), F32, io("attn1T"))
    xn1T = dram(nc, "xn1T", (D, NT), F32, io("xn1T"))
    moeT = dram(nc, "moeT", (D, NT), F32, io("moeT"))
    outT = dram(nc, "outT", (D, NT), F32, "out")
    v = nc.vector
    sy = nc.sync
    al = Alloc(nc)
    a_ = al.sb("p7_a", [128, DC * TCH], F32)
    b_ = al.sb("p7_b", [128, DC * TCH], F32)
    v3 = lambda ap: ap.rearrange("(c p) n -> p c n", p=128)
    s3 = lambda sb: sb[:, :].rearrange("p (c n) -> p c n", c=DC)
    srcs = [v3(x1T), v3(attn1T), v3(xn1T), v3(moeT)]
    with fl.loop(0, NTCH, {"d": 64, "c": 3, "st": 16}, barrier=[sy, v]) as tc:
        off = tc * TCH
        fl.dma(sy.dma_start(out=s3(a_), in_=srcs[0][:, :, bass.ds(off, TCH)]))
        for k in range(1, 4):
            fl.dma(sy.dma_start(out=s3(b_), in_=srcs[k][:, :, bass.ds(off, TCH)]))
            fl.wait(v, "d")
            fl.inc(v.tensor_add(a_[:, :], a_[:, :], b_[:, :]), "c")
            if k < 3:
                fl.wait(sy, "c")
        fl.wait(sy, "c")
        fl.dma(sy.dma_start(out=v3(outT)[:, :, bass.ds(off, TCH)], in_=s3(a_)), "st")
        fl.wait(sy, "st")
    al.close()
    return {}



def emit_p5(nc, fl, io):
    xn1Tb = dram(nc, "xn1Tb", (D, NT), BF16, io("xn1Tb"))
    attn1T = dram(nc, "attn1T", (D, NT), F32, io("attn1T"))
    WqT_d = dram(nc, "WqT", (D, D), F32, "in")
    WkT_d = dram(nc, "WkT", (D, D), F32, "in")
    WvT_d = dram(nc, "WvT", (D, D), F32, "in")
    WoT_d = dram(nc, "WoT", (D, D), F32, "in")
    bq_d = dram(nc, "l1_bq", (128, DC), F32, "in")
    bk_d = dram(nc, "l1_bk", (128, DC), F32, "in")
    bv_d = dram(nc, "l1_bv_rep", (128, D), F32, "in")
    bo_d = dram(nc, "l1_bo", (128, DC), F32, "in")
    embKT_d = dram(nc, "embKT", (64, 129), F32, "in")
    embV_d = dram(nc, "embV", (129, 64), F32, "in")
    idf_d = dram(nc, "iden128_p5", (128, 128), F32, "in")
    dlo64_d = dram(nc, "dlo64", (128, 64), F32, "in")
    d65_d = dram(nc, "d65", (128, 65), F32, "in")
    sel65_d = dram(nc, "sel65", (128, 1), F32, "in")
    bmf_d = dram(nc, "bmask_first", (128, 127), F32, "in")
    bml_d = dram(nc, "bmask_last", (128, 127), F32, "in")
    qT = dram(nc, "p5_qT", (D, NT), BF16, "int")
    kT = dram(nc, "p5_kT", (D, NT), BF16, "int")
    Vtok = dram(nc, "p5_vtok", (NT, D), BF16, "int")
    aT = dram(nc, "p5_aT", (D, NT), BF16, "int")
    Zb_t = nc.dram_tensor("p5_zb", [16 * (128 * 641) + 512], BF16, kind="Internal")
    Wb_t = nc.dram_tensor("p5_wb", [128 * 513 + 512], BF16, kind="Internal")

    v = nc.vector
    s = nc.scalar
    t = nc.tensor
    sy = nc.sync
    g = nc.gpsimd

    al = Alloc(nc)
    idb = al.sb("p5_idb", [128, 128], BF16)
    idf = al.sb("p5_idf", [128, 128], F32)
    embk = al.sb("p5_embk", [64, 129], BF16)
    embv_sb = al.sb("p5_embv", [128, 64], BF16)
    embv_hi = al.sb("p5_embvh", [1, 64], BF16)
    dlo64 = al.sb("p5_dlo64", [128, 64], F32)
    d65 = al.sb("p5_d65", [128, 65], F32)
    sel65 = al.sb("p5_sel65", [128, 1], F32)
    bmf = al.sb("p5_bmf", [128, 127], BF16)
    bml = al.sb("p5_bml", [128, 127], BF16)
    ones1f = al.sb("p5_ones1f", [1, 128], F32)
    onesb = al.sb("p5_onesb", [128, 191], BF16)
    onesrow = al.sb("p5_onesrow", [1, 512], BF16)
    zrow1 = al.sb("p5_zrow1", [1, 128], BF16)
    wo_sb = al.sb("p5_wo", [128, DC * D], BF16)
    fl.dma(g.dma_start(out=idb[:, :], in_=idf_d), "g")
    fl.dma(g.dma_start(out=embk[:, :], in_=embKT_d), "g")
    fl.dma(g.dma_start(out=embv_sb[:, :], in_=embV_d[0:128, :]), "g")
    fl.dma(g.dma_start(out=embv_hi[:, :], in_=embV_d[128:129, :]), "g")
    fl.dma(g.dma_start(out=bmf[:, :], in_=bmf_d), "g")
    fl.dma(g.dma_start(out=bml[:, :], in_=bml_d), "g")
    fl.dma(g.dma_start(out=wo_sb[:, :].rearrange("p (c n) -> p c n", c=DC),
                       in_=WoT_d.rearrange("(c p) n -> p c n", p=128)), "g")
    fl.wait(sy, "g")
    fl.dma(sy.dma_start(out=idf[:, :], in_=idf_d))
    fl.dma(sy.dma_start(out=dlo64[:, :], in_=dlo64_d))
    fl.dma(sy.dma_start(out=d65[:, :], in_=d65_d))
    fl.dma(sy.dma_start(out=sel65[:, :], in_=sel65_d))
    fl.inc(v.memset(ones1f[:, :], 1.0), "c")
    fl.inc(v.memset(onesb[:, :], 1.0), "c")
    fl.inc(v.memset(onesrow[:, :], 1.0), "c")
    fl.inc(v.memset(zrow1[:, :], 0.0), "c")

    xn3 = xn1Tb.rearrange("(c p) n -> p c n", p=128)
    qT3 = qT.rearrange("(m p) n -> p m n", p=128)
    kT3 = kT.rearrange("(m p) n -> p m n", p=128)
    vt3 = Vtok.rearrange("(a p) n -> p a n", p=128)

    alp = Alloc(nc)
    wq_sb = alp.sb("p5_wq", [128, DC * D], BF16)
    wk_sb = alp.sb("p5_wk", [128, DC * D], BF16)
    wv_sb = alp.sb("p5_wv", [128, DC * D], BF16)
    bqc = alp.sb("p5_bqc", [128, 2 * DC], F32)
    bvr = alp.sb("p5_bvr", [128, D], F32)
    xin = alp.sb("p5_xin", [128, DC * TCH], BF16)
    prj = alp.sb("p5_prj", [128, 2 * DC * TCH], BF16)
    vpr = alp.sb("p5_vpr", [128, 4 * D], BF16)
    psA = alp.ps("p5_psA", [128, 512], F32)
    psB_ = alp.ps("p5_psB", [128, 512], F32)
    fl.wait(g, "d")
    fl.dma(g.dma_start(out=wq_sb[:, :].rearrange("p (c n) -> p c n", c=DC),
                       in_=WqT_d.rearrange("(c p) n -> p c n", p=128)), "g")
    fl.dma(g.dma_start(out=wk_sb[:, :].rearrange("p (c n) -> p c n", c=DC),
                       in_=WkT_d.rearrange("(c p) n -> p c n", p=128)), "g")
    fl.dma(g.dma_start(out=wv_sb[:, :].rearrange("p (c n) -> p c n", c=DC),
                       in_=WvT_d.rearrange("(c p) n -> p c n", p=128)), "g")
    fl.wait(sy, "g")
    fl.dma(sy.dma_start(out=bqc[:, 0:DC], in_=bq_d))
    fl.dma(sy.dma_start(out=bqc[:, DC:2 * DC], in_=bk_d))
    fl.dma(sy.dma_start(out=bvr[:, :], in_=bv_d))
    fl.wait(t, "g")
    fl.wait(t, "d")
    fl.wait(t, "c")
    fl.wait(sy, "g")
    with fl.loop(0, NTCH, {"d": 16, "c": 48, "st": 320}, barrier=[sy, v, s, t]) as tc:
        off = tc * TCH
        fl.dma(sy.dma_start(out=xin[:, :].rearrange("p (c n) -> p c n", c=DC),
                            in_=xn3[:, :, bass.ds(off, TCH)]))
        fl.wait(t, "d")
        for (wsb, bcol, dst3) in ((wq_sb, 0, qT3), (wk_sb, DC, kT3)):
            for m in range(DC):
                pdst = psA if m % 2 == 0 else psB_
                fl.wait(t, "c")
                lastm = None
                for c in range(DC):
                    lastm = t.matmul(pdst[:, :],
                                     wsb[:, c * D + m * 128:c * D + (m + 1) * 128],
                                     xin[:, c * TCH:(c + 1) * TCH],
                                     start=(c == 0), stop=(c == DC - 1))
                fl.inc(lastm, "c")
                fl.wait(v, "c")
                po = 0 if dst3 is qT3 else DC * TCH
                fl.inc(v.tensor_scalar(prj[:, po + m * TCH:po + (m + 1) * TCH],
                                       pdst[:, :], bqc[:, bcol + m:bcol + m + 1],
                                       None, ALU.add), "c")
                fl.wait(sy, "c")
                fl.dma(sy.dma_start(out=dst3[:, m:m + 1, bass.ds(off, TCH)].opt(),
                                    in_=prj[:, po + m * TCH:po + (m + 1) * TCH]), "st")
        for tt in range(4):
            for half in range(2):
                pdst = psA if half == 0 else psB_
                fl.wait(t, "c")
                lastm = None
                for c in range(DC):
                    lastm = t.matmul(pdst[:, :],
                                     xin[:, c * TCH + tt * 128:c * TCH + tt * 128 + 128],
                                     wv_sb[:, c * D + half * 512:c * D + (half + 1) * 512],
                                     start=(c == 0), stop=(c == DC - 1))
                fl.inc(lastm, "c")
                fl.wait(v, "c")
                fl.inc(v.tensor_tensor(vpr[:, tt * D + half * 512:tt * D + (half + 1) * 512],
                                       pdst[:, :], bvr[:, half * 512:(half + 1) * 512],
                                       ALU.add), "c")
            fl.wait(sy, "c")
            fl.dma(sy.dma_start(out=vt3[:, bass.ds(tc * 4 + tt, 1), :].opt(),
                                in_=vpr[:, tt * D:(tt + 1) * D]), "st")
        fl.wait(sy, "st")
    alp.close()

    ala = Alloc(nc)
    qh = ala.sb("p5_qh", [64, T], BF16)
    kh = ala.sb("p5_kh", [64, T], BF16)
    vh = ala.sb("p5_vh", [128, 16 * 64], BF16)
    rpad = ala.sb("p5_rpad", [128, 512], BF16)
    rpad2 = ala.sb("p5_rpad2", [128, 512], BF16)
    chall = ala.sb("p5_chall", [128, 16], F32)
    chb = ala.sb("p5_chb", [128, 128], BF16)
    chrow = ala.sb("p5_chrow", [1, 128], BF16)
    zeros128 = ala.sb("p5_z128", [128, 128], BF16)
    zer2k = ala.sb("p5_zer2k", [128, T], BF16)
    pex = ala.sb("p5_pex", [128, T], BF16)
    Csb = ala.sb("p5_Csb", [128, T], F32)
    rblk = ala.sb("p5_rblk", [128, 3 * 128], BF16)
    relf = ala.sb("p5_relf", [128, T], F32)
    onesF = ala.sb("p5_onesF", [128, T], F32)
    lg2 = ala.sb("p5_lg2", [128, T], F32)
    rB = ala.sb("p5_rB", [128, 127], BF16)
    ext = ala.sb("p5_ext", [128, 128], F32)
    cols5 = ala.sb("p5_cols", [128, 8], F32)
    wsum = ala.sb("p5_wsum", [128, 129], BF16)
    T1sb = ala.sb("p5_T1", [128, 128], BF16)
    t2sb = ala.sb("p5_t2", [1, 128], BF16)
    ptbA = ala.sb("p5_ptbA", [128, 128], BF16)
    ptbB = ala.sb("p5_ptbB", [128, 128], BF16)
    rrow = ala.sb("p5_rrow", [1, 128], F32)
    rbb = ala.sb("p5_rbb", [64, 128], F32)
    rsb = ala.sb("p5_rsb", [128, 130], F32)
    aall = ala.sb("p5_aall", [64, T], BF16)
    psL = ala.ps("p5_psL", [128, T], F32)
    psR = ala.ps("p5_psR", [128, 512], F32)
    psO = ala.ps("p5_psO", [64, 128], F32)
    psT_b = ala.ps("p5_psT", [128, 128], BF16)
    fl.inc(v.memset(rpad[:, 0:192], 0.0), "c")
    fl.inc(v.memset(rpad2[:, 0:192], 0.0), "c")
    fl.inc(v.memset(zeros128[:, :], 0.0), "c")
    fl.inc(v.memset(zer2k[:, :], 0.0), "c")
    fl.inc(v.memset(onesF[:, :], 1.0), "c")
    fl.wait(sy, "c")
    fl.dma(sy.dma_start(out=AP(Wb_t, 0, [[513, 128], [1, 513]]),
                        in_=zer2k[:, 0:513]), "st")
    fl.wait(sy, "st")
    qT4 = qT.rearrange("(h d) n -> d h n", d=64)
    kT4 = kT.rearrange("(h d) n -> d h n", d=64)
    aT4 = aT.rearrange("(h d) n -> d h n", d=64)
    with fl.loop(0, 32, {"d": 1040, "c": 1235, "st": 528}, barrier=[sy, v, s, t]) as bh:
        b = bh // 16
        h = bh % 16
        fl.dma(sy.dma_start(out=qh[:, :], in_=qT4[:, bass.ds(h, 1), bass.ds(b * T, T)].opt()))
        fl.dma(sy.dma_start(out=kh[:, :], in_=kT4[:, bass.ds(h, 1), bass.ds(b * T, T)].opt()))
        fl.dma(sy.dma_start(out=vh[:, :].rearrange("p (a n) -> p a n", a=16),
                            in_=vt3[:, bass.ds(b * 16, 16), bass.ds(h * 64, 64)]))
        fl.wait(t, "d")
        for qt in range(16):
            q0 = qt * 128
            rp = rpad if qt % 2 == 0 else rpad2
            fl.wait(t, "c")
            fl.inc(t.matmul(psR[:, 0:129], qh[:, q0:q0 + 128], embk[:, :],
                            start=True, stop=True), "c")
            fl.wait(v, "c")
            fl.inc(v.tensor_copy(rsb[:, 0:129], psR[:, 0:129]), "c")
            fl.wait(v, "c")
            fl.wait(v, "st", lag=16)
            fl.inc(v.tensor_scalar(rp[:, 192:321], rsb[:, 0:129], rsb[:, 0:1],
                                   None, ALU.subtract), "c")
            fl.inc(v.tensor_tensor(chall[:, qt:qt + 1], rsb[:, 128:129],
                                   rsb[:, 0:1], ALU.subtract), "c")
            fl.wait(v, "c")
            fl.inc(v.tensor_scalar(rp[:, 321:512], onesb[:, :],
                                   chall[:, qt:qt + 1], None, ALU.mult), "c")
            fl.wait(sy, "c")
            fl.dma(sy.dma_start(out=AP(Zb_t, qt * (128 * 641), [[641, 128], [1, 512]]),
                                in_=rp[:, 0:512]), "st")
        fl.wait(sy, "st")
        for qt in range(16):
            q0 = qt * 128
            qtile = qh[:, q0:q0 + 128]
            bb_lo = max(qt - 1, 0)
            bb_hi = min(qt + 1, 15)
            fl.wait(sy, "c")
            for j, bb in enumerate(range(bb_lo, bb_hi + 1)):
                d0 = (bb - qt) * 128
                fl.dma(sy.dma_start(out=rblk[:, j * 128:(j + 1) * 128],
                                    in_=AP(Zb_t, qt * (128 * 641) + d0 + 256,
                                           [[640, 128], [1, 128]])))
            fl.wait(t, "d")
            fl.wait(t, "c")
            lastm = None
            for a in range(4):
                lastm = t.matmul(psL[:, a * 512:(a + 1) * 512], qtile,
                                 kh[:, a * 512:(a + 1) * 512],
                                 start=True, stop=True)
            fl.inc(lastm, "c")
            klo2 = max(q0 - 128, 0)
            khi2 = min(q0 + 256, T)
            fl.wait(v, "c")
            if klo2 > 0:
                v.memset(relf[:, 0:klo2], 0.0)
            v.tensor_copy(relf[:, klo2:khi2], rblk[:, 0:khi2 - klo2])
            if khi2 < T:
                v.tensor_scalar(relf[:, khi2:T], onesF[:, khi2:T],
                                chall[:, qt:qt + 1], None, ALU.mult)
            fl.inc(v.tensor_copy(cols5[:, 7:8], chall[:, qt:qt + 1]), "c")
            fl.wait(v, "c")
            fl.inc(v.scalar_tensor_tensor(lg2[:, :], psL[:, :], 1.0, relf[:, :],
                                          ALU.mult, ALU.add), "c")
            fl.wait(v, "c")
            fl.inc(v.max(cols5[:, 0:8], lg2[:, :]), "c")
            fl.wait(v, "c")
            fl.inc(v.tensor_scalar_mul(cols5[:, 0:1], cols5[:, 0:1], -1.0), "c")
            fl.wait(s, "c")
            fl.inc(s.activation(pex[:, :], lg2[:, :], AF.Exp, bias=cols5[:, 0:1],
                                accum_out=cols5[:, 1:2]), "c")
            klo = max(q0 - 128, 0)
            khi = min(q0 + 256, T)
            fl.wait(sy, "c")
            fl.wait(sy, "d")
            fl.dma(sy.dma_start(out=AP(Wb_t, 64, [[512, 128], [1, khi - klo]]),
                                in_=pex[:, klo:khi]), "st")
            fl.wait(v, "c")
            fl.inc(v.tensor_tensor_scan(Csb[:, :], pex[:, :], zer2k[:, :], 0.0,
                                        ALU.add, ALU.add), "c")
            fl.wait(v, "c")
            if qt == 0:
                fl.inc(v.tensor_tensor(ext[:, 0:64], Csb[:, 0:64], dlo64[:, :],
                                       ALU.mult), "c")
                fl.wait(v, "c")
                fl.inc(v.tensor_reduce(cols5[:, 2:3], ext[:, 0:64], AX, ALU.add), "c")
            else:
                fl.inc(v.tensor_tensor(ext[:, :], Csb[:, q0 - 64:q0 + 64], idf[:, :],
                                       ALU.mult), "c")
                fl.wait(v, "c")
                fl.inc(v.tensor_reduce(cols5[:, 2:3], ext[:, :], AX, ALU.add), "c")
            fl.wait(v, "c")
            if qt == 15:
                fl.inc(v.tensor_tensor(ext[:, 0:65], Csb[:, 1983:2048], d65[:, :],
                                       ALU.mult), "c")
                fl.wait(v, "c")
                fl.inc(v.tensor_reduce(cols5[:, 3:4], ext[:, 0:65], AX, ALU.add), "c")
            else:
                fl.inc(v.tensor_tensor(ext[:, :], Csb[:, q0 + 63:q0 + 191], idf[:, :],
                                       ALU.mult), "c")
                fl.wait(v, "c")
                fl.inc(v.tensor_reduce(cols5[:, 3:4], ext[:, :], AX, ALU.add), "c")
            fl.wait(v, "c")
            fl.inc(v.tensor_tensor(cols5[:, 4:5], Csb[:, T - 1:T], cols5[:, 3:4],
                                   ALU.subtract), "c")
            if qt == 15:
                fl.wait(v, "c")
                fl.inc(v.tensor_mul(cols5[:, 4:5], cols5[:, 4:5], sel65[:, :]), "c")
            for kt in range(16):
                ptb = ptbA if kt % 2 == 0 else ptbB
                fl.wait(t, "c")
                fl.inc(t.transpose(psT_b[:, :], pex[:, kt * 128:(kt + 1) * 128],
                                   idb[:, :]), "c")
                fl.wait(v, "c")
                fl.inc(v.tensor_copy(ptb[:, :], psT_b[:, :]), "c")
                fl.wait(t, "c")
                fl.inc(t.matmul(psO[:, :], vh[:, kt * 64:(kt + 1) * 64], ptb[:, :],
                                start=(kt == 0), stop=False), "c")
            fl.wait(sy, "st")
            fl.dma(sy.dma_start(out=rB[:, :],
                                in_=AP(Wb_t, 1 if qt == 0 else 129,
                                       [[513, 128], [1, 127]])))
            fl.wait(v, "d")
            if qt == 0:
                fl.inc(v.tensor_mul(rB[:, :], rB[:, :], bmf[:, :]), "c")
            if qt == 15:
                fl.inc(v.tensor_mul(rB[:, :], rB[:, :], bml[:, :]), "c")
            fl.wait(v, "c")
            v.tensor_copy(wsum[:, 1:128], rB[:, :])
            v.tensor_copy(wsum[:, 0:1], cols5[:, 2:3])
            fl.inc(v.tensor_copy(wsum[:, 128:129], cols5[:, 4:5]), "c")
            fl.wait(t, "c")
            fl.inc(t.transpose(psT_b[:, :], wsum[:, 0:128], idb[:, :]), "c")
            fl.wait(v, "c")
            fl.inc(v.tensor_copy(T1sb[:, :], psT_b[:, :]), "c")
            fl.wait(t, "c")
            fl.inc(t.transpose(psT_b[0:1, 0:128], wsum[:, 128:129], idb[:, :]), "c")
            fl.wait(v, "c")
            fl.inc(v.tensor_copy(t2sb[:, :], psT_b[0:1, 0:128]), "c")
            fl.wait(t, "c")
            t.matmul(psO[:, :], embv_sb[:, :], T1sb[:, :],
                     start=False, stop=False)
            fl.inc(t.matmul(psO[:, :], embv_hi[:, :], t2sb[:, :],
                            start=False, stop=True), "c")
            fl.wait(v, "c")
            fl.inc(v.reciprocal(cols5[:, 5:6], cols5[:, 1:2]), "c")
            fl.wait(t, "c")
            fl.inc(t.transpose(psR[0:1, 0:128], cols5[:, 5:6], idf[:, :]), "c")
            fl.wait(v, "c")
            fl.inc(v.tensor_copy(rrow[:, :], psR[0:1, 0:128]), "c")
            fl.wait(t, "c")
            fl.inc(t.matmul(psR[0:64, 128:256], ones1f[:, 0:64], rrow[0:1, :],
                            start=True, stop=True), "c")
            fl.wait(v, "c")
            fl.inc(v.tensor_copy(rbb[:, :], psR[0:64, 128:256]), "c")
            fl.wait(v, "c")
            fl.inc(v.tensor_tensor(aall[:, q0:q0 + 128], psO[:, :],
                                   rbb[:, :], ALU.mult), "c")
        fl.wait(sy, "c")
        fl.dma(sy.dma_start(out=aT4[0:64, bass.ds(h, 1), bass.ds(b * T, T)].opt(),
                            in_=aall[:, :]), "st")
        fl.wait(sy, "st")
    ala.close()

    alo = Alloc(nc)
    boc = alo.sb("p5_boc", [128, DC], F32)
    ain = alo.sb("p5_ain", [128, DC * TCH], BF16)
    oout = alo.sb("p5_oout", [128, DC * TCH], F32)
    psA2 = alo.ps("p5_psA2", [128, 512], F32)
    psB2 = alo.ps("p5_psB2", [128, 512], F32)
    fl.dma(sy.dma_start(out=boc[:, :], in_=bo_d))
    aT3 = aT.rearrange("(c p) n -> p c n", p=128)
    at3o = attn1T.rearrange("(c p) n -> p c n", p=128)
    fl.wait(t, "d")
    with fl.loop(0, NTCH, {"d": 16, "c": 16, "st": 16}, barrier=[sy, v, s, t]) as tc:
        off = tc * TCH
        fl.dma(sy.dma_start(out=ain[:, :].rearrange("p (c n) -> p c n", c=DC),
                            in_=aT3[:, :, bass.ds(off, TCH)]))
        fl.wait(t, "d")
        for m in range(DC):
            pdst = psA2 if m % 2 == 0 else psB2
            fl.wait(t, "c")
            lastm = None
            for c in range(DC):
                lastm = t.matmul(pdst[:, :],
                                 wo_sb[:, c * D + m * 128:c * D + (m + 1) * 128],
                                 ain[:, c * TCH:(c + 1) * TCH],
                                 start=(c == 0), stop=(c == DC - 1))
            fl.inc(lastm, "c")
            fl.wait(v, "c")
            fl.inc(v.tensor_scalar(oout[:, m * TCH:(m + 1) * TCH], pdst[:, :],
                                   boc[:, m:m + 1], None, ALU.add), "c")
        fl.wait(sy, "c")
        fl.dma(sy.dma_start(out=at3o[:, :, bass.ds(off, TCH)],
                            in_=oout[:, :].rearrange("p (c n) -> p c n", c=DC)), "st")
        fl.wait(sy, "st")
    alo.close()
    al.close()
    return {}



KCAP = 512
NEXP = 32
NSLOT = NEXP * KCAP
CGLOB = 1280


def emit_p6(nc, fl, io):
    xn1T = dram(nc, "xn1T", (D, NT), F32, io("xn1T"))
    xn1Tb = dram(nc, "xn1Tb", (D, NT), BF16, io("xn1Tb"))
    moeT = dram(nc, "moeT", (D, NT), F32, io("moeT"))
    wr_d = dram(nc, "w_router", (D, NEXP), F32, "in")
    wg_d = dram(nc, "w_gate", (NEXP, D, D), F32, "in")
    wu_d = dram(nc, "w_up", (NEXP, D, D), F32, "in")
    wd_d = dram(nc, "w_down", (NEXP, D, D), F32, "in")
    iota_d = dram(nc, "iota32r", (128, NEXP), F32, "in")
    tri_d = dram(nc, "tri_strict128", (128, 128), F32, "in")
    idf_d = dram(nc, "iden128", (128, 128), F32, "in")
    Xslots = dram(nc, "p6_xslots", (NSLOT, D), BF16, "int")
    Yslots = dram(nc, "p6_yslots", (NSLOT, D), BF16, "int")
    amat_i = nc.dram_tensor("p6_amat_i", [8, NEXP], F32, kind="Internal").ap()
    amat_o = nc.dram_tensor("p6_amat_o", [8, NEXP], F32, kind="Internal").ap()
    incl_d = dram(nc, "p6_incl", (8, NEXP), F32, "int")

    v = nc.vector
    s = nc.scalar
    t = nc.tensor
    sy = nc.sync
    g = nc.gpsimd
    rank = nc.partition_id()

    al = Alloc(nc)
    wr_sb = al.sb("p6_wr", [128, DC * NEXP], F32)
    iota32 = al.sb("p6_iota", [128, NEXP], F32)
    iotam = al.sb("p6_iotam", [128, NEXP], F32)
    tria = al.sb("p6_tri", [128, 128], F32)
    idf = al.sb("p6_idf", [128, 128], F32)
    idb = al.sb("p6_idb", [128, 128], BF16)
    ones1f = al.sb("p6_ones1", [1, 128], F32)
    onescf = al.sb("p6_onesc", [128, 1], F32)
    big = al.sb("p6_big", [128, 1], F32)
    ecol_a = al.sb("p6_ecol", [128, 32], F32)
    pos_a = al.sb("p6_pos", [128, 32], F32)
    gate_a = al.sb("p6_gate", [128, 32], F32)
    islot = al.sb("p6_islot", [128, 32], I32)
    islot1 = al.sb("p6_islot1", [128, 1], I32)
    rt = al.sb("p6_rt", [1, NEXP], F32)
    capb = al.sb("p6_capb", [128, NEXP], F32)
    xn1_3 = xn1T.rearrange("(c p) n -> p c n", p=128)
    xn1b_3 = xn1Tb.rearrange("(c p) n -> p c n", p=128)
    moe_3 = moeT.rearrange("(c p) n -> p c n", p=128)
    xs_3 = Xslots.rearrange("(a p) n -> p a n", p=128)
    ys_3 = Yslots.rearrange("(a p) n -> p a n", p=128)

    fl.inc(v.memset(ones1f[:, :], 1.0), "c")
    fl.inc(v.memset(onescf[:, :], 1.0), "c")
    fl.inc(v.memset(big[:, :], 1.0e9), "c")
    fl.inc(v.memset(rt[:, :], 0.0), "c")
    fl.dma(g.dma_start(out=idb[:, :], in_=idf_d), "g")
    fl.wait(sy, "g")
    fl.dma(sy.dma_start(out=wr_sb[:, :].rearrange("p (c n) -> p c n", c=DC),
                        in_=wr_d.rearrange("(c p) n -> p c n", p=128)))
    fl.dma(sy.dma_start(out=iota32[:, :], in_=iota_d))
    fl.dma(sy.dma_start(out=tria[:, :], in_=tri_d))
    fl.dma(sy.dma_start(out=idf[:, :], in_=idf_d))
    fl.wait(v, "d")
    fl.inc(v.tensor_scalar(iotam[:, :], iota32[:, :], -1.0e4, None, ALU.add), "c")

    alr = Alloc(nc)
    xt_sb = alr.sb("p6_xt", [128, DC * 128], F32)
    lg_sb = alr.sb("p6_lg", [128, NEXP], F32)
    esc = alr.sb("p6_esc", [128, NEXP], F32)
    oh = alr.sb("p6_oh", [128, NEXP], F32)
    oh2 = alr.sb("p6_oh2", [128, NEXP], F32)
    tmp = alr.sb("p6_tmp", [128, NEXP], F32)
    m8 = alr.sb("p6_m8", [128, 8], F32)
    ng = alr.sb("p6_ng", [128, 1], F32)
    den = alr.sb("p6_den", [128, 1], F32)
    ps1 = alr.ps("p6_ps1", [128, 512], F32)
    ps2 = alr.ps("p6_ps2", [128, 512], F32)
    ps3 = alr.ps("p6_ps3", [128, 512], F32)
    ps4 = alr.ps("p6_ps4", [128, 512], F32)
    fl.wait(t, "d")
    fl.wait(t, "c")
    fl.wait(sy, "g")
    with fl.loop(0, 32, {"d": 16, "c": 17}, barrier=[sy, v, s, t]) as ti:
        toff = ti * 128
        fl.dma(sy.dma_start(out=xt_sb[:, :].rearrange("p (c n) -> p c n", c=DC),
                            in_=xn1_3[:, :, bass.ds(toff, 128)]))
        fl.wait(t, "d")
        fl.wait(t, "c")
        lastm = None
        for c in range(DC):
            lastm = t.matmul(ps1[:, 0:NEXP], xt_sb[:, c * 128:(c + 1) * 128],
                             wr_sb[:, c * NEXP:(c + 1) * NEXP],
                             start=(c == 0), stop=(c == DC - 1))
        fl.inc(lastm, "c")
        fl.wait(v, "c")
        fl.inc(v.tensor_copy(lg_sb[:, :], ps1[:, 0:NEXP]), "c")
        fl.wait(v, "c")
        fl.inc(v.max(m8[:, :], lg_sb[:, :]), "c")
        fl.wait(v, "c")
        fl.inc(v.tensor_scalar_mul(ng[:, :], m8[:, 0:1], -1.0), "c")
        fl.inc(v.tensor_scalar(oh[:, :], lg_sb[:, :], m8[:, 0:1], None,
                               ALU.is_ge), "c")
        fl.wait(s, "c")
        fl.inc(s.activation(esc[:, :], lg_sb[:, :], AF.Exp, bias=ng[:, 0:1],
                            accum_out=den[:, 0:1]), "c")
        fl.wait(v, "c")
        fl.inc(v.reciprocal(gate_a[:, bass.ds(ti, 1)], den[:, 0:1]), "c")
        fl.inc(v.tensor_mul(tmp[:, :], oh[:, :], iotam[:, :]), "c")
        fl.wait(v, "c")
        fl.inc(v.tensor_scalar(tmp[:, :], tmp[:, :], 1.0e4, None, ALU.add), "c")
        fl.wait(v, "c")
        fl.inc(v.tensor_reduce(ecol_a[:, bass.ds(ti, 1)], tmp[:, :], AX,
                               ALU.min), "c")
        fl.wait(v, "c")
        fl.inc(v.tensor_scalar(oh2[:, :], iota32[:, :],
                               ecol_a[:, bass.ds(ti, 1)], None, ALU.is_equal), "c")
        fl.wait(t, "c")
        t.matmul(ps2[:, 0:NEXP], tria[:, :], oh2[:, :], start=True, stop=True)
        t.matmul(ps4[0:1, 0:NEXP], onescf[:, :], oh2[:, :], start=True, stop=True)
        fl.inc(t.matmul(ps3[:, 0:NEXP], ones1f[:, :], rt[0:1, :],
                        start=True, stop=True), "c")
        fl.wait(v, "c")
        fl.inc(v.tensor_copy(tmp[:, :], ps2[:, 0:NEXP]), "c")
        fl.wait(v, "c")
        fl.inc(v.tensor_tensor(tmp[:, :], tmp[:, :], ps3[:, 0:NEXP], ALU.add), "c")
        fl.wait(v, "c")
        fl.inc(v.tensor_mul(tmp[:, :], tmp[:, :], oh2[:, :]), "c")
        fl.wait(v, "c")
        fl.inc(v.tensor_reduce(pos_a[:, bass.ds(ti, 1)], tmp[:, :], AX,
                               ALU.add), "c")
        fl.inc(v.tensor_add(rt[0:1, :], rt[0:1, :], ps4[0:1, 0:NEXP]), "c")

    zz8 = alr.sb("p6_zz8", [NEXP, 8], F32)
    sc8 = alr.sb("p6_sc8", [NEXP, 8], F32)
    incl = alr.sb("p6_incl_sb", [NEXP, 8], F32)
    a8 = alr.sb("p6_a8", [8, NEXP], F32)
    prow = alr.sb("p6_prow", [1, NEXP], F32)
    caprow = alr.sb("p6_caprow", [1, NEXP], F32)
    fl.wait(v, "c")
    fl.inc(v.memset(zz8[:, :], 0.0), "c")
    fl.inc(v.memset(a8[:, :], 0.0), "c")
    fl.wait(sy, "c")
    fl.dma(sy.dma_start(out=amat_i, in_=a8[:, :]), "st")
    fl.wait(sy, "st")
    fl.dma(sy.dma_start(out=amat_i[bass.ds(rank, 1), :], in_=rt[0:1, :]), "st")
    fl.wait(g, "st")
    fl.inc(g.collective_compute("AllReduce", ALU.add, [[0, 1, 2, 3, 4, 5, 6, 7]],
                                [amat_i], [amat_o], cc_dim="Partition"), "cc")
    fl.wait(sy, "cc")
    fl.dma(sy.dma_start(out=a8[:, :], in_=amat_o))
    fl.wait(t, "d")
    fl.wait(t, "c")
    fl.inc(t.transpose(ps1[0:NEXP, 0:8], a8[:, :], idf[0:8, 0:8]), "c")
    fl.wait(v, "c")
    fl.inc(v.tensor_copy(sc8[:, :], ps1[0:NEXP, 0:8]), "c")
    fl.wait(v, "c")
    fl.inc(v.tensor_tensor_scan(incl[:, :], sc8[:, :], zz8[:, :], 0.0,
                                ALU.add, ALU.add), "c")
    fl.wait(t, "c")
    fl.inc(t.transpose(ps2[0:8, 0:NEXP], incl[:, :], idf[0:NEXP, 0:NEXP]), "c")
    fl.wait(v, "c")
    fl.inc(v.tensor_copy(a8[:, :], ps2[0:8, 0:NEXP]), "c")
    fl.wait(sy, "c")
    fl.dma(sy.dma_start(out=incl_d, in_=a8[:, :]), "st")
    fl.wait(sy, "st")
    fl.dma(sy.dma_start(out=prow[:, :], in_=incl_d[bass.ds(rank, 1), :]))
    fl.wait(v, "d")
    fl.inc(v.tensor_sub(prow[:, :], prow[:, :], rt[0:1, :]), "c")
    fl.wait(v, "c")
    fl.inc(v.tensor_scalar(caprow[:, :], prow[:, :], -1.0, float(CGLOB),
                           ALU.mult, ALU.add), "c")
    fl.wait(v, "c")
    fl.inc(v.tensor_scalar(caprow[:, :], caprow[:, :], 0.0, float(KCAP),
                           ALU.max, ALU.min), "c")
    fl.wait(t, "c")
    fl.inc(t.matmul(ps2[:, 0:NEXP], ones1f[:, :], caprow[0:1, :],
                    start=True, stop=True), "c")
    fl.wait(v, "c")
    fl.inc(v.tensor_copy(capb[:, :], ps2[:, 0:NEXP]), "c")
    if DEBUG_INT_OUT:
        for dbn, dbuf, dbw in (("lg", lg_sb, 32), ("m8", m8, 8), ("oh", oh, 32),
                               ("tmp", tmp, 32), ("ec", ecol_a, 32),
                               ("pos", pos_a, 32), ("gate", gate_a, 32)):
            dd2 = dram(nc, "p6_dbg_" + dbn, (128, dbw), F32, "out")
            fl.wait(sy, "c")
            fl.dma(sy.dma_start(out=dd2, in_=dbuf[:, 0:dbw]), "st")
        fl.wait(sy, "st")
        d_a8 = dram(nc, "p6_dbg_a8", (8, NEXP), F32, "out")
        d_cap = dram(nc, "p6_dbg_cap", (1, NEXP), F32, "out")
        d_rt = dram(nc, "p6_dbg_rt", (1, NEXP), F32, "out")
        fl.wait(sy, "c")
        fl.dma(sy.dma_start(out=d_a8, in_=a8[:, :]), "st")
        fl.dma(sy.dma_start(out=d_cap, in_=caprow[:, :]), "st")
        fl.dma(sy.dma_start(out=d_rt, in_=rt[0:1, :]), "st")
        fl.wait(sy, "st")

    xcol = alr.sb("p6_xcol", [128, DC * 128], BF16)
    xrow = alr.sb("p6_xrow", [128, D], BF16)
    slotf = alr.sb("p6_slotf", [128, 4], F32)
    acc_i = alr.sb("p6_acci", [128, 1], I32)
    pst = alr.ps("p6_pst", [128, 128], BF16)
    with fl.loop(0, 32, {"d": 16, "c": 24, "g": 16}, barrier=[sy, v, s, t, g]) as ti:
        toff = ti * 128
        fl.dma(sy.dma_start(out=xcol[:, :].rearrange("p (c n) -> p c n", c=DC),
                            in_=xn1b_3[:, :, bass.ds(toff, 128)]))
        fl.wait(t, "d")
        for c in range(DC):
            fl.wait(t, "c")
            fl.inc(t.transpose(pst[:, :], xcol[:, c * 128:(c + 1) * 128],
                               idb[:, :]), "c")
            fl.wait(v, "c")
            fl.inc(v.tensor_copy(xrow[:, c * 128:(c + 1) * 128], pst[:, :]), "c")
        fl.wait(v, "c")
        fl.inc(v.tensor_scalar(oh2[:, :], iota32[:, :],
                               ecol_a[:, bass.ds(ti, 1)], None, ALU.is_equal), "c")
        fl.wait(v, "c")
        fl.inc(v.tensor_mul(tmp[:, :], oh2[:, :], capb[:, :]), "c")
        fl.wait(v, "c")
        fl.inc(v.tensor_reduce(slotf[:, 0:1], tmp[:, :], AX, ALU.add), "c")
        fl.wait(v, "c")
        fl.inc(v.tensor_tensor(acc_i[:, :], pos_a[:, bass.ds(ti, 1)],
                               slotf[:, 0:1], ALU.is_lt), "c")
        fl.inc(v.scalar_tensor_tensor(slotf[:, 2:3], ecol_a[:, bass.ds(ti, 1)],
                                      float(KCAP), pos_a[:, bass.ds(ti, 1)],
                                      ALU.mult, ALU.add), "c")
        fl.wait(v, "c")
        fl.inc(v.select(slotf[:, 3:4], acc_i[:, :], slotf[:, 2:3],
                        big[:, 0:1]), "c")
        fl.wait(v, "c")
        fl.inc(v.tensor_copy(islot1[:, :], slotf[:, 3:4]), "c")
        fl.wait(v, "c")
        fl.inc(v.tensor_copy(islot[:, bass.ds(ti, 1)], islot1[:, :]), "c")
        fl.wait(g, "c")
        fl.dma(g.indirect_dma_start(
            out=Xslots[:, :], out_offset=IndirectOffsetOnAxis(islot1[:, 0:1], 0),
            in_=xrow[:, :], in_offset=None,
            bounds_check=NSLOT - 1, oob_is_err=False), "g")
    alr.close()

    alw = Alloc(nc)
    wga = alw.sb("p6_wga", [128, DC * D], BF16)
    wua = alw.sb("p6_wua", [128, DC * D], BF16)
    wda = alw.sb("p6_wda", [128, DC * D], BF16)
    xe = alw.sb("p6_xe", [128, 4 * D], BF16)
    xeT = alw.sb("p6_xeT", [128, DC * 128], BF16)
    h1sb = alw.sb("p6_h1", [128, D], BF16)
    h1T = alw.sb("p6_h1T", [128, DC * 128], BF16)
    gsc = alw.sb("p6_gsc", [128, 512], BF16)
    usb = alw.sb("p6_usb", [128, 512], BF16)
    ysb = alw.sb("p6_ysb", [128, 4 * D], BF16)
    psG0 = alw.ps("p6_psG0", [128, 512], F32)
    psG1 = alw.ps("p6_psG1", [128, 512], F32)
    psU0 = alw.ps("p6_psU0", [128, 512], F32)
    psU1 = alw.ps("p6_psU1", [128, 512], F32)
    psD0 = alw.ps("p6_psD0", [128, 512], F32)
    psD1 = alw.ps("p6_psD1", [128, 512], F32)
    pstE = alw.ps("p6_pstE", [128, 128], BF16)
    wg3 = wg_d.rearrange("e (c p) n -> p e c n", p=128)
    wu3 = wu_d.rearrange("e (c p) n -> p e c n", p=128)
    wd3 = wd_d.rearrange("e (c p) n -> p e c n", p=128)
    with fl.loop(0, NEXP, {"d": 16, "c": 184, "g": 48, "st": 16},
                 barrier=[sy, v, s, t, g]) as ee:
        fl.dma(g.dma_start(out=wga[:, :].rearrange("p (c n) -> p c n", c=DC),
                           in_=wg3[:, bass.ds(ee, 1), :, :].opt()), "g")
        fl.dma(g.dma_start(out=wua[:, :].rearrange("p (c n) -> p c n", c=DC),
                           in_=wu3[:, bass.ds(ee, 1), :, :].opt()), "g")
        fl.dma(g.dma_start(out=wda[:, :].rearrange("p (c n) -> p c n", c=DC),
                           in_=wd3[:, bass.ds(ee, 1), :, :].opt()), "g")
        fl.wait(sy, "g")
        fl.dma(sy.dma_start(out=xe[:, :].rearrange("p (k n) -> p k n", k=4),
                            in_=xs_3[:, bass.ds(ee * 4, 4), :]))
        for tt in range(4):
            for c in range(DC):
                fl.wait(t, "c")
                if tt == 0 and c == 0:
                    fl.wait(t, "d")
                fl.inc(t.transpose(pstE[:, :],
                                   xe[:, tt * D + c * 128:tt * D + (c + 1) * 128],
                                   idb[:, :]), "c")
                fl.wait(v, "c")
                fl.inc(v.tensor_copy(xeT[:, c * 128:(c + 1) * 128], pstE[:, :]), "c")
            fl.wait(t, "c")
            if tt == 0:
                fl.wait(t, "g")
            for fc, pdst in ((0, psG0), (1, psG1)):
                lastm = None
                for hc in range(DC):
                    lastm = t.matmul(pdst[:, :], xeT[:, hc * 128:(hc + 1) * 128],
                                     wga[:, hc * D + fc * 512:hc * D + (fc + 1) * 512],
                                     start=(hc == 0), stop=(hc == DC - 1))
                fl.inc(lastm, "c")
            for fc, pdst in ((0, psU0), (1, psU1)):
                lastm = None
                for hc in range(DC):
                    lastm = t.matmul(pdst[:, :], xeT[:, hc * 128:(hc + 1) * 128],
                                     wua[:, hc * D + fc * 512:hc * D + (fc + 1) * 512],
                                     start=(hc == 0), stop=(hc == DC - 1))
                fl.inc(lastm, "c")
            for fc in range(2):
                fl.wait(s, "c")
                fl.inc(s.activation(gsc[:, :], (psG0 if fc == 0 else psG1)[:, :],
                                    AF.Silu), "c")
                fl.inc(s.copy(usb[:, :], (psU0 if fc == 0 else psU1)[:, :]), "c")
                fl.wait(v, "c")
                fl.inc(v.tensor_mul(h1sb[:, fc * 512:(fc + 1) * 512],
                                    gsc[:, :], usb[:, :]), "c")
            for c in range(DC):
                fl.wait(t, "c")
                fl.inc(t.transpose(pstE[:, :], h1sb[:, c * 128:(c + 1) * 128],
                                   idb[:, :]), "c")
                fl.wait(v, "c")
                fl.inc(v.tensor_copy(h1T[:, c * 128:(c + 1) * 128], pstE[:, :]), "c")
            fl.wait(t, "c")
            for oc, pdst in ((0, psD0), (1, psD1)):
                lastm = None
                for fc in range(DC):
                    lastm = t.matmul(pdst[:, :], h1T[:, fc * 128:(fc + 1) * 128],
                                     wda[:, fc * D + oc * 512:fc * D + (oc + 1) * 512],
                                     start=(fc == 0), stop=(fc == DC - 1))
                fl.inc(lastm, "c")
            for oc in range(2):
                fl.wait(v, "c")
                fl.inc(v.tensor_copy(ysb[:, tt * D + oc * 512:tt * D + (oc + 1) * 512],
                                     (psD0 if oc == 0 else psD1)[:, :]), "c")
        fl.wait(sy, "c")
        fl.dma(sy.dma_start(out=ys_3[:, bass.ds(ee * 4, 4), :],
                            in_=ysb[:, :].rearrange("p (k n) -> p k n", k=4)), "st")
    alw.close()

    alg = Alloc(nc)
    ytk = alg.sb("p6_ytk", [128, D], BF16)
    ytk32 = alg.sb("p6_ytk32", [128, D], F32)
    ytkb = alg.sb("p6_ytkb", [128, D], BF16)
    ymT = alg.sb("p6_ymT", [128, DC * 128], F32)
    pstG = alg.ps("p6_pstG", [128, 128], BF16)
    fl.wait(g, "st")
    with fl.loop(0, 32, {"c": 21, "g": 16, "st": 16}, barrier=[sy, v, s, t, g]) as ti:
        toff = ti * 128
        fl.wait(v, "g")
        fl.inc(v.memset(ytk[:, :], 0.0), "c")
        fl.wait(v, "c")
        fl.inc(v.tensor_copy(islot1[:, :], islot[:, bass.ds(ti, 1)]), "c")
        fl.wait(g, "c")
        fl.dma(g.indirect_dma_start(
            out=ytk[:, :], out_offset=None,
            in_=Yslots[:, :], in_offset=IndirectOffsetOnAxis(islot1[:, 0:1], 0),
            bounds_check=NSLOT - 1, oob_is_err=False), "g")
        fl.wait(v, "g")
        fl.inc(v.tensor_copy(ytk32[:, :], ytk[:, :]), "c")
        fl.wait(v, "c")
        fl.inc(v.tensor_scalar(ytk32[:, :], ytk32[:, :],
                               gate_a[:, bass.ds(ti, 1)], None, ALU.mult), "c")
        fl.wait(v, "c")
        fl.inc(v.tensor_copy(ytkb[:, :], ytk32[:, :]), "c")
        for c in range(DC):
            fl.wait(t, "c")
            fl.inc(t.transpose(pstG[:, :], ytkb[:, c * 128:(c + 1) * 128],
                               idb[:, :]), "c")
            fl.wait(v, "c")
            fl.inc(v.tensor_copy(ymT[:, c * 128:(c + 1) * 128], pstG[:, :]), "c")
        fl.wait(sy, "c")
        fl.dma(sy.dma_start(out=moe_3[:, :, bass.ds(toff, 128)],
                            in_=ymT[:, :].rearrange("p (c n) -> p c n", c=DC)), "st")
        fl.wait(sy, "st")
    alg.close()
    al.close()
    return {}



def _colpack(vec):
    v = np.asarray(vec, np.float32)
    k = v.size // 128
    return np.ascontiguousarray(v.reshape(k, 128).T)


def host_prep(inputs):
    h = {}
    x0 = np.asarray(inputs["L0_attn_query"], np.float32)
    percore = []
    for c in range(NCORES):
        xc = x0[2 * c:2 * c + 2].reshape(NT, D)
        percore.append({"x0T": np.ascontiguousarray(xc.T)})
    h["percore"] = percore

    sh = {}
    sh["qwT"] = np.ascontiguousarray(np.asarray(inputs["L0_attn_q_w"], np.float32).T)
    sh["kwT"] = np.ascontiguousarray(np.asarray(inputs["L0_attn_k_w"], np.float32).T)
    sh["vwT"] = np.ascontiguousarray(np.asarray(inputs["L0_attn_v_w"], np.float32).T)
    sh["owT"] = np.ascontiguousarray(np.asarray(inputs["L0_attn_o_w"], np.float32).T)
    order = ["ln_q_g", "ln_q_b", "ln_k_g", "ln_k_b", "ln_v_g", "ln_v_b",
             "q_b", "k_b", "v_b", "o_b"]
    sh["a_gb"] = np.stack([_colpack(inputs["L0_attn_" + nm]) for nm in order])
    sh["a_vb_rep"] = np.ascontiguousarray(
        np.tile(np.asarray(inputs["L0_attn_v_b"], np.float32).reshape(1, D), (128, 1)))
    sh["beta_param"] = np.asarray(inputs["L0_attn_beta_param"], np.float32).reshape(1, 16)

    HID = 2730
    HIDP = 2816
    wqkv = np.asarray(inputs["L0_ffn_wqkv"], np.float32)
    w1 = np.asarray(inputs["L0_ffn_w1"], np.float32)
    b1 = np.asarray(inputs["L0_ffn_b1"], np.float32)
    w2 = np.asarray(inputs["L0_ffn_w2"], np.float32)
    b2 = np.asarray(inputs["L0_ffn_b2"], np.float32)
    wo = np.asarray(inputs["L0_ffn_wo"], np.float32)
    w1p = np.zeros((2, D, 2 * HIDP), np.float32)
    w1p[:, :, :HID] = w1[:, :, :HID]
    w1p[:, :, HIDP:HIDP + HID] = w1[:, :, HID:]
    b1p = np.zeros((2, 2 * HIDP), np.float32)
    b1p[:, :HID] = b1[:, :HID]
    b1p[:, HIDP:HIDP + HID] = b1[:, HID:]
    w2p = np.zeros((2, HIDP, D), np.float32)
    w2p[:, :HID, :] = w2
    for l in range(2):
        sh[f"f{l}_wqkv"] = np.ascontiguousarray(wqkv[l])
        sh[f"f{l}_w1"] = np.ascontiguousarray(w1p[l])
        sh[f"f{l}_b1"] = _colpack(b1p[l])
        sh[f"f{l}_w2"] = np.ascontiguousarray(w2p[l])
        sh[f"f{l}_b2"] = _colpack(b2[l])
        sh[f"f{l}_wo"] = np.ascontiguousarray(wo[l])
        qs = np.asarray(inputs["L0_ffn_q_scale"], np.float32)[l]
        ks = np.asarray(inputs["L0_ffn_k_scale"], np.float32)[l]
        sh[f"f{l}_qs"] = np.ascontiguousarray(np.tile(qs, 2).reshape(128, 1))
        sh[f"f{l}_ks"] = np.ascontiguousarray(np.tile(ks, 2).reshape(128, 1))
        sh[f"f{l}_lg"] = _colpack(np.asarray(inputs["L0_ffn_ln_g"], np.float32)[l])
        sh[f"f{l}_lb"] = _colpack(np.asarray(inputs["L0_ffn_ln_b"], np.float32)[l])

    sh["WqT"] = np.ascontiguousarray(np.asarray(inputs["L1_attn_Wq"], np.float32).T)
    sh["WkT"] = np.ascontiguousarray(np.asarray(inputs["L1_attn_Wk"], np.float32).T)
    sh["WvT"] = np.ascontiguousarray(np.asarray(inputs["L1_attn_Wv"], np.float32).T)
    sh["WoT"] = np.ascontiguousarray(np.asarray(inputs["L1_attn_Wo"], np.float32).T)
    for nm in ["bq", "bk", "bv", "bo"]:
        sh["l1_" + nm] = _colpack(inputs["L1_attn_" + nm])
    sh["l1_bv_rep"] = np.ascontiguousarray(
        np.tile(np.asarray(inputs["L1_attn_bv"], np.float32).reshape(1, D), (128, 1)))
    sh["embKT"] = np.ascontiguousarray(np.asarray(inputs["L1_attn_embK"], np.float32).T)
    sh["embV"] = np.ascontiguousarray(np.asarray(inputs["L1_attn_embV"], np.float32))

    sh["w_router"] = np.ascontiguousarray(np.asarray(inputs["L1_ffn_w_router"], np.float32))
    sh["w_gate"] = np.asarray(inputs["L1_ffn_w_gate"], np.float32)
    sh["w_up"] = np.asarray(inputs["L1_ffn_w_up"], np.float32)
    sh["w_down"] = np.asarray(inputs["L1_ffn_w_down"], np.float32)

    co = {}
    co["iden_bf"] = np.eye(128, dtype=np.float32)
    DIM_HEAD, WINDOW = 64, 512
    half = np.arange(0, DIM_HEAD, 2, dtype=np.float64)
    inv_freq = 1.0 / 10000.0 ** (half / DIM_HEAD)
    tpos = np.arange(T, dtype=np.float64)
    freqs = tpos[:, None] * inv_freq[None, :]
    freqs = np.concatenate([freqs, freqs], axis=-1)
    base = (half + 0.4 * DIM_HEAD) / (1.4 * DIM_HEAD)
    power = (tpos - T // 2) / float(WINDOW // 2)
    scale = base[None, :] ** power[:, None]
    scale = np.concatenate([scale, scale], axis=-1)
    cos, sin = np.cos(freqs), np.sin(freqs)
    sgn = np.where(np.arange(DIM_HEAD) < DIM_HEAD // 2, -1.0, 1.0)
    CA_q = (cos * scale).T
    CB_q = (sin * scale * sgn[:, None].T).T if False else (sin * scale * sgn[None, :]).T
    CA_k = (cos / scale).T
    CB_k = (sin / scale * sgn[None, :]).T
    def tile_tab(tab):
        tt = np.concatenate([tab, tab], axis=0)
        return np.ascontiguousarray(np.concatenate([tt, tt], axis=1).astype(np.float32))
    co["rot_caq"] = tile_tab(CA_q)
    co["rot_cbq"] = tile_tab(CB_q)
    co["rot_cak"] = tile_tab(CA_k)
    co["rot_cbk"] = tile_tab(CB_k)
    kk = np.arange(128)
    qq = np.arange(512)
    masks = []
    for kt in range(8):
        j = kt * 128 + kk[:, None]
        i = 512 + qq[None, :]
        m = ((i - j) >= 0) & ((i - j) <= 512)
        masks.append(m)
    lm = np.concatenate(masks, axis=0).astype(np.float32)
    lm0 = lm.copy()
    lm0[:512] = 0.0
    co["lmask2"] = np.ascontiguousarray(np.stack([lm, lm0]))
    hm = np.zeros((4, 128, 8), np.float32)
    for c in range(4):
        hm[c, 0:64, 2 * c] = 1.0
        hm[c, 64:128, 2 * c + 1] = 1.0
    co["hm4"] = hm
    co["hm4T"] = np.ascontiguousarray(hm.transpose(0, 2, 1))
    co["w0tab"] = np.array([[1, 0, 0, 0, 1, 0, 0, 0]], np.int32)
    qq = np.arange(128)[:, None]
    w_ = np.arange(192)[None, :]
    co["pmask_lo"] = ((w_ >= 128 * (qq >= 64)) & (w_ <= 64 + qq)).astype(np.float32)
    w2_ = np.arange(256)[None, :]
    co["pmask_hi"] = ((w2_ >= qq + 64) & (w2_ < 128 * (1 + (qq >= 64)))).astype(np.float32)
    a16 = np.arange(16)
    co["tri_lo16"] = (a16[:, None] < a16[None, :]).astype(np.float32)
    co["tri_hi16"] = (a16[:, None] > a16[None, :]).astype(np.float32)
    co["selq64"] = (np.arange(128) < 64).astype(np.float32).reshape(128, 1)
    vv = np.arange(127)[None, :]
    co["bmask_first"] = ((qq - 63 + vv) >= 0).astype(np.float32)
    co["bmask_last"] = ((1920 + qq - 63 + vv) < 2048).astype(np.float32)
    co["iota32_col"] = np.arange(32, dtype=np.float32).reshape(32, 1)
    co["iota32r"] = np.ascontiguousarray(
        np.tile(np.arange(32, dtype=np.float32), (128, 1)))
    co["tri_strict128"] = (np.arange(128)[:, None] <
                           np.arange(128)[None, :]).astype(np.float32)
    co["iden128"] = np.eye(128, dtype=np.float32)
    co["iden128_p5"] = np.eye(128, dtype=np.float32)
    qq128 = np.arange(128)[:, None]
    co["dlo64"] = (np.arange(64)[None, :] == qq128 - 64).astype(np.float32)
    co["d65"] = (np.arange(65)[None, :] == qq128).astype(np.float32)
    co["sel65"] = (qq128 < 65).astype(np.float32)
    co["tri8"] = (np.arange(8)[:, None] <= np.arange(8)[None, :]).astype(np.float32)
    h["shared"] = sh
    h["const"] = co
    return h



def build_p1():
    nc = bass.Bass("TRN2", target_bir_lowering=False, detect_race_conditions=False)
    fl = Flow(nc)
    register_consts(nc, [EPS])
    io = lambda name: "in" if name == "x0T" else "out"
    emit_p1(nc, fl, io)
    return nc


def build_p3():
    nc = bass.Bass("TRN2", target_bir_lowering=False, detect_race_conditions=False)
    fl = Flow(nc)
    register_consts(nc, [EPS, 1.0, 8.0])
    io = lambda name: "in" if name == "xn0T" else "out"
    emit_p3(nc, fl, io)
    return nc


def build_p4():
    nc = bass.Bass("TRN2", target_bir_lowering=False, detect_race_conditions=False)
    fl = Flow(nc)
    register_consts(nc, [EPS])
    ins = {"x0T", "xn0T", "attn0T", "u_finT"}
    io = lambda name: "in" if name in ins else "out"
    emit_p4(nc, fl, io)
    return nc


def build_p7():
    nc = bass.Bass("TRN2", target_bir_lowering=False, detect_race_conditions=False)
    fl = Flow(nc)
    register_consts(nc, [EPS])
    io = lambda name: "in"
    emit_p7(nc, fl, io)
    return nc


def build_p6():
    nc = bass.Bass("TRN2", target_bir_lowering=False, detect_race_conditions=False,
                   num_devices=8)
    fl = Flow(nc)
    register_consts(nc, [EPS, 1.0, 8.0])
    ins = {"xn1T", "xn1Tb"}
    io = lambda name: "in" if name in ins else "out"
    emit_p6(nc, fl, io)
    return nc


def build_p5():
    nc = bass.Bass("TRN2", target_bir_lowering=False, detect_race_conditions=False)
    fl = Flow(nc)
    register_consts(nc, [EPS, 1.0, 8.0])
    io = lambda name: "in" if name == "xn1Tb" else "out"
    emit_p5(nc, fl, io)
    return nc


def build_p2():
    nc = bass.Bass("TRN2", target_bir_lowering=False, detect_race_conditions=False)
    fl = Flow(nc)
    register_consts(nc, [EPS, 1.0])
    io = lambda name: "in" if name == "t1Tb" else "out"
    emit_p2(nc, fl, io)
    return nc



def build_full():
    nc = bass.Bass("TRN2", target_bir_lowering=False, detect_race_conditions=False,
                   num_devices=8)
    fl = Flow(nc)
    register_consts(nc, [EPS, 1.0, 8.0])
    ins = {"x0T"}
    mk = lambda out_names: (lambda name: "in" if name in ins else
                            ("out" if name in out_names else "int"))
    def fence():
        for e in (nc.sync, nc.vector, nc.scalar, nc.tensor, nc.gpsimd):
            for s_ in ("st", "d", "g", "c"):
                if s_ in fl.sems:
                    fl.wait(e, s_)
    emit_p1(nc, fl, mk(set()))
    fence()
    emit_p2(nc, fl, mk(set()))
    fence()
    emit_p3(nc, fl, mk(set()))
    fence()
    emit_p4(nc, fl, mk(set()))
    fence()
    emit_p5(nc, fl, mk(set()))
    fence()
    emit_p6(nc, fl, mk(set()))
    fence()
    emit_p7(nc, fl, mk(set()))
    return nc


P2_NAMES = ["qwT", "kwT", "vwT", "owT", "a_gb", "beta_param", "a_vb_rep"]
P3_NAMES = [f"f{l}_{n}" for l in range(2) for n in
            ["wqkv", "w1", "b1", "w2", "b2", "wo", "qs", "ks", "lg", "lb"]]
P3_CO = ["rot_caq", "rot_cbq", "rot_cak", "rot_cbk", "lmask2", "hm4", "hm4T", "w0tab"]
P5_NAMES = ["WqT", "WkT", "WvT", "WoT", "l1_bq", "l1_bk", "l1_bv_rep", "l1_bo",
            "embKT", "embV"]
P5_CO = ["iden128_p5", "dlo64", "d65", "sel65", "bmask_first", "bmask_last"]
P6_NAMES = ["w_router", "w_gate", "w_up", "w_down"]
P6_CO = ["iota32r", "tri_strict128", "iden128"]


KERNEL_FUSED = True


def kernel(**inputs):
    if KERNEL_FUSED:
        return kernel_fused(**inputs)
    return kernel_multi(**inputs)


def kernel_fused(**inputs):
    h = host_prep(inputs)
    sh = h["shared"]
    co = h["const"]
    R = list(range(NCORES))
    nc = build_full()
    base = {**{n: sh[n] for n in P2_NAMES}, **{n: sh[n] for n in P3_NAMES},
            **{n: co[n] for n in P3_CO}, **{n: sh[n] for n in P5_NAMES},
            **{n: co[n] for n in P5_CO}, **{n: sh[n] for n in P6_NAMES},
            **{n: co[n] for n in P6_CO}}
    maps = [{**base, "x0T": h["percore"][c]["x0T"]} for c in R]
    rr = run_bass_kernel_spmd(nc, maps, core_ids=R).results
    out = np.empty((16, T, D), np.float32)
    for c in R:
        out[2 * c:2 * c + 2] = np.asarray(rr[c]["outT"], np.float32).T.reshape(BLOC, T, D)
    return out


def kernel_multi(**inputs):
    h = host_prep(inputs)
    sh = h["shared"]
    co = h["const"]
    R = list(range(NCORES))

    def run(nc, maps, keys):
        rr = run_bass_kernel_spmd(nc, maps, core_ids=R).results
        return {k: [rr[c][k] for c in R] for k in keys}

    r1 = run(build_p1(), [{"x0T": h["percore"][c]["x0T"]} for c in R],
             ["xn0T", "t1Tb"])
    r2 = run(build_p2(),
             [{**{n: sh[n] for n in P2_NAMES}, "t1Tb": r1["t1Tb"][c]} for c in R],
             ["attn0T"])
    m3 = {**{n: sh[n] for n in P3_NAMES}, **{n: co[n] for n in P3_CO}}
    r3 = run(build_p3(), [{**m3, "xn0T": r1["xn0T"][c]} for c in R], ["u_finT"])
    r4 = run(build_p4(),
             [{"x0T": h["percore"][c]["x0T"], "xn0T": r1["xn0T"][c],
               "attn0T": r2["attn0T"][c], "u_finT": r3["u_finT"][c]} for c in R],
             ["x1T", "xn1T", "xn1Tb"])
    m5 = {**{n: sh[n] for n in P5_NAMES}, **{n: co[n] for n in P5_CO}}
    r5 = run(build_p5(), [{**m5, "xn1Tb": r4["xn1Tb"][c]} for c in R], ["attn1T"])
    m6 = {**{n: sh[n] for n in P6_NAMES}, **{n: co[n] for n in P6_CO}}
    r6 = run(build_p6(),
             [{**m6, "xn1T": r4["xn1T"][c], "xn1Tb": r4["xn1Tb"][c]} for c in R],
             ["moeT"])
    r7 = run(build_p7(),
             [{"x1T": r4["x1T"][c], "attn1T": r5["attn1T"][c],
               "xn1T": r4["xn1T"][c], "moeT": r6["moeT"][c]} for c in R],
             ["outT"])
    out = np.empty((16, T, D), np.float32)
    for c in R:
        out[2 * c:2 * c + 2] = np.asarray(r7["outT"][c], np.float32).T.reshape(BLOC, T, D)
    return out
